# version 02
# baseline (speedup 1.0000x reference)
import contextlib
import numpy as np
import ml_dtypes
import concourse.bass as bass
import concourse.mybir as mybir
from concourse.bass_utils import run_bass_kernel_spmd

F32 = mybir.dt.float32
BF16 = mybir.dt.bfloat16
AF = mybir.ActivationFunctionType
ALU = mybir.AluOpType
AX = mybir.AxisListType

MAGIC = 12582912.0
EPS = 1e-5
NEGBIG = -30000.0
D, DC = 1024, 8
TB, NTB = 4096, 32
TO, NTO = 2048, 16
I0, NI0 = 2816, 22
LAT, NLAT = 512, 4
NHEAD, HD = 16, 64
FF = 2048

ENGS = ("sync", "gpsimd", "vector", "scalar", "tensor")


class Sem:
    __slots__ = ("h", "n")
    def __init__(self, h):
        self.h, self.n = h, 0
    def bump(self, k):
        self.n += k
        return self.n


class P:
    def __init__(self, nc, es):
        self.nc, self.es = nc, es
        self.ops = {e: [] for e in ENGS}
        self._sems = {}
        self.dbg_en = set()
        self.dbg_shapes = {}

    def sem(self, name):
        s = self._sems.get(name)
        if s is None:
            s = Sem(self.es.enter_context(self.nc.semaphore(name)))
            self._sems[name] = s
        return s

    def emit(self, eng, fn, wait=(), inc=None):
        wait = tuple(wait)
        tgt = inc[0].bump(inc[1]) if inc is not None else None
        def op(E, fn=fn, wait=wait, inc=inc):
            for s, v in wait:
                E.wait_ge(s.h, v)
            r = fn(E)
            if inc is not None:
                r.then_inc(inc[0].h, inc[1])
        self.ops[eng].append(op)
        return tgt

    def wait(self, eng, sem, val):
        self.ops[eng].append(lambda E, s=sem, v=val: E.wait_ge(s.h, v))

    def barrier(self, engs=ENGS):
        pb = self.sem("pb")
        tgt = pb.n + len(engs)
        pb.bump(len(engs))
        for eng in engs:
            def op(E, tgt=tgt, pb=pb):
                E.nop().then_inc(pb.h, 1)
                E.wait_ge(pb.h, tgt)
            self.ops[eng].append(op)

    def flush(self):
        if not any(self.ops.values()):
            return
        with self.nc.Block() as blk:
            for eng in ENGS:
                ops = self.ops[eng]
                if ops:
                    def runner(E, ops=tuple(ops)):
                        for op in ops:
                            op(E)
                    getattr(blk, eng)(runner)
        self.ops = {e: [] for e in ENGS}

    def dbg(self, name, src_ap_fn, wait):
        if name not in self.dbg_en:
            return
        st = self.sem("st")
        def mk(E, f=src_ap_fn, name=name):
            src = f()
            t = self.nc.dram_tensor("dbg_" + name,
                                    [int(x) for x in src.shape], src.dtype,
                                    kind="ExternalOutput").ap()
            return E.dma_start(out=t, in_=src)
        self.emit("sync", mk, wait=wait, inc=(st, 16))


def own_order(hc):
    idx = []
    for j in range(16):
        g = 2 * j + hc
        idx.extend(range(g * 128, (g + 1) * 128))
    return np.array(idx, np.int64)


def make_diagm(hc):
    dm = np.zeros((32, 2, 128, 128), np.float32)
    tril = np.tril(np.full((128, 128), NEGBIG, np.float32), -1)
    full = np.full((128, 128), NEGBIG, np.float32)
    for sc in range(32):
        j0 = sc // 2
        for m in range(2):
            j = j0 + m
            if j > 15:
                continue
            g = 2 * j + hc
            if g == sc:
                dm[sc, m] = tril
            elif g < sc:
                dm[sc, m] = full
    return dm.astype(ml_dtypes.bfloat16)


def host_inputs(inputs, core):
    b, hc = core // 2, core % 2
    f32 = lambda a: np.ascontiguousarray(np.asarray(a), dtype=np.float32)
    x = f32(inputs["L0_attn_input_sequence"][b])
    xro = np.ascontiguousarray(np.concatenate([x[own_order(hc)], x[own_order(1 - hc)]], 0))
    pv = np.stack([
        f32(inputs["L0_attn_bk"]), f32(inputs["L0_attn_gk"]), f32(inputs["L0_attn_betak"]),
        f32(inputs["L0_attn_bq"]), f32(inputs["L0_attn_gq"]), f32(inputs["L0_attn_betaq"]),
        f32(inputs["L0_attn_bv"]), f32(inputs["L0_attn_gv"]), f32(inputs["L0_attn_betav"]),
        f32(inputs["L1_ffn_ln_gamma"]), f32(inputs["L1_ffn_ln_beta"]),
        np.zeros(D, np.float32)]).astype(np.float32)
    return {
        "xro": xro,
        "Wk": f32(inputs["L0_attn_Wk"]), "Wq": f32(inputs["L0_attn_Wq"]),
        "Wv": f32(inputs["L0_attn_Wv"]), "pv": pv,
        "clips": np.array([[float(np.asarray(inputs["L0_ffn_clip_x"]).ravel()[0]),
                            float(np.asarray(inputs["L0_ffn_clip_down"]).ravel()[0])]],
                          np.float32),
        "wg": f32(inputs["L0_ffn_w_gate"]), "wu": f32(inputs["L0_ffn_w_up"]),
        "wd": f32(inputs["L0_ffn_w_down"]),
        "Wcd": f32(inputs["L1_attn_Wc_down"]), "Wqu": f32(inputs["L1_attn_Wq_up"]),
        "Wkv": f32(inputs["L1_attn_Wkv_up"]), "Wcp": f32(inputs["L1_attn_Wc_proj"]),
        "rwT": np.ascontiguousarray(f32(inputs["L1_ffn_router_w"]).T),
        "guT": np.ascontiguousarray(f32(inputs["L1_ffn_gate_up_w"]).transpose(0, 2, 1)),
        "dwT": np.ascontiguousarray(f32(inputs["L1_ffn_down_w"]).transpose(0, 2, 1)),
        "idb": np.eye(128, dtype=ml_dtypes.bfloat16),
        "idf": np.eye(128, dtype=np.float32),
        "ind2": np.concatenate([np.tile([[1, 0]], (64, 1)), np.tile([[0, 1]], (64, 1))],
                               0).astype(ml_dtypes.bfloat16),
        "diagm": make_diagm(hc),
    }



def build_program(dbg=(), upto="DONE"):
    nc = bass.Bass("TRN2", target_bir_lowering=False)
    ein = lambda n, s, d: nc.dram_tensor(n, list(s), d, kind="ExternalInput")
    idr = lambda n, s, d: nc.dram_tensor(n, list(s), d)

    xro_h = ein("xro", [TB, D], F32)
    Wk_h = ein("Wk", [D, D], F32)
    Wq_h = ein("Wq", [D, D], F32)
    Wv_h = ein("Wv", [D, D], F32)
    pv_h = ein("pv", [12, D], F32)
    clips_h = ein("clips", [1, 2], F32)
    wg_h = ein("wg", [I0, D], F32)
    wu_h = ein("wu", [I0, D], F32)
    wd_h = ein("wd", [D, I0], F32)
    Wcd_h = ein("Wcd", [D, LAT], F32)
    Wqu_h = ein("Wqu", [LAT, D], F32)
    Wkv_h = ein("Wkv", [LAT, 2 * HD], F32)
    Wcp_h = ein("Wcp", [D, D], F32)
    rwT_h = ein("rwT", [D, 8], F32)
    guT_h = ein("guT", [8, D, 2 * FF], F32)
    dwT_h = ein("dwT", [8, FF, D], F32)
    idb_h = ein("idb", [128, 128], BF16)
    idf_h = ein("idf", [128, 128], F32)
    ind2_h = ein("ind2", [128, 2], BF16)
    diagm_h = ein("diagm", [32, 2, 128, 128], BF16)
    out_h = nc.dram_tensor("out", [TO, D], F32, kind="ExternalOutput")

    wqT_d = idr("wqT_d", [D, I0], BF16)
    wuT_d = idr("wuT_d", [D, I0], BF16)
    wdT_d = idr("wdT_d", [I0, D], BF16)
    x1_d = idr("x1_d", [TO, D], F32)
    g_d = idr("g_d", [TO, D], BF16)
    xf_d = idr("xf_d", [DC, 128, TO], BF16)
    qf_d = idr("qf_d", [DC, 128, TO], BF16)
    cc1i = idr("cc1i", [1, 1], F32)
    cc1o = idr("cc1o", [1, 1], F32)
    cc2i = idr("cc2i", [1, 1], F32)
    cc2o = idr("cc2o", [1, 1], F32)
    kvi = idr("kvi", [TO, 2 * HD], BF16)
    kvo = idr("kvo", [TB, 2 * HD], BF16)

    xro = xro_h.ap().rearrange("(n p) m -> n p m", p=128)
    x1t = x1_d.ap().rearrange("(n p) m -> n p m", p=128)

    es = contextlib.ExitStack()
    with es:
        p = P(nc, es)
        p.dbg_en = set(dbg)
        SB = lambda n, s, d: es.enter_context(nc.sbuf_tensor(n, list(s), d))
        ld = p.sem("ld");  gd = p.sem("gd");  st = p.sem("st")
        cv = p.sem("cv");  cs = p.sem("cs");  ct = p.sem("ct");  cc = p.sem("cc")

        id_sb = SB("id_sb", [128, 128], BF16)
        id_f = SB("id_f", [128, 128], F32)
        ones_row = SB("ones_row", [1, 128], F32)
        ones_col = SB("ones_col", [128, 1], BF16)
        pv_pp = SB("pv_pp", [128, 12 * DC], F32)
        onesrb = SB("onesrb", [1, 128], BF16)
        stat = SB("stat", [128, 4 * NTB], F32)
        st1 = SB("st1", [128, 4 * NTO], F32)
        st2 = SB("st2", [128, 4 * NTO], F32)
        scal = SB("scal", [128, 16], F32)
        row11 = SB("row11", [1, 16], F32)
        clip_sb = SB("clip_sb", [1, 2], F32)
        tiny = SB("tiny", [128, 80], F32)

        def bcast128(dst_ap_fn, src_row_ap, ncols, ps, wait, tag):
            done = []
            for n0 in range(0, ncols, 512):
                n1 = min(n0 + 512, ncols)
                tm = p.emit("tensor", lambda E, n0=n0, n1=n1: E.matmul(
                    ps[:, n0:n1], ones_row[:1, :], src_row_ap[:, n0:n1],
                    start=True, stop=True), wait=wait if n0 == 0 else [], inc=(ct, 1))
                done.append((n0, n1, tm))
            outs = []
            for n0, n1, tm in done:
                tv = p.emit("vector", lambda E, n0=n0, n1=n1: E.tensor_copy(
                    dst_ap_fn(n0, n1), ps[:, n0:n1]), wait=[(ct, tm)], inc=(cv, 1))
                outs.append(tv)
            return outs[-1]

        l0c_es = contextlib.ExitStack()
        brow = l0c_es.enter_context(nc.sbuf_tensor("brow", [1, 3 * D], BF16))
        gvbc = l0c_es.enter_context(nc.sbuf_tensor("gvbc", [128, D], BF16))
        bvbc = l0c_es.enter_context(nc.sbuf_tensor("bvbc", [128, D], BF16))

        with contextlib.ExitStack() as esA:
            pvrow = esA.enter_context(nc.sbuf_tensor("pvrow", [1, D], F32))
            psA = esA.enter_context(nc.psum_tensor("psA", [128, 1024], F32))
            p.emit("sync", lambda E: E.dma_start(out=id_sb[:], in_=idb_h.ap()), inc=(ld, 16))
            p.emit("sync", lambda E: E.dma_start(out=id_f[:], in_=idf_h.ap()), inc=(ld, 16))
            clip_tgt = p.emit("sync", lambda E: E.dma_start(out=clip_sb[:], in_=clips_h.ap()), inc=(ld, 16))
            p.emit("sync", lambda E: E.dma_start(
                out=pv_pp[:], in_=bass.AP(pv_h, 0, [[1, 128], [1024, 12], [128, 8]]),
                allow_slow_non_contiguous=True), inc=(ld, 16))
            p.emit("vector", lambda E: E.memset(ones_row[:], 1.0), inc=(cv, 1))
            p.emit("vector", lambda E: E.memset(ones_col[:], 1.0), inc=(cv, 1))
            p.emit("vector", lambda E: E.memset(onesrb[:], 1.0), inc=(cv, 1))
            for i, r in enumerate((0, 3, 6)):
                t0 = p.emit("sync", lambda E, r=r: E.dma_start(
                    out=pvrow[:], in_=pv_h.ap()[r:r + 1, :]), wait=[(cv, cv.n)], inc=(ld, 16))
                p.emit("vector", lambda E, i=i: E.tensor_copy(
                    brow[:, i * D:(i + 1) * D], pvrow[:]), wait=[(ld, t0)], inc=(cv, 1))
            for rr, dst in ((7, gvbc), (8, bvbc)):
                t0 = p.emit("sync", lambda E, rr=rr: E.dma_start(
                    out=pvrow[:], in_=pv_h.ap()[rr:rr + 1, :]), wait=[(cv, cv.n)], inc=(ld, 16))
                bcast128(lambda n0, n1, dst=dst: dst[:, n0:n1], pvrow,
                         D, psA, [(ld, t0), (cv, cv.n)], "bc")
            p.flush()

        with contextlib.ExitStack() as esB:
            xbuf = [esB.enter_context(nc.sbuf_tensor(f"xbuf{i}", [128, D], F32)) for i in range(2)]
            sq_scr = esB.enter_context(nc.sbuf_tensor("sq_scr", [128, D], F32))
            xnf = esB.enter_context(nc.sbuf_tensor("xnf", [128, D], F32))
            psB = esB.enter_context(nc.psum_tensor("psB", [1, 128], F32))

            def ln_stats(src_tiles, n_tiles, sbuf_stat):
                tgts = []
                for i in range(n_tiles):
                    buf = xbuf[i % 2]
                    w = tgts[i - 2] if i >= 2 else []
                    t0 = p.emit("sync", lambda E, i=i, buf=buf: E.dma_start(
                        out=buf[:], in_=src_tiles[i]), wait=w, inc=(ld, 16))
                    tv = p.emit("vector", lambda E, i=i, buf=buf: E.reduce_sum(
                        sbuf_stat[:, i:i + 1], buf[:], axis=AX.X),
                        wait=[(ld, t0)], inc=(cv, 1))
                    ts_ = p.emit("scalar", lambda E, i=i, buf=buf: E.activation(
                        sq_scr[:], buf[:], AF.Square,
                        accum_out=sbuf_stat[:, n_tiles + i:n_tiles + i + 1]),
                        wait=[(ld, t0)], inc=(cs, 1))
                    tgts.append([(cv, tv), (cs, ts_)])
                n = n_tiles
                p.emit("vector", lambda E, n=n, sb_=sbuf_stat: E.tensor_scalar(
                    sb_[:, 2 * n:3 * n], sb_[:, 0:n], -1.0 / D, None, ALU.mult),
                    wait=[(cs, cs.n), (cv, cv.n)], inc=(cv, 1))
                tmu2 = p.emit("scalar", lambda E, n=n, sb_=sbuf_stat: E.activation(
                    tiny[:, 0:n], sb_[:, 2 * n:3 * n], AF.Square),
                    wait=[(cv, cv.n)], inc=(cs, 1))
                p.emit("vector", lambda E, n=n, sb_=sbuf_stat: E.tensor_scalar(
                    tiny[:, n:2 * n], sb_[:, n:2 * n], 1.0 / D, EPS, ALU.mult, ALU.add),
                    inc=(cv, 1))
                p.emit("vector", lambda E, n=n, sb_=sbuf_stat: E.tensor_tensor(
                    sb_[:, 3 * n:4 * n], tiny[:, n:2 * n], tiny[:, 0:n], ALU.subtract),
                    wait=[(cs, tmu2)], inc=(cv, 1))
                p.emit("vector", lambda E, n=n, sb_=sbuf_stat: E.reciprocal(
                    tiny[:, 0:n], sb_[:, 3 * n:4 * n]), inc=(cv, 1))
                return p.emit("scalar", lambda E, n=n, sb_=sbuf_stat: E.activation(
                    sb_[:, 3 * n:4 * n], tiny[:, 0:n], AF.Sqrt),
                    wait=[(cv, cv.n)], inc=(cs, 1))

            stats_done = ln_stats(xro, NTB, stat)
            p.dbg("stat", lambda: stat[:, :], [(cs, stats_done)])
            lastv = None
            sweep_cv0, sweep_cs0 = cv.n, cs.n
            for i in range(NTO):
                buf = xbuf[i % 2]
                w = [(cv, lastv)] if lastv else [(cv, sweep_cv0), (cs, sweep_cs0)]
                t0 = p.emit("sync", lambda E, i=i, buf=buf: E.dma_start(
                    out=buf[:], in_=xro[i]), wait=w, inc=(ld, 16))
                p.emit("vector", lambda E, i=i, buf=buf: E.tensor_scalar(
                    xnf[:], buf[:], stat[:, 2 * NTB + i:2 * NTB + i + 1],
                    stat[:, 3 * NTB + i:3 * NTB + i + 1], ALU.add, ALU.mult),
                    wait=[(ld, t0), (cs, stats_done)], inc=(cv, 1))
                lastv = p.emit("vector", lambda E, i=i: E.tensor_reduce(
                    tiny[:, i:i + 1], xnf[:], AX.X, ALU.max, apply_absolute_value=True),
                    inc=(cv, 1))
                if i == 0:
                    p.dbg("xnf0", lambda: xnf[:, :], [(cv, cv.n)])
            p.emit("vector", lambda E: E.tensor_reduce(
                scal[:, 0:1], tiny[:, 0:NTO], AX.X, ALU.max),
                wait=[(cv, lastv)], inc=(cv, 1))
            tt = p.emit("tensor", lambda E: E.transpose(
                psB[:, :], scal[:, 0:1], id_f[:]), wait=[(cv, cv.n)], inc=(ct, 1))
            tv = p.emit("vector", lambda E: E.reduce_max(
                row11[:, 0:1], psB[:, :], axis=AX.X), wait=[(ct, tt)], inc=(cv, 1))
            t0 = p.emit("gpsimd", lambda E: E.dma_start(out=cc1i.ap()[:, :], in_=row11[:, 0:1]),
                        wait=[(cv, tv)], inc=(gd, 16))
            p.wait("gpsimd", gd, t0)
            tc = p.emit("gpsimd", lambda E: E.collective_compute(
                "AllReduce", ALU.max, replica_groups=[list(range(8))],
                ins=[cc1i.ap().opt()], outs=[cc1o.ap().opt()]), inc=(cc, 1))
            p.wait("gpsimd", cc, tc)
            c1_done = p.emit("gpsimd", lambda E: E.dma_start(
                out=row11[:, 1:2], in_=cc1o.ap()[:, :]), inc=(gd, 16))
            p.dbg("gmax1", lambda: row11[:, :], [(gd, c1_done)])
            p.flush()

        with contextlib.ExitStack() as esC:
            wbuf = [esC.enter_context(nc.sbuf_tensor(f"wbuf{i}", [128, I0], F32)) for i in range(2)]
            wq_bf = esC.enter_context(nc.sbuf_tensor("wq_bf", [128, I0], BF16))
            wscr = esC.enter_context(nc.sbuf_tensor("wscr", [128, I0], F32))
            wstage = esC.enter_context(nc.sbuf_tensor("wstage", [128, I0], BF16))
            psT = esC.enter_context(nc.psum_tensor("psT", [128, 2048], BF16))

            def quant_tr(w_h, rows, cols, outT_d):
                nt, ncc = rows // 128, cols // 128
                tq_done, st_done = [], []
                for i in range(nt):
                    buf = wbuf[i % 2]
                    w = [(cv, tq_done[i - 2])] if i >= 2 else []
                    t0 = p.emit("sync", lambda E, i=i, buf=buf, w_h=w_h, cols=cols: E.dma_start(
                        out=buf[:, 0:cols], in_=w_h.ap()[i * 128:(i + 1) * 128, :]),
                        wait=w, inc=(ld, 16))
                    r0 = p.emit("vector", lambda E, buf=buf, cols=cols: E.tensor_reduce(
                        tiny[:, 64:65], buf[:, 0:cols], AX.X, ALU.max,
                        apply_absolute_value=True), wait=[(ld, t0)], inc=(cv, 1))
                    r1 = p.emit("vector", lambda E: E.tensor_scalar(
                        tiny[:, 65:66], tiny[:, 64:65], 1.0 / 127.0, None, ALU.mult),
                        wait=[(cv, r0)], inc=(cv, 1))
                    r2 = p.emit("vector", lambda E: E.reciprocal(
                        tiny[:, 66:67], tiny[:, 64:65]), wait=[(cv, r1)], inc=(cv, 1))
                    r3 = p.emit("vector", lambda E: E.tensor_scalar(
                        tiny[:, 67:68], tiny[:, 66:67], 127.0, None, ALU.mult),
                        wait=[(cv, r2)], inc=(cv, 1))
                    tbufdone = p.emit("vector", lambda E, buf=buf, cols=cols: E.tensor_scalar(
                        wscr[:, 0:cols], buf[:, 0:cols], tiny[:, 67:68], MAGIC,
                        ALU.mult, ALU.add), wait=[(cv, r3)], inc=(cv, 1))
                    tq = p.emit("vector", lambda E, cols=cols: E.tensor_scalar(
                        wq_bf[:, 0:cols], wscr[:, 0:cols], MAGIC, tiny[:, 65:66],
                        ALU.subtract, ALU.mult), wait=[(cv, tbufdone)], inc=(cv, 1))
                    tq_done.append(tbufdone)
                    evs = []
                    for c in range(ncc):
                        off = (c % 2) * 1024
                        wt = [(cv, tq)] if c < 2 else [(cv, evs[c - 2])]
                        if c == 0 and st_done:
                            wt = wt + [(st, st_done[-1])]
                        tt = p.emit("tensor", lambda E, c=c, off=off: E.transpose(
                            psT[:, off:off + 128], wq_bf[:, c * 128:(c + 1) * 128],
                            id_sb[:]), wait=wt, inc=(ct, 1))
                        evs.append(p.emit("vector", lambda E, c=c, off=off: E.tensor_copy(
                            wstage[:, c * 128:(c + 1) * 128], psT[:, off:off + 128]),
                            wait=[(ct, tt)] + ([(st, st_done[-1])] if (c == 0 and st_done) else []),
                            inc=(cv, 1)))
                    tm = p.emit("sync", lambda E, i=i, ncc=ncc, outT_d=outT_d, rows=rows,
                                cols=cols: E.dma_start(
                        out=bass.AP(outT_d, i * 128,
                                    [[rows, 128], [128 * rows, ncc], [1, 128]]),
                        in_=wstage[:, 0:ncc * 128]), wait=[(cv, evs[-1])], inc=(st, 16))
                    st_done.append(tm)
            quant_tr(wg_h, I0, D, wqT_d)
            quant_tr(wu_h, I0, D, wuT_d)
            quant_tr(wd_h, D, I0, wdT_d)
            wq_stores = st.n
            p.flush()

        def scales(col_clip, gmax_col, out_inv, out_s, ps):
            t0 = p.emit("scalar", lambda E: E.activation(
                row11[:, 2:3], clip_sb[:, col_clip:col_clip + 1], AF.Sigmoid),
                wait=[(cv, cv.n), (ld, clip_tgt)], inc=(cs, 1))
            t1 = p.emit("vector", lambda E: E.tensor_tensor(
                row11[:, 3:4], row11[:, 2:3], row11[:, gmax_col:gmax_col + 1], ALU.mult),
                wait=[(cs, t0)], inc=(cv, 1))
            t2 = p.emit("vector", lambda E: E.tensor_scalar(
                row11[:, 9:10], row11[:, 3:4], 1.0 / 127.0, None, ALU.mult),
                wait=[(cv, t1)], inc=(cv, 1))
            p.emit("vector", lambda E: E.reciprocal(row11[:, 10:11], row11[:, 9:10]),
                   wait=[(cv, t2)], inc=(cv, 1))
            tv = cv.n
            tm = p.emit("tensor", lambda E, ps=ps: E.matmul(
                ps[:, 0:2], ones_row[:1, :], row11[:, 9:11], start=True, stop=True),
                wait=[(cv, tv)], inc=(ct, 1))
            p.emit("vector", lambda E, ps=ps: E.tensor_copy(
                scal[:, out_s:out_s + 1], ps[:, 0:1]), wait=[(ct, tm)], inc=(cv, 1))
            return p.emit("vector", lambda E, ps=ps: E.tensor_copy(
                scal[:, out_inv:out_inv + 1], ps[:, 1:2]), inc=(cv, 1))

        xq_es = contextlib.ExitStack()
        xq_fm = xq_es.enter_context(nc.sbuf_tensor("xq_fm", [128, DC * TO], BF16))
        with contextlib.ExitStack() as esE:
            xbuf = [esE.enter_context(nc.sbuf_tensor(f"exbuf{i}", [128, D], F32)) for i in range(2)]
            q_scr = esE.enter_context(nc.sbuf_tensor("q_scr", [128, D], F32))
            q_scr2 = esE.enter_context(nc.sbuf_tensor("q_scr2", [128, D], F32))
            xq_bf = esE.enter_context(nc.sbuf_tensor("xq_bf", [128, D], BF16))
            psS = esE.enter_context(nc.psum_tensor("psS", [128, 16], F32))
            psT = esE.enter_context(nc.psum_tensor("psTE", [128, 2048], BF16))
            p.wait("scalar", gd, c1_done)
            sc1_done = scales(0, 1, 2, 3, psS)
            lastq = None
            for i in range(NTO):
                buf = xbuf[i % 2]
                w = [(cv, lastq)] if lastq else []
                t0 = p.emit("sync", lambda E, i=i, buf=buf: E.dma_start(
                    out=buf[:], in_=xro[i]), wait=w, inc=(ld, 16))
                p.emit("vector", lambda E, i=i, buf=buf: E.tensor_scalar(
                    q_scr[:], buf[:], stat[:, 2 * NTB + i:2 * NTB + i + 1],
                    stat[:, 3 * NTB + i:3 * NTB + i + 1], ALU.add, ALU.mult),
                    wait=[(ld, t0), (cv, max(sc1_done, cv.n))], inc=(cv, 1))
                ta = p.emit("scalar", lambda E: E.activation(
                    q_scr2[:], q_scr[:], AF.Copy, bias=MAGIC, scale=scal[:, 2:3]),
                    wait=[(cv, cv.n)], inc=(cs, 1))
                p.emit("vector", lambda E: E.tensor_scalar(
                    q_scr[:], q_scr2[:], MAGIC, 127.0, ALU.subtract, ALU.min),
                    wait=[(cs, ta)], inc=(cv, 1))
                tq = p.emit("vector", lambda E: E.tensor_scalar(
                    xq_bf[:], q_scr[:], -128.0, scal[:, 3:4], ALU.max, ALU.mult),
                    inc=(cv, 1))
                if i == 0:
                    p.dbg("scal", lambda: scal[:, :], [(cv, tq)])
                    p.dbg("xq_bf0", lambda: xq_bf[:, :], [(cv, tq)])
                evs = []
                for c in range(DC):
                    off = (c % 2) * 1024
                    wt = [(cv, tq)] if c < 2 else [(cv, evs[c - 2])]
                    tt = p.emit("tensor", lambda E, c=c, off=off: E.transpose(
                        psT[:, off:off + 128], xq_bf[:, c * 128:(c + 1) * 128], id_sb[:]),
                        wait=wt, inc=(ct, 1))
                    evs.append(p.emit("vector", lambda E, c=c, i=i, off=off: E.tensor_copy(
                        xq_fm[:, c * TO + i * 128:c * TO + (i + 1) * 128],
                        psT[:, off:off + 128]), wait=[(ct, tt)], inc=(cv, 1)))
                lastq = evs[-1]
            p.dbg("xq_fm", lambda: xq_fm[:, :], [(cv, cv.n)])
            p.flush()

        if upto == "E":
            p.emit("sync", lambda E: E.nop(), wait=[(st, st.n)])
            p.flush()
            xq_es.close()
            return nc

        h_es = contextlib.ExitStack()
        h_sb = h_es.enter_context(nc.sbuf_tensor("h_sb", [128, NI0 * TO], BF16))
        with contextlib.ExitStack() as esF:
            wT_sb = [esF.enter_context(nc.sbuf_tensor(f"wT{i}", [128, D], BF16))
                     for i in range(4)]
            sl_bf = esF.enter_context(nc.sbuf_tensor("sl_bf", [128, TO], BF16))
            psG = esF.enter_context(nc.psum_tensor("psG", [128, TO], F32))
            psU = esF.enter_context(nc.psum_tensor("psU", [128, TO], F32))
            p.wait("sync", st, wq_stores)
            hdone = []
            for i in range(NI0):
                bg, bu = wT_sb[2 * (i % 2)], wT_sb[2 * (i % 2) + 1]
                w = [hdone[i - 2]] if i >= 2 else []
                tg = p.emit("sync", lambda E, i=i, bg=bg: E.dma_start(
                    out=bg[:, 0:D],
                    in_=bass.AP(wqT_d, i * 128, [[I0, 128], [128 * I0, DC], [1, 128]])),
                    wait=w, inc=(ld, 16))
                tu = p.emit("sync", lambda E, i=i, bu=bu: E.dma_start(
                    out=bu[:, 0:D],
                    in_=bass.AP(wuT_d, i * 128, [[I0, 128], [128 * I0, DC], [1, 128]])),
                    inc=(ld, 16))
                sil_prev = cs.n
                hm_prev = cv.n
                tgm = None
                for d in range(DC):
                    for s4 in range(4):
                        wt = [(ld, tg), (cs, sil_prev)] if (d == 0 and s4 == 0) else []
                        tgm = p.emit("tensor", lambda E, d=d, s4=s4, bg=bg: E.matmul(
                            psG[:, s4 * 512:(s4 + 1) * 512],
                            bg[:, d * 128:(d + 1) * 128],
                            xq_fm[:, d * TO + s4 * 512:d * TO + (s4 + 1) * 512],
                            start=(d == 0), stop=(d == DC - 1)), wait=wt,
                            inc=(ct, 1) if (d == DC - 1 and s4 == 3) else None)
                tsl = p.emit("scalar", lambda E: E.activation(
                    sl_bf[:], psG[:], AF.Silu),
                    wait=[(ct, tgm), (cv, hm_prev)], inc=(cs, 1))
                tum = None
                for d in range(DC):
                    for s4 in range(4):
                        wt = [(ld, tu), (cv, hm_prev)] if (d == 0 and s4 == 0) else []
                        tum = p.emit("tensor", lambda E, d=d, s4=s4, bu=bu: E.matmul(
                            psU[:, s4 * 512:(s4 + 1) * 512],
                            bu[:, d * 128:(d + 1) * 128],
                            xq_fm[:, d * TO + s4 * 512:d * TO + (s4 + 1) * 512],
                            start=(d == 0), stop=(d == DC - 1)), wait=wt,
                            inc=(ct, 1) if (d == DC - 1 and s4 == 3) else None)
                th = p.emit("vector", lambda E, i=i: E.scalar_tensor_tensor(
                    h_sb[:, i * TO:(i + 1) * TO], psU[:], 0.0, sl_bf[:],
                    ALU.bypass, ALU.mult), wait=[(ct, tum), (cs, tsl)], inc=(cv, 1))
                th2 = p.emit("vector", lambda E, i=i: E.tensor_reduce(
                    tiny[:, i:i + 1], h_sb[:, i * TO:(i + 1) * TO], AX.X, ALU.max,
                    apply_absolute_value=True), inc=(cv, 1))
                hdone.append((cv, th2))
            p.emit("vector", lambda E: E.tensor_reduce(
                scal[:, 4:5], tiny[:, 0:NI0], AX.X, ALU.max),
                wait=[hdone[-1]], inc=(cv, 1))
            p.flush()
        with contextlib.ExitStack() as esF2:
            psB2 = esF2.enter_context(nc.psum_tensor("psB2", [1, 128], F32))
            tt = p.emit("tensor", lambda E: E.transpose(
                psB2[:, :], scal[:, 4:5], id_f[:]), wait=[(cv, cv.n)], inc=(ct, 1))
            tv = p.emit("vector", lambda E: E.reduce_max(
                row11[:, 5:6], psB2[:, :], axis=AX.X), wait=[(ct, tt)], inc=(cv, 1))
            t0 = p.emit("gpsimd", lambda E: E.dma_start(
                out=cc2i.ap()[:, :], in_=row11[:, 5:6]), wait=[(cv, tv)], inc=(gd, 16))
            p.wait("gpsimd", gd, t0)
            tc = p.emit("gpsimd", lambda E: E.collective_compute(
                "AllReduce", ALU.max, replica_groups=[list(range(8))],
                ins=[cc2i.ap().opt()], outs=[cc2o.ap().opt()]), inc=(cc, 1))
            p.wait("gpsimd", cc, tc)
            c2_done = p.emit("gpsimd", lambda E: E.dma_start(
                out=row11[:, 6:7], in_=cc2o.ap()[:, :]), inc=(gd, 16))
            p.dbg("gmax2", lambda: row11[:, :], [(gd, c2_done)])
            p.dbg("h_sb", lambda: h_sb[:, :], [(cv, cv.n)])
            p.flush()

        with contextlib.ExitStack() as esH0:
            psS2 = esH0.enter_context(nc.psum_tensor("psS2", [128, 16], F32))
            p.wait("scalar", gd, c2_done)
            sc2_done = scales(1, 6, 6, 7, psS2)
            p.flush()
        with contextlib.ExitStack() as esH:
            wdT_sb = [esH.enter_context(nc.sbuf_tensor(f"wdT{i}", [128, D], BF16))
                      for i in range(2)]
            hq_f = esH.enter_context(nc.sbuf_tensor("hq_f", [128, 512], F32))
            hq_g = esH.enter_context(nc.sbuf_tensor("hq_g", [128, 512], F32))
            hq_bf = esH.enter_context(nc.sbuf_tensor("hq_bf", [128, 512], BF16))
            psD = esH.enter_context(nc.psum_tensor("psD", [128, 4 * D], F32))
            gstage = esH.enter_context(nc.sbuf_tensor("gstage", [128, 4 * D], BF16))
            gst = []
            carry_mm = None
            carry_sil = None
            for tg4 in range(4):
                mm_done = []
                for i in range(NI0):
                    bw = wdT_sb[i % 2]
                    w = [mm_done[i - 2]] if i >= 2 else ([carry_mm] if carry_mm else [])
                    t0 = p.emit("sync", lambda E, i=i, bw=bw: E.dma_start(
                        out=bw[:], in_=wdT_d.ap()[i * 128:(i + 1) * 128, :]),
                        wait=w, inc=(ld, 16))
                    ta = p.emit("scalar", lambda E, i=i, tg4=tg4: E.activation(
                        hq_f[:], h_sb[:, i * TO + tg4 * 512:i * TO + tg4 * 512 + 512],
                        AF.Copy, bias=MAGIC, scale=scal[:, 6:7]),
                        wait=[(cv, max(sc2_done, cv.n))], inc=(cs, 1))
                    p.emit("vector", lambda E: E.tensor_scalar(
                        hq_g[:], hq_f[:], MAGIC, 127.0, ALU.subtract, ALU.min),
                        wait=[(cs, ta)], inc=(cv, 1))
                    tq = p.emit("vector", lambda E: E.tensor_scalar(
                        hq_bf[:], hq_g[:], -128.0, scal[:, 7:8], ALU.max, ALU.mult),
                        wait=[mm_done[-1]] if mm_done else ([carry_mm] if carry_mm else []),
                        inc=(cv, 1))
                    tmm = None
                    for ts4 in range(4):
                        for nn2 in range(2):
                            wt = ([(ld, t0), (cv, tq)] +
                                  ([carry_sil] if (carry_sil and i == 0) else [])
                                  ) if (ts4 == 0 and nn2 == 0) else []
                            tmm = p.emit("tensor", lambda E, i=i, ts4=ts4, nn2=nn2, bw=bw:
                                E.matmul(
                                    psD[:, ts4 * D + nn2 * 512:ts4 * D + (nn2 + 1) * 512],
                                    hq_bf[:, ts4 * 128:(ts4 + 1) * 128],
                                    bw[:, nn2 * 512:(nn2 + 1) * 512],
                                    start=(i == 0), stop=(i == NI0 - 1)), wait=wt,
                                inc=(ct, 1) if (ts4 == 3 and nn2 == 1) else None)
                    mm_done.append((ct, tmm if tmm else ct.n))
                tdn = ct.n
                tsil = None
                for ts4 in range(4):
                    lc = tg4 * 4 + ts4
                    wg_ = [(st, gst[-1])] if gst else []
                    tsil = p.emit("scalar", lambda E, ts4=ts4: E.activation(
                        gstage[:, ts4 * D:(ts4 + 1) * D], psD[:, ts4 * D:(ts4 + 1) * D],
                        AF.Silu), wait=[(ct, tdn)] + (wg_ if ts4 == 0 else []), inc=(cs, 1))
                    gst.append(p.emit("sync", lambda E, ts4=ts4, lc=lc: E.dma_start(
                        out=g_d.ap()[lc * 128:(lc + 1) * 128, :],
                        in_=gstage[:, ts4 * D:(ts4 + 1) * D]),
                        wait=[(cs, tsil)], inc=(st, 16)))
                carry_mm = (ct, tdn)
                carry_sil = (cs, tsil)
            p.flush()
        h_es.close()
        xq_es.close()

        if upto == "H":
            p.emit("sync", lambda E: E.nop(), wait=[(st, st.n)])
            p.barrier()
            p.flush()
            return nc


        katt_es = contextlib.ExitStack()
        k_fm = katt_es.enter_context(nc.sbuf_tensor("k_fm", [128, DC * TO], BF16))
        acc = katt_es.enter_context(nc.sbuf_tensor("acc", [128, NTO * D], BF16))
        fls = katt_es.enter_context(nc.sbuf_tensor("fls", [128, 64], F32))
        mcol, scolf, rcol = fls[:, 0:16], fls[:, 16:32], fls[:, 32:48]

        def ln_mini(ybuf, sqscr, wait_ev):
            t1 = p.emit("vector", lambda E: E.reduce_sum(
                tiny[:, 68:69], ybuf[:], axis=AX.X), wait=wait_ev, inc=(cv, 1))
            t2 = p.emit("scalar", lambda E: E.activation(
                sqscr[:], ybuf[:], AF.Square, accum_out=tiny[:, 69:70]),
                wait=wait_ev, inc=(cs, 1))
            t3 = p.emit("vector", lambda E: E.tensor_scalar(
                tiny[:, 70:71], tiny[:, 68:69], -1.0 / D, None, ALU.mult),
                wait=[(cv, t1)], inc=(cv, 1))
            t4 = p.emit("scalar", lambda E: E.activation(
                tiny[:, 71:72], tiny[:, 70:71], AF.Square), wait=[(cv, t3)], inc=(cs, 1))
            t5 = p.emit("vector", lambda E: E.tensor_scalar(
                tiny[:, 72:73], tiny[:, 69:70], 1.0 / D, EPS, ALU.mult, ALU.add),
                wait=[(cs, t2)], inc=(cv, 1))
            t6 = p.emit("vector", lambda E: E.tensor_tensor(
                tiny[:, 73:74], tiny[:, 72:73], tiny[:, 71:72], ALU.subtract),
                wait=[(cv, t5), (cs, t4)], inc=(cv, 1))
            t7 = p.emit("vector", lambda E: E.reciprocal(
                tiny[:, 74:75], tiny[:, 73:74]), wait=[(cv, t6)], inc=(cv, 1))
            t8 = p.emit("scalar", lambda E: E.activation(
                tiny[:, 75:76], tiny[:, 74:75], AF.Sqrt), wait=[(cv, t7)], inc=(cs, 1))
            return tiny[:, 70:71], tiny[:, 75:76], (cs, t8)

        def tr8(src_bf, dst_fn, psTx, wait_src, fuse=None):
            evs = []
            for c in range(DC):
                off = (c % 2) * 1024
                wt = list(wait_src) if c < 2 else [(cv, evs[c - 2])]
                tt = p.emit("tensor", lambda E, c=c, off=off, s=src_bf: E.transpose(
                    psTx[:, off:off + 128], s[:, c * 128:(c + 1) * 128], id_sb[:]),
                    wait=wt, inc=(ct, 1))
                if fuse is None:
                    ev = p.emit("vector", lambda E, c=c, off=off: E.tensor_copy(
                        dst_fn(c), psTx[:, off:off + 128]), wait=[(ct, tt)], inc=(cv, 1))
                else:
                    gr, br = fuse
                    ev = p.emit("vector", lambda E, c=c, off=off, gr=gr, br=br: E.tensor_scalar(
                        dst_fn(c), psTx[:, off:off + 128],
                        pv_pp[:, gr * DC + c:gr * DC + c + 1],
                        pv_pp[:, br * DC + c:br * DC + c + 1], ALU.mult, ALU.add),
                        wait=[(ct, tt)], inc=(cv, 1))
                evs.append(ev)
            return evs

        def proj_tm(ps, xnT, W_bf, brow_ix, wait_first):
            for d in range(DC):
                for nn in range(2):
                    p.emit("tensor", lambda E, d=d, nn=nn, ps=ps, xnT=xnT, W_bf=W_bf:
                        E.matmul(ps[:, nn * 512:(nn + 1) * 512],
                                 xnT[:, d * 128:(d + 1) * 128],
                                 W_bf[:, d * D + nn * 512:d * D + (nn + 1) * 512],
                                 start=(d == 0), stop=False),
                        wait=wait_first if (d == 0 and nn == 0) else [])
            tb = None
            for nn in range(2):
                tb = p.emit("tensor", lambda E, nn=nn, ps=ps, brow_ix=brow_ix: E.matmul(
                    ps[:, nn * 512:(nn + 1) * 512], onesrb[:1, :],
                    brow[:, brow_ix * D + nn * 512:brow_ix * D + (nn + 1) * 512],
                    start=False, stop=True), inc=(ct, 1) if nn == 1 else None)
            return tb

        with contextlib.ExitStack() as esK:
            xbK = [esK.enter_context(nc.sbuf_tensor(f"xbK{i}", [128, D], F32)) for i in range(2)]
            xnbK = esK.enter_context(nc.sbuf_tensor("xnbK", [128, D], BF16))
            xnTK = esK.enter_context(nc.sbuf_tensor("xnTK", [128, D], BF16))
            ykf = esK.enter_context(nc.sbuf_tensor("ykf", [128, D], F32))
            kb = esK.enter_context(nc.sbuf_tensor("kb", [128, D], BF16))
            sqK = esK.enter_context(nc.sbuf_tensor("sqK", [128, D], F32))
            Wk_bf = esK.enter_context(nc.sbuf_tensor("Wk_bf", [128, DC * D], BF16))
            psTK = esK.enter_context(nc.psum_tensor("psTK", [128, 2048], BF16))
            psK = esK.enter_context(nc.psum_tensor("psK", [128, D], F32))
            wk_t = p.emit("gpsimd", lambda E: E.dma_start(
                out=Wk_bf[:], in_=bass.AP(Wk_h, 0, [[D, 128], [128 * D, DC], [1, D]])),
                inc=(gd, 16))
            prevdone = None
            for i in range(NTO):
                buf = xbK[i % 2]
                w = [prevdone] if prevdone else []
                t0 = p.emit("sync", lambda E, i=i, buf=buf: E.dma_start(
                    out=buf[:], in_=xro[i]), wait=w, inc=(ld, 16))
                txn = p.emit("vector", lambda E, i=i, buf=buf: E.tensor_scalar(
                    xnbK[:], buf[:], stat[:, 2 * NTB + i:2 * NTB + i + 1],
                    stat[:, 3 * NTB + i:3 * NTB + i + 1], ALU.add, ALU.mult),
                    wait=[(ld, t0), (cv, cv.n)], inc=(cv, 1))
                tr8(xnbK, lambda c: xnTK[:, c * 128:(c + 1) * 128], psTK, [(cv, txn)])
                tev = cv.n
                tkm = proj_tm(psK, xnTK, Wk_bf, 0, [(cv, tev), (gd, wk_t), (cv, cv.n)])
                tye = p.emit("vector", lambda E: E.tensor_copy(ykf[:], psK[:]),
                             wait=[(ct, tkm), (cs, cs.n)], inc=(cv, 1))
                negmu, inv, sdone = ln_mini(ykf, sqK, [(cv, tye)])
                tkb = p.emit("vector", lambda E, negmu=negmu, inv=inv: E.tensor_scalar(
                    kb[:], ykf[:], negmu, inv, ALU.add, ALU.mult),
                    wait=[sdone], inc=(cv, 1))
                evs = tr8(kb, (lambda c, i=i: k_fm[:, c * TO + i * 128:c * TO + (i + 1) * 128]
                               ) if False else (lambda c, i=i: k_fm[:, c * TO + i * 128:c * TO + (i + 1) * 128]),
                          psTK, [(cv, tkb)], fuse=(1, 2))
                prevdone = (cv, evs[-1])
            p.barrier()
            p.flush()

        x1_written = []
        for hf in range(2):
            with contextlib.ExitStack() as esQ:
                qh = esQ.enter_context(nc.sbuf_tensor("qh%d" % hf, [128, DC * TO], BF16))
                vh = esQ.enter_context(nc.sbuf_tensor("vh%d" % hf, [128, NTO * D], BF16))
                Wq_bf = esQ.enter_context(nc.sbuf_tensor("Wq_bf%d" % hf, [128, DC * D], BF16))
                Wv_bf = esQ.enter_context(nc.sbuf_tensor("Wv_bf%d" % hf, [128, DC * D], BF16))
                wq_t = p.emit("gpsimd", lambda E: E.dma_start(
                    out=Wq_bf[:], in_=bass.AP(Wq_h, 0, [[D, 128], [128 * D, DC], [1, D]])),
                    inc=(gd, 16))
                wv_t = p.emit("gpsimd", lambda E: E.dma_start(
                    out=Wv_bf[:], in_=bass.AP(Wv_h, 0, [[D, 128], [128 * D, DC], [1, D]])),
                    inc=(gd, 16))
                with contextlib.ExitStack() as esP:
                    xbP = [esP.enter_context(nc.sbuf_tensor(f"xbP{i}_%d" % hf, [128, D], F32)) for i in range(2)]
                    xnbP = esP.enter_context(nc.sbuf_tensor("xnbP%d" % hf, [128, D], BF16))
                    xnTP = esP.enter_context(nc.sbuf_tensor("xnTP%d" % hf, [128, D], BF16))
                    yqf = esP.enter_context(nc.sbuf_tensor("yqf%d" % hf, [128, D], F32))
                    yvf = esP.enter_context(nc.sbuf_tensor("yvf%d" % hf, [128, D], F32))
                    qb = esP.enter_context(nc.sbuf_tensor("qb%d" % hf, [128, D], BF16))
                    sqP = esP.enter_context(nc.sbuf_tensor("sqP%d" % hf, [128, D], F32))
                    psTP = esP.enter_context(nc.psum_tensor("psTP%d" % hf, [128, 2048], BF16))
                    psQ = esP.enter_context(nc.psum_tensor("psQ%d" % hf, [128, D], F32))
                    psV = esP.enter_context(nc.psum_tensor("psV%d" % hf, [128, D], F32))
                    prevq = None
                    for j in range(NTO):
                        ti = hf * NTO + j
                        buf = xbP[j % 2]
                        w = [prevq] if prevq else []
                        t0 = p.emit("sync", lambda E, ti=ti, buf=buf: E.dma_start(
                            out=buf[:], in_=xro[ti]), wait=w, inc=(ld, 16))
                        txn = p.emit("vector", lambda E, ti=ti, buf=buf: E.tensor_scalar(
                            xnbP[:], buf[:], stat[:, 2 * NTB + ti:2 * NTB + ti + 1],
                            stat[:, 3 * NTB + ti:3 * NTB + ti + 1], ALU.add, ALU.mult),
                            wait=[(ld, t0), (cv, cv.n)], inc=(cv, 1))
                        tr8(xnbP, lambda c: xnTP[:, c * 128:(c + 1) * 128], psTP, [(cv, txn)])
                        tev = cv.n
                        tqm = proj_tm(psQ, xnTP, Wq_bf, 1, [(cv, tev), (gd, wq_t), (cv, cv.n)])
                        tvm = proj_tm(psV, xnTP, Wv_bf, 2, [(gd, wv_t), (cv, cv.n)])
                        tqe = p.emit("vector", lambda E: E.tensor_copy(yqf[:], psQ[:]),
                                     wait=[(ct, tqm), (cs, cs.n)], inc=(cv, 1))
                        tve = p.emit("vector", lambda E: E.tensor_copy(yvf[:], psV[:]),
                                     wait=[(ct, tvm), (cs, cs.n)], inc=(cv, 1))
                        negmu, inv, sdone = ln_mini(yqf, sqP, [(cv, tqe)])
                        tqb = p.emit("vector", lambda E, negmu=negmu, inv=inv: E.tensor_scalar(
                            qb[:], yqf[:], negmu, inv, ALU.add, ALU.mult),
                            wait=[sdone], inc=(cv, 1))
                        tr8(qb, (lambda c, j=j: qh[:, c * TO + j * 128:c * TO + (j + 1) * 128]),
                            psTP, [(cv, tqb)], fuse=(4, 5))
                        negmu2, inv2, sdone2 = ln_mini(yvf, sqP, [(cv, tve), (cv, cv.n)])
                        prevq = (cv, p.emit("vector", lambda E, j=j, negmu2=negmu2, inv2=inv2:
                            E.tensor_scalar(vh[:, j * D:(j + 1) * D], yvf[:],
                                            negmu2, inv2, ALU.add, ALU.mult),
                            wait=[sdone2], inc=(cv, 1)))
                    p.barrier()
                    p.flush()
                with contextlib.ExitStack() as esFl:
                    e_bf = esFl.enter_context(nc.sbuf_tensor("e_bf%d" % hf, [128, TO], BF16))
                    eT = esFl.enter_context(nc.sbuf_tensor("eT%d" % hf, [128, TO], BF16))
                    fA = esFl.enter_context(nc.sbuf_tensor("fA%d" % hf, [128, D], F32))
                    fB = esFl.enter_context(nc.sbuf_tensor("fB%d" % hf, [128, D], F32))
                    fC = esFl.enter_context(nc.sbuf_tensor("fC%d" % hf, [128, D], F32))
                    xb3 = esFl.enter_context(nc.sbuf_tensor("xb3%d" % hf, [128, D], F32))
                    gbuf = esFl.enter_context(nc.sbuf_tensor("gbuf%d" % hf, [128, D], BF16))
                    psSC = esFl.enter_context(nc.psum_tensor("psSC%d" % hf, [128, TO], F32))
                    psTF = esFl.enter_context(nc.psum_tensor("psTF%d" % hf, [128, 2048], BF16))
                    psAV = esFl.enter_context(nc.psum_tensor("psAV%d" % hf, [128, D], F32))
                    if hf == 0:
                        p.emit("vector", lambda E: E.memset(mcol, -3.0e38), inc=(cv, 1))
                        p.emit("vector", lambda E: E.memset(scolf, 0.0), inc=(cv, 1))
                        p.emit("vector", lambda E: E.memset(acc[:], 0.0), inc=(cv, 1))
                    prev_exp = None
                    prev_x1 = None
                    prev_tr = None
                    for lc in range(NTO):
                        smw = ([(cs, prev_exp)] if prev_exp else []) + [(cv, cv.n)]
                        tsc = None
                        for c in range(DC):
                            for s4 in range(4):
                                tsc = p.emit("tensor", lambda E, c=c, s4=s4, lc=lc: E.matmul(
                                    psSC[:, s4 * 512:(s4 + 1) * 512],
                                    k_fm[:, c * TO + lc * 128:c * TO + (lc + 1) * 128],
                                    qh[:, c * TO + s4 * 512:c * TO + (s4 + 1) * 512],
                                    start=(c == 0), stop=(c == DC - 1)),
                                    wait=smw if (c == 0 and s4 == 0) else [],
                                    inc=(ct, 1) if (c == DC - 1 and s4 == 3) else None)
                        u1 = p.emit("vector", lambda E: E.tensor_reduce(
                            fls[:, 48:49], psSC[:], AX.X, ALU.max), wait=[(ct, tsc)], inc=(cv, 1))
                        u2 = p.emit("vector", lambda E, lc=lc: E.tensor_tensor(
                            fls[:, 49:50], mcol[:, lc:lc + 1], fls[:, 48:49], ALU.max),
                            wait=[(cv, u1)], inc=(cv, 1))
                        u3 = p.emit("vector", lambda E, lc=lc: E.tensor_tensor(
                            fls[:, 50:51], mcol[:, lc:lc + 1], fls[:, 49:50], ALU.subtract),
                            wait=[(cv, u2)], inc=(cv, 1))
                        u4 = p.emit("scalar", lambda E, lc=lc: E.activation(
                            rcol[:, lc:lc + 1], fls[:, 50:51], AF.Exp),
                            wait=[(cv, u3)], inc=(cs, 1))
                        u5 = p.emit("vector", lambda E: E.tensor_scalar(
                            fls[:, 51:52], fls[:, 49:50], -1.0, None, ALU.mult),
                            wait=[(cv, u3)], inc=(cv, 1))
                        u6 = p.emit("vector", lambda E, lc=lc: E.tensor_copy(
                            mcol[:, lc:lc + 1], fls[:, 49:50]), wait=[(cv, u5)], inc=(cv, 1))
                        texp = p.emit("scalar", lambda E: E.activation(
                            e_bf[:], psSC[:], AF.Exp, bias=fls[:, 51:52],
                            accum_out=fls[:, 52:53]),
                            wait=[(cv, u5)] + ([(ct, prev_tr)] if prev_tr else []),
                            inc=(cs, 1))
                        prev_exp = texp
                        avdone = None
                        evs = []
                        for Lc in range(NTO):
                            off = (Lc % 2) * 1024
                            wt = [(cs, texp)] if Lc < 2 else [(cv, evs[Lc - 2])]
                            tt = p.emit("tensor", lambda E, Lc=Lc, off=off: E.transpose(
                                psTF[:, off:off + 128], e_bf[:, Lc * 128:(Lc + 1) * 128],
                                id_sb[:]), wait=wt, inc=(ct, 1))
                            ev = p.emit("vector", lambda E, Lc=Lc, off=off: E.tensor_copy(
                                eT[:, Lc * 128:(Lc + 1) * 128], psTF[:, off:off + 128]),
                                wait=[(ct, tt)], inc=(cv, 1))
                            evs.append(ev)
                            prev_tr = tt
                            for nn in range(2):
                                avdone = p.emit("tensor", lambda E, Lc=Lc, nn=nn: E.matmul(
                                    psAV[:, nn * 512:(nn + 1) * 512],
                                    eT[:, Lc * 128:(Lc + 1) * 128],
                                    vh[:, Lc * D + nn * 512:Lc * D + (nn + 1) * 512],
                                    start=(Lc == 0), stop=(Lc == NTO - 1)),
                                    wait=[(cv, ev)] if nn == 0 else [],
                                    inc=(ct, 1) if (Lc == NTO - 1 and nn == 1) else None)
                        if hf == 0:
                            ta = p.emit("vector", lambda E, lc=lc: E.tensor_copy(
                                acc[:, lc * D:(lc + 1) * D], psAV[:]),
                                wait=[(ct, avdone)], inc=(cv, 1))
                            p.emit("vector", lambda E, lc=lc: E.tensor_copy(
                                scolf[:, lc:lc + 1], fls[:, 52:53]),
                                wait=[(cs, texp), (cv, ta)], inc=(cv, 1))
                        else:
                            wx = [prev_x1] if prev_x1 else []
                            tg0 = p.emit("sync", lambda E, lc=lc: E.dma_start(
                                out=gbuf[:], in_=g_d.ap()[lc * 128:(lc + 1) * 128, :]),
                                wait=wx, inc=(ld, 16))
                            tx0 = p.emit("sync", lambda E, lc=lc: E.dma_start(
                                out=xb3[:], in_=xro[lc]), inc=(ld, 16))
                            v1_ = p.emit("vector", lambda E, lc=lc: E.scalar_tensor_tensor(
                                fA[:], acc[:, lc * D:(lc + 1) * D], rcol[:, lc:lc + 1],
                                psAV[:], ALU.mult, ALU.add),
                                wait=[(ct, avdone)] + ([prev_x1] if prev_x1 else []),
                                inc=(cv, 1))
                            v2_ = p.emit("vector", lambda E, lc=lc: E.scalar_tensor_tensor(
                                fls[:, 53:54], scolf[:, lc:lc + 1], rcol[:, lc:lc + 1],
                                fls[:, 52:53], ALU.mult, ALU.add),
                                wait=[(cs, texp), (cv, v1_)], inc=(cv, 1))
                            v3_ = p.emit("vector", lambda E: E.reciprocal(
                                fls[:, 54:55], fls[:, 53:54]), wait=[(cv, v2_)], inc=(cv, 1))
                            v4_ = p.emit("vector", lambda E: E.scalar_tensor_tensor(
                                fB[:], fA[:], fls[:, 54:55], gvbc[:], ALU.mult, ALU.mult),
                                wait=[(cv, v3_)], inc=(cv, 1))
                            v5_ = p.emit("vector", lambda E, lc=lc: E.tensor_scalar(
                                fC[:], xb3[:], stat[:, 2 * NTB + lc:2 * NTB + lc + 1],
                                stat[:, 3 * NTB + lc:3 * NTB + lc + 1], ALU.add, ALU.mult),
                                wait=[(ld, tx0), (cv, v4_)], inc=(cv, 1))
                            v6_ = p.emit("vector", lambda E: E.tensor_tensor(
                                fA[:], fC[:], bvbc[:], ALU.add), wait=[(cv, v5_)], inc=(cv, 1))
                            v7_ = p.emit("vector", lambda E: E.tensor_tensor(
                                fC[:], fB[:], fA[:], ALU.add), wait=[(cv, v6_)], inc=(cv, 1))
                            v8_ = p.emit("vector", lambda E: E.tensor_tensor(
                                fB[:], fC[:], gbuf[:], ALU.mult),
                                wait=[(ld, tg0), (cv, v7_)], inc=(cv, 1))
                            v9_ = p.emit("vector", lambda E: E.tensor_tensor(
                                fC[:], fB[:], xb3[:], ALU.add), wait=[(cv, v8_)], inc=(cv, 1))
                            tx1 = p.emit("sync", lambda E, lc=lc: E.dma_start(
                                out=x1t[lc], in_=fC[:]), wait=[(cv, v9_)], inc=(st, 16))
                            if upto == "L0":
                                p.emit("sync", lambda E, lc=lc: E.dma_start(
                                    out=out_h.ap()[lc * 128:(lc + 1) * 128, :], in_=fC[:]),
                                    inc=(st, 16))
                            prev_x1 = (st, st.n)
                            x1_written.append(tx1)
                    p.barrier()
                    p.flush()
        katt_es.close()
        l0c_es.close()

        if upto == "L0":
            p.emit("sync", lambda E: E.nop(), wait=[(st, st.n)])
            p.barrier()
            p.flush()
            return nc

        def agrow(sc):
            return (sc // 2) * 128 + (0 if sc % 2 == 0 else TO)

        L1_es = contextlib.ExitStack()
        out_acc = L1_es.enter_context(nc.sbuf_tensor("out_acc", [128, NTO * D], F32))
        gmoe_es = contextlib.ExitStack()
        gate_fm = gmoe_es.enter_context(nc.sbuf_tensor("gate_fm", [8, TO], BF16))

        with contextlib.ExitStack() as es9:
            xbuf9 = [es9.enter_context(nc.sbuf_tensor(f"xb9{i}", [128, D], F32)) for i in range(2)]
            sq9 = es9.enter_context(nc.sbuf_tensor("sq9", [128, D], F32))
            def ln_stats2(src_tiles, n_tiles, sbuf_stat, xwait):
                tgts = []
                for i in range(n_tiles):
                    buf = xbuf9[i % 2]
                    w = list(xwait) + (tgts[i - 2] if i >= 2 else [])
                    t0 = p.emit("sync", lambda E, i=i, buf=buf: E.dma_start(
                        out=buf[:], in_=src_tiles[i]), wait=w, inc=(ld, 16))
                    tv = p.emit("vector", lambda E, i=i, buf=buf: E.reduce_sum(
                        sbuf_stat[:, i:i + 1], buf[:], axis=AX.X),
                        wait=[(ld, t0)], inc=(cv, 1))
                    ts_ = p.emit("scalar", lambda E, i=i, buf=buf: E.activation(
                        sq9[:], buf[:], AF.Square,
                        accum_out=sbuf_stat[:, n_tiles + i:n_tiles + i + 1]),
                        wait=[(ld, t0)], inc=(cs, 1))
                    tgts.append([(cv, tv), (cs, ts_)])
                n = n_tiles
                p.emit("vector", lambda E, n=n, sb_=sbuf_stat: E.tensor_scalar(
                    sb_[:, 2 * n:3 * n], sb_[:, 0:n], -1.0 / D, None, ALU.mult),
                    wait=[(cs, cs.n), (cv, cv.n)], inc=(cv, 1))
                tmu2 = p.emit("scalar", lambda E, n=n, sb_=sbuf_stat: E.activation(
                    tiny[:, 0:n], sb_[:, 2 * n:3 * n], AF.Square),
                    wait=[(cv, cv.n)], inc=(cs, 1))
                p.emit("vector", lambda E, n=n, sb_=sbuf_stat: E.tensor_scalar(
                    tiny[:, n:2 * n], sb_[:, n:2 * n], 1.0 / D, EPS, ALU.mult, ALU.add),
                    inc=(cv, 1))
                p.emit("vector", lambda E, n=n, sb_=sbuf_stat: E.tensor_tensor(
                    sb_[:, 3 * n:4 * n], tiny[:, n:2 * n], tiny[:, 0:n], ALU.subtract),
                    wait=[(cs, tmu2), (cv, cv.n)], inc=(cv, 1))
                p.emit("vector", lambda E, n=n, sb_=sbuf_stat: E.reciprocal(
                    tiny[:, 0:n], sb_[:, 3 * n:4 * n]), wait=[(cv, cv.n)], inc=(cv, 1))
                return p.emit("scalar", lambda E, n=n, sb_=sbuf_stat: E.activation(
                    sb_[:, 3 * n:4 * n], tiny[:, 0:n], AF.Sqrt),
                    wait=[(cv, cv.n)], inc=(cs, 1))
            st1_done = ln_stats2(x1t, NTO, st1, [(st, st.n)])
            p.barrier()
            p.flush()

        with contextlib.ExitStack() as es10:
            xn1_fm = es10.enter_context(nc.sbuf_tensor("xn1_fm", [128, DC * TO], BF16))
            xf_fm = es10.enter_context(nc.sbuf_tensor("xf_fm", [128, DC * TO], BF16))
            with contextlib.ExitStack() as es10i:
                xb10 = [es10i.enter_context(nc.sbuf_tensor(f"xb10{i}", [128, D], F32)) for i in range(2)]
                xn1f = es10i.enter_context(nc.sbuf_tensor("xn1f", [128, D], F32))
                xn1b = es10i.enter_context(nc.sbuf_tensor("xn1b", [128, D], BF16))
                xfb = es10i.enter_context(nc.sbuf_tensor("xfb", [128, D], BF16))
                sq10 = es10i.enter_context(nc.sbuf_tensor("sq10", [128, D], F32))
                psT10 = es10i.enter_context(nc.psum_tensor("psT10", [128, 2048], BF16))
                prevt = None
                for i in range(NTO):
                    buf = xb10[i % 2]
                    w = [prevt] if prevt else []
                    t0 = p.emit("sync", lambda E, i=i, buf=buf: E.dma_start(
                        out=buf[:], in_=x1t[i]), wait=w, inc=(ld, 16))
                    tx = p.emit("vector", lambda E, i=i, buf=buf: E.tensor_scalar(
                        xn1f[:], buf[:], st1[:, 2 * NTO + i:2 * NTO + i + 1],
                        st1[:, 3 * NTO + i:3 * NTO + i + 1], ALU.add, ALU.mult),
                        wait=[(ld, t0), (cs, max(st1_done, cs.n)), (cv, cv.n)], inc=(cv, 1))
                    txb = p.emit("vector", lambda E: E.tensor_copy(xn1b[:], xn1f[:]),
                                 wait=[(cv, tx)], inc=(cv, 1))
                    tr8(xn1b, (lambda c, i=i: xn1_fm[:, c * TO + i * 128:c * TO + (i + 1) * 128]),
                        psT10, [(cv, txb)])
                    negmu2, inv2, sdone2 = ln_mini(xn1f, sq10, [(cv, tx), (cv, cv.n)])
                    txf = p.emit("vector", lambda E, negmu2=negmu2, inv2=inv2: E.tensor_scalar(
                        xfb[:], xn1f[:], negmu2, inv2, ALU.add, ALU.mult),
                        wait=[sdone2], inc=(cv, 1))
                    evs = tr8(xfb, (lambda c, i=i: xf_fm[:, c * TO + i * 128:c * TO + (i + 1) * 128]),
                              psT10, [(cv, txf)], fuse=(9, 10))
                    prevt = (cv, evs[-1])
                p.barrier()
                p.flush()
            for c_ in range(DC):
                p.emit("sync", lambda E, c_=c_: E.dma_start(
                    out=xf_d.ap()[c_, :, :], in_=xf_fm[:, c_ * TO:(c_ + 1) * TO]),
                    wait=[(cv, cv.n)] if c_ == 0 else [], inc=(st, 16))
            with contextlib.ExitStack() as esR:
                rw_bf = esR.enter_context(nc.sbuf_tensor("rw_bf", [128, DC * 8], BF16))
                lg = esR.enter_context(nc.sbuf_tensor("lg", [128, 8], F32))
                lg2 = esR.enter_context(nc.sbuf_tensor("lg2", [128, 8], F32))
                prb = esR.enter_context(nc.sbuf_tensor("prb", [128, 8], F32))
                gt_tm = esR.enter_context(nc.sbuf_tensor("gt_tm", [128, NTO * 8], BF16))
                psR = esR.enter_context(nc.psum_tensor("psR", [128, 8], F32))
                psRT = esR.enter_context(nc.psum_tensor("psRT", [8, 2048], BF16))
                trw = p.emit("gpsimd", lambda E: E.dma_start(
                    out=rw_bf[:], in_=bass.AP(rwT_h, 0, [[8, 128], [128 * 8, DC], [1, 8]])),
                    inc=(gd, 16))
                for i in range(NTO):
                    tm = None
                    for d in range(DC):
                        tm = p.emit("tensor", lambda E, d=d, i=i: E.matmul(
                            psR[:], xf_fm[:, d * TO + i * 128:d * TO + (i + 1) * 128],
                            rw_bf[:, d * 8:(d + 1) * 8],
                            start=(d == 0), stop=(d == DC - 1)),
                            wait=[(gd, trw), (cv, cv.n)] if d == 0 else [],
                            inc=(ct, 1) if d == DC - 1 else None)
                    a1 = p.emit("vector", lambda E: E.tensor_scalar(
                        lg[:], psR[:], -10000.0, 10000.0, ALU.max, ALU.min),
                        wait=[(ct, tm)], inc=(cv, 1))
                    a2 = p.emit("vector", lambda E: E.tensor_reduce(
                        tiny[:, 48:49], lg[:], AX.X, ALU.max), wait=[(cv, a1)], inc=(cv, 1))
                    a3 = p.emit("vector", lambda E: E.tensor_scalar(
                        tiny[:, 49:50], tiny[:, 48:49], -1.0, None, ALU.mult),
                        wait=[(cv, a2)], inc=(cv, 1))
                    a4 = p.emit("scalar", lambda E: E.activation(
                        lg2[:], lg[:], AF.Exp, bias=tiny[:, 49:50],
                        accum_out=tiny[:, 50:51]), wait=[(cv, a3)], inc=(cs, 1))
                    a5 = p.emit("vector", lambda E: E.tensor_scalar(
                        tiny[:, 51:52], tiny[:, 50:51], 1.0e-8, None, ALU.add),
                        wait=[(cs, a4)], inc=(cv, 1))
                    a6 = p.emit("vector", lambda E: E.reciprocal(
                        tiny[:, 52:53], tiny[:, 51:52]), wait=[(cv, a5)], inc=(cv, 1))
                    a7 = p.emit("vector", lambda E: E.tensor_scalar(
                        prb[:], lg2[:], tiny[:, 52:53], None, ALU.mult),
                        wait=[(cv, a6)], inc=(cv, 1))
                    a8 = p.emit("vector", lambda E: E.tensor_scalar(
                        lg[:], prb[:], 1.0e-8, 1.0, ALU.max, ALU.min),
                        wait=[(cv, a7)], inc=(cv, 1))
                    b1 = p.emit("vector", lambda E: E.tensor_reduce(
                        tiny[:, 53:54], lg[:], AX.X, ALU.max), wait=[(cv, a8)], inc=(cv, 1))
                    b2 = p.emit("vector", lambda E: E.tensor_single_scalar(
                        lg2[:], lg[:], tiny[:, 53:54], ALU.is_ge), wait=[(cv, b1)], inc=(cv, 1))
                    b3 = p.emit("vector", lambda E: E.scalar_tensor_tensor(
                        prb[:], lg2[:], -1.0e30, lg[:], ALU.mult, ALU.add),
                        wait=[(cv, b2)], inc=(cv, 1))
                    b4 = p.emit("vector", lambda E: E.tensor_reduce(
                        tiny[:, 54:55], prb[:], AX.X, ALU.max), wait=[(cv, b3)], inc=(cv, 1))
                    b5 = p.emit("vector", lambda E: E.tensor_single_scalar(
                        lg2[:], lg[:], tiny[:, 54:55], ALU.is_ge), wait=[(cv, b4)], inc=(cv, 1))
                    b6 = p.emit("vector", lambda E, i=i: E.tensor_tensor(
                        gt_tm[:, i * 8:(i + 1) * 8], lg[:], lg2[:], ALU.mult),
                        wait=[(cv, b5)], inc=(cv, 1))
                prevev = None
                for i in range(NTO):
                    wt = [(cv, cv.n)] if prevev is None else [(cv, prevev)]
                    tt = p.emit("tensor", lambda E, i=i: E.transpose(
                        psRT[:, 0:128], gt_tm[:, i * 8:(i + 1) * 8], id_sb[:]),
                        wait=wt, inc=(ct, 1))
                    prevev = p.emit("vector", lambda E, i=i: E.tensor_copy(
                        gate_fm[:, i * 128:(i + 1) * 128], psRT[:, 0:128]),
                        wait=[(ct, tt)], inc=(cv, 1))
                p.barrier()
                p.flush()
            with contextlib.ExitStack() as esL:
                lat_fm = esL.enter_context(nc.sbuf_tensor("lat_fm", [128, NLAT * TO], BF16))
                Wcd_bf = esL.enter_context(nc.sbuf_tensor("Wcd_bf", [128, DC * LAT], BF16))
                Wqu_bf = esL.enter_context(nc.sbuf_tensor("Wqu_bf", [128, NLAT * D], BF16))
                Wkv_bf = esL.enter_context(nc.sbuf_tensor("Wkv_bf", [128, NLAT * 128], BF16))
                qtmp = esL.enter_context(nc.sbuf_tensor("qtmp", [128, TO], BF16))
                kvb = esL.enter_context(nc.sbuf_tensor("kvb", [128, 128], BF16))
                psL = esL.enter_context(nc.psum_tensor("psL", [128, TO], F32))
                psKV = esL.enter_context(nc.psum_tensor("psKV", [128, 128], F32))
                tcd = p.emit("gpsimd", lambda E: E.dma_start(
                    out=Wcd_bf[:], in_=bass.AP(Wcd_h, 0, [[LAT, 128], [128 * LAT, DC], [1, LAT]])),
                    inc=(gd, 16))
                tqu = p.emit("gpsimd", lambda E: E.dma_start(
                    out=Wqu_bf[:], in_=bass.AP(Wqu_h, 0, [[D, 128], [128 * D, NLAT], [1, D]])),
                    inc=(gd, 16))
                tkv = p.emit("gpsimd", lambda E: E.dma_start(
                    out=Wkv_bf[:], in_=bass.AP(Wkv_h, 0, [[128, 128], [128 * 128, NLAT], [1, 128]])),
                    inc=(gd, 16))
                prevev = None
                for L in range(NLAT):
                    tm = None
                    for d in range(DC):
                        for s4 in range(4):
                            tm = p.emit("tensor", lambda E, L=L, d=d, s4=s4: E.matmul(
                                psL[:, s4 * 512:(s4 + 1) * 512],
                                Wcd_bf[:, d * LAT + L * 128:d * LAT + (L + 1) * 128],
                                xn1_fm[:, d * TO + s4 * 512:d * TO + (s4 + 1) * 512],
                                start=(d == 0), stop=(d == DC - 1)),
                                wait=([(gd, tcd)] + ([(cv, prevev)] if prevev else [(cv, cv.n)])
                                      ) if (d == 0 and s4 == 0) else [],
                                inc=(ct, 1) if (d == DC - 1 and s4 == 3) else None)
                    prevev = p.emit("vector", lambda E, L=L: E.tensor_copy(
                        lat_fm[:, L * TO:(L + 1) * TO], psL[:]), wait=[(ct, tm)], inc=(cv, 1))
                qsp = None
                for qc in range(DC):
                    tm = None
                    for L in range(NLAT):
                        for s4 in range(4):
                            tm = p.emit("tensor", lambda E, qc=qc, L=L, s4=s4: E.matmul(
                                psL[:, s4 * 512:(s4 + 1) * 512],
                                Wqu_bf[:, L * D + qc * 128:L * D + (qc + 1) * 128],
                                lat_fm[:, L * TO + s4 * 512:L * TO + (s4 + 1) * 512],
                                start=(L == 0), stop=(L == NLAT - 1)),
                                wait=[(gd, tqu), (cv, prevev)] if (L == 0 and s4 == 0) else [],
                                inc=(ct, 1) if (L == NLAT - 1 and s4 == 3) else None)
                    prevev = p.emit("vector", lambda E: E.tensor_copy(
                        qtmp[:], psL[:]),
                        wait=[(ct, tm)] + ([(st, qsp)] if qsp else []), inc=(cv, 1))
                    qsp = p.emit("sync", lambda E, qc=qc: E.dma_start(
                        out=qf_d.ap()[qc, :, :], in_=qtmp[:]),
                        wait=[(cv, prevev)], inc=(st, 16))
                pkv_cv, pkv_st = None, None
                for i in range(NTO):
                    tm = None
                    for L in range(NLAT):
                        tm = p.emit("tensor", lambda E, i=i, L=L: E.matmul(
                            psKV[:], lat_fm[:, L * TO + i * 128:L * TO + (i + 1) * 128],
                            Wkv_bf[:, L * 128:(L + 1) * 128],
                            start=(L == 0), stop=(L == NLAT - 1)),
                            wait=([(gd, tkv)] + ([(cv, pkv_cv)] if pkv_cv else [(cv, cv.n)])
                                  ) if L == 0 else [],
                            inc=(ct, 1) if L == NLAT - 1 else None)
                    pkv_cv = p.emit("vector", lambda E: E.tensor_copy(
                        kvb[:], psKV[:]),
                        wait=[(ct, tm)] + ([(st, pkv_st)] if pkv_st else []), inc=(cv, 1))
                    pkv_st = p.emit("sync", lambda E, i=i: E.dma_start(
                        out=kvi.ap()[i * 128:(i + 1) * 128, :], in_=kvb[:]),
                        wait=[(cv, pkv_cv)], inc=(st, 16))
                kv_stores = st.n
                p.barrier()
                p.flush()
            p.barrier()
            p.flush()
        p.wait("gpsimd", st, kv_stores)
        tag = p.emit("gpsimd", lambda E: E.collective_compute(
            "AllGather", ALU.bypass, replica_groups=[[0, 1], [2, 3], [4, 5], [6, 7]],
            ins=[kvi.ap().opt()], outs=[kvo.ap().opt()]), inc=(cc, 1))


        moe_es = contextlib.ExitStack()
        with moe_es:
            h1 = moe_es.enter_context(nc.sbuf_tensor("h1", [128, 16 * TO], BF16))
            xe_fm = moe_es.enter_context(nc.sbuf_tensor("xe_fm", [128, DC * TO], BF16))
            lam_b = moe_es.enter_context(nc.sbuf_tensor("lam_b", [128, TO], BF16))
            xfb2 = [moe_es.enter_context(nc.sbuf_tensor(f"xfb2{i}", [128, TO], BF16))
                    for i in range(2)]
            wgu = [moe_es.enter_context(nc.sbuf_tensor(f"wgu{i}", [128, D], BF16))
                   for i in range(4)]
            wdn = [moe_es.enter_context(nc.sbuf_tensor(f"wdn{i}", [128, D], BF16))
                   for i in range(2)]
            sl1 = moe_es.enter_context(nc.sbuf_tensor("sl1", [128, TO], BF16))
            grow = moe_es.enter_context(nc.sbuf_tensor("grow", [1, TO], BF16))
            tz = p.emit("vector", lambda E: E.memset(out_acc[:], 0.0), inc=(cv, 1))
            guT_t = guT_h.ap()
            dwT_t = dwT_h.ap()
            prev_gdn = None
            pxe = None
            carry_gu = None
            for e in range(8):
                esE2 = contextlib.ExitStack()
                psGm = esE2.enter_context(nc.psum_tensor("psGm_%d" % e, [128, TO], F32))
                psUm = esE2.enter_context(nc.psum_tensor("psUm_%d" % e, [128, TO], F32))
                tgr = p.emit("sync", lambda E, e=e: E.dma_start(
                    out=grow[:1, :], in_=gate_fm[e:e + 1, :]),
                    wait=[(ct, ct.n), (cv, cv.n)], inc=(ld, 16))
                tlb = None
                for s4 in range(4):
                    tlb = p.emit("tensor", lambda E, e=e, s4=s4: E.matmul(
                        psGm[:, s4 * 512:(s4 + 1) * 512], onesrb[:1, :],
                        grow[:1, s4 * 512:(s4 + 1) * 512], start=True, stop=True),
                        wait=[(cv, cv.n), (ld, tgr)] if s4 == 0 else [], inc=(ct, 1))
                tlv = p.emit("vector", lambda E: E.tensor_copy(lam_b[:], psGm[:]),
                             wait=[(ct, tlb)], inc=(cv, 1))
                for d in range(DC):
                    bx = xfb2[d % 2]
                    w = ([(st, st.n)] if (e == 0 and d == 0) else []) + ([pxe] if pxe else [])
                    tx0 = p.emit("sync", lambda E, d=d, bx=bx: E.dma_start(
                        out=bx[:], in_=xf_d.ap()[d, :, :]), wait=w, inc=(ld, 16))
                    tx1 = p.emit("vector", lambda E, d=d, bx=bx: E.tensor_tensor(
                        xe_fm[:, d * TO:(d + 1) * TO], bx[:], lam_b[:], ALU.mult),
                        wait=[(ld, tx0), (cv, tlv)], inc=(cv, 1))
                    pxe = (cv, tx1)
                hprev_sl = None
                mml = []
                for j in range(16):
                    bg, bu = wgu[2 * (j % 2)], wgu[2 * (j % 2) + 1]
                    if j >= 2:
                        wld = [mml[j - 2]]
                    elif carry_gu is not None:
                        wld = [carry_gu]
                    else:
                        wld = []
                    tg0 = p.emit("gpsimd", lambda E, e=e, j=j, bg=bg: E.dma_start(
                        out=bg[:], in_=bass.AP(guT_h, e * D * 2 * FF + j * 128,
                                               [[2 * FF, 128], [128 * 2 * FF, DC], [1, 128]])),
                        wait=wld, inc=(gd, 16))
                    tu0 = p.emit("gpsimd", lambda E, e=e, j=j, bu=bu: E.dma_start(
                        out=bu[:], in_=bass.AP(guT_h, e * D * 2 * FF + FF + j * 128,
                                               [[2 * FF, 128], [128 * 2 * FF, DC], [1, 128]])),
                        inc=(gd, 16))
                    tgm = None
                    silw = [(cs, hprev_sl)] if hprev_sl else []
                    for d in range(DC):
                        for s4 in range(4):
                            tgm = p.emit("tensor", lambda E, d=d, s4=s4, bg=bg: E.matmul(
                                psGm[:, s4 * 512:(s4 + 1) * 512],
                                bg[:, d * 128:(d + 1) * 128],
                                xe_fm[:, d * TO + s4 * 512:d * TO + (s4 + 1) * 512],
                                start=(d == 0), stop=(d == DC - 1)),
                                wait=[(gd, tg0), pxe] + silw if (d == 0 and s4 == 0) else [],
                                inc=(ct, 1) if (d == DC - 1 and s4 == 3) else None)
                    tsl = p.emit("scalar", lambda E: E.activation(
                        sl1[:], psGm[:], AF.Silu), wait=[(ct, tgm), (cv, cv.n)], inc=(cs, 1))
                    hprev_sl = tsl
                    tum = None
                    for d in range(DC):
                        for s4 in range(4):
                            tum = p.emit("tensor", lambda E, d=d, s4=s4, bu=bu: E.matmul(
                                psUm[:, s4 * 512:(s4 + 1) * 512],
                                bu[:, d * 128:(d + 1) * 128],
                                xe_fm[:, d * TO + s4 * 512:d * TO + (s4 + 1) * 512],
                                start=(d == 0), stop=(d == DC - 1)),
                                wait=[(gd, tu0), (cv, cv.n)] if (d == 0 and s4 == 0) else [],
                                inc=(ct, 1) if (d == DC - 1 and s4 == 3) else None)
                    p.emit("vector", lambda E, j=j: E.scalar_tensor_tensor(
                        h1[:, j * TO:(j + 1) * TO], psUm[:], 0.0, sl1[:],
                        ALU.bypass, ALU.mult), wait=[(ct, tum), (cs, tsl)], inc=(cv, 1))
                    mml.append((ct, tum))
                carry_gu = (ct, tum)
                p.flush()
                esE2.close()
                with contextlib.ExitStack() as esD:
                    psD3 = esD.enter_context(nc.psum_tensor("psD3_%d" % e, [128, 4 * D], F32))
                    for tg4 in range(4):
                        mmd = []
                        for jI in range(16):
                            bw = wdn[jI % 2]
                            w = [mmd[jI - 2]] if jI >= 2 else ([prev_gdn] if prev_gdn else [])
                            td0 = p.emit("gpsimd", lambda E, e=e, jI=jI, bw=bw: E.dma_start(
                                out=bw[:], in_=dwT_t[e, jI * 128:(jI + 1) * 128, :]),
                                wait=w, inc=(gd, 16))
                            tmm = None
                            for ts4 in range(4):
                                for nn in range(2):
                                    tmm = p.emit("tensor",
                                        lambda E, jI=jI, tg4=tg4, ts4=ts4, nn=nn, bw=bw:
                                        E.matmul(
                                            psD3[:, ts4 * D + nn * 512:ts4 * D + (nn + 1) * 512],
                                            h1[:, jI * TO + tg4 * 512 + ts4 * 128:
                                               jI * TO + tg4 * 512 + (ts4 + 1) * 128],
                                            bw[:, nn * 512:(nn + 1) * 512],
                                            start=(jI == 0), stop=(jI == 15)),
                                        wait=[(gd, td0), (cv, cv.n)] if (ts4 == 0 and nn == 0) else [],
                                        inc=(ct, 1) if (ts4 == 3 and nn == 1) else None)
                            mmd.append((ct, tmm))
                        accw = [(ct, ct.n)]
                        for ts4 in range(4):
                            lc = tg4 * 4 + ts4
                            if e == 0:
                                ta_ = p.emit("vector", lambda E, ts4=ts4, lc=lc: E.tensor_copy(
                                    out_acc[:, lc * D:(lc + 1) * D],
                                    psD3[:, ts4 * D:(ts4 + 1) * D]),
                                    wait=accw if ts4 == 0 else [], inc=(cv, 1))
                            else:
                                ta_ = p.emit("vector", lambda E, ts4=ts4, lc=lc: E.tensor_tensor(
                                    out_acc[:, lc * D:(lc + 1) * D],
                                    out_acc[:, lc * D:(lc + 1) * D],
                                    psD3[:, ts4 * D:(ts4 + 1) * D], ALU.add),
                                    wait=accw if ts4 == 0 else [], inc=(cv, 1))
                        prev_gdn = (cv, ta_)
            p.barrier()
            p.flush()

        gmoe_es.close()

        with contextlib.ExitStack() as esG1:
            xbG = [esG1.enter_context(nc.sbuf_tensor(f"xbG{i}", [128, D], F32)) for i in range(2)]
            tmpG = esG1.enter_context(nc.sbuf_tensor("tmpG", [128, D], F32))
            prevg = None
            for i in range(NTO):
                buf = xbG[i % 2]
                w = [prevg] if prevg else []
                t0 = p.emit("sync", lambda E, i=i, buf=buf: E.dma_start(
                    out=buf[:], in_=x1t[i]), wait=w, inc=(ld, 16))
                t1 = p.emit("vector", lambda E, i=i, buf=buf: E.tensor_scalar(
                    tmpG[:], buf[:], st1[:, 2 * NTO + i:2 * NTO + i + 1],
                    st1[:, 3 * NTO + i:3 * NTO + i + 1], ALU.add, ALU.mult),
                    wait=[(ld, t0), (cv, cv.n)], inc=(cv, 1))
                t2 = p.emit("vector", lambda E, i=i: E.tensor_tensor(
                    tmpG[:], tmpG[:], out_acc[:, i * D:(i + 1) * D], ALU.add),
                    wait=[(cv, t1)], inc=(cv, 1))
                t3 = p.emit("scalar", lambda E, i=i: E.activation(
                    out_acc[:, i * D:(i + 1) * D], tmpG[:], AF.Silu),
                    wait=[(cv, t2)], inc=(cs, 1))
                prevg = (cs, t3)
            p.barrier()
            p.flush()

        kv_es = contextlib.ExitStack()
        k_fm1 = kv_es.enter_context(nc.sbuf_tensor("k_fm1", [128, TB], BF16))
        v_aug = kv_es.enter_context(nc.sbuf_tensor("v_aug", [128, 32 * 65], BF16))
        with contextlib.ExitStack() as esKV:
            kvt = esKV.enter_context(nc.sbuf_tensor("kvt", [128, 64], BF16))
            k2b = esKV.enter_context(nc.sbuf_tensor("k2b", [64, TB], BF16))
            kn2 = esKV.enter_context(nc.sbuf_tensor("kn2", [1, TB], F32))
            psKT = esKV.enter_context(nc.psum_tensor("psKT", [64, 2048], BF16))
            psN = esKV.enter_context(nc.psum_tensor("psN", [1, 512], F32))
            p.wait("sync", cc, tag)
            p.emit("vector", lambda E: E.memset(v_aug[:], 1.0), inc=(cv, 1))
            vload = p.emit("sync", lambda E: E.dma_start(
                out=bass.AP(v_aug.tensor if hasattr(v_aug, "tensor") else v_aug, 0,
                            [[32 * 65, 128], [65, 32], [1, 64]]),
                in_=bass.AP(kvo, 64, [[128, 128], [128 * 128, 32], [1, 64]]),
                allow_slow_non_contiguous=True), wait=[(cv, cv.n)], inc=(ld, 16))
            prevk = None
            for sc in range(32):
                t0 = p.emit("sync", lambda E, sc=sc: E.dma_start(
                    out=kvt[:], in_=kvo.ap()[sc * 128:(sc + 1) * 128, 0:64]),
                    wait=[prevk] if prevk else [], inc=(ld, 16))
                off = (sc % 2) * 1024
                tt = p.emit("tensor", lambda E, off=off: E.transpose(
                    psKT[:, off:off + 128], kvt[:, 0:64], id_sb[:]),
                    wait=[(ld, t0)], inc=(ct, 1))
                prevk = (cv, p.emit("vector", lambda E, sc=sc, off=off: E.tensor_scalar(
                    k_fm1[0:64, sc * 128:(sc + 1) * 128], psKT[:, off:off + 128],
                    0.125, None, ALU.mult), wait=[(ct, tt)], inc=(cv, 1)))
            kdup = p.emit("sync", lambda E: E.dma_start(
                out=k_fm1[64:128, :], in_=k_fm1[0:64, :]), wait=[prevk], inc=(ld, 16))
            p.flush()

        att_es = contextlib.ExitStack()
        with att_es:
            y_fm = att_es.enter_context(nc.sbuf_tensor("y_fm", [128, DC * TO], BF16))
            att2_es = att_es.enter_context(contextlib.ExitStack())
            q_fm1 = att2_es.enter_context(nc.sbuf_tensor("q_fm1", [128, DC * TO], BF16))
            eT1 = att2_es.enter_context(nc.sbuf_tensor("eTL1", [128, 2 * TO], BF16))
            dm_sb = att2_es.enter_context(nc.sbuf_tensor("dm_sb", [128, 64 * 128], BF16))
            srowB = att2_es.enter_context(nc.sbuf_tensor("srowB", [1, TO], F32))
            srow2 = att2_es.enter_context(nc.sbuf_tensor("srow2", [1, TO], BF16))
            invSb = att2_es.enter_context(nc.sbuf_tensor("invSb", [64, TO], BF16))
            ystage = att2_es.enter_context(nc.sbuf_tensor("ystage", [64, TO], BF16))
            q2b = att2_es.enter_context(nc.sbuf_tensor("q2b", [128, TO], BF16))
            ind2_sb = att2_es.enter_context(nc.sbuf_tensor("ind2_sb", [128, 2], BF16))
            psY = att2_es.enter_context(nc.psum_tensor("psY", [65, TO], F32))
            qld = []
            for qc in range(DC):
                qld.append(p.emit("sync", lambda E, qc=qc: E.dma_start(
                    out=q_fm1[:, qc * TO:(qc + 1) * TO], in_=qf_d.ap()[qc, :, :]),
                    inc=(ld, 16)))
            tdm = p.emit("sync", lambda E: E.dma_start(
                out=dm_sb[:], in_=bass.AP(diagm_h, 0, [[128, 128], [128 * 128, 64], [1, 128]])),
                inc=(ld, 16))
            ti2 = p.emit("sync", lambda E: E.dma_start(out=ind2_sb[:], in_=ind2_h.ap()),
                         inc=(ld, 16))
            with contextlib.ExitStack() as esB0:
                psQ2 = esB0.enter_context(nc.psum_tensor("psQ2", [2, TO], F32))
                prevq2 = None
                for qc in range(DC):
                    tsq = p.emit("scalar", lambda E, qc=qc: E.activation(
                        q2b[:], q_fm1[:, qc * TO:(qc + 1) * TO], AF.Square),
                        wait=[(ld, qld[qc])] + ([prevq2] if prevq2 else []), inc=(cs, 1))
                    tm = None
                    for s4 in range(4):
                        tm = p.emit("tensor", lambda E, s4=s4: E.matmul(
                            psQ2[:, s4 * 512:(s4 + 1) * 512], ind2_sb[:],
                            q2b[:, s4 * 512:(s4 + 1) * 512], start=True, stop=True),
                            wait=[(cs, tsq), (ld, ti2)] if s4 == 0 else [], inc=(ct, 1))
                    prevq2 = (cv, p.emit("vector", lambda E, qc=qc: E.tensor_reduce(
                        tiny[0:2, 57 + qc:58 + qc], psQ2[:], AX.X, ALU.max),
                        wait=[(ct, tm)], inc=(cv, 1)))
                tk2 = p.emit("scalar", lambda E: E.activation(
                    k2b[:], k_fm1[0:64, :], AF.Square),
                    wait=[(cv, cv.n), (ld, kdup)], inc=(cs, 1))
                kn_ev = []
                for s8 in range(8):
                    tm = p.emit("tensor", lambda E, s8=s8: E.matmul(
                        psQ2[0:1, s8 % 4 * 512:(s8 % 4 + 1) * 512],
                        ones_col[0:64, 0:1],
                        k2b[:, s8 * 512:(s8 + 1) * 512], start=True, stop=True),
                        wait=[(cs, tk2)] + ([(cv, kn_ev[s8 - 4])] if s8 >= 4 else []),
                        inc=(ct, 1))
                    kn_ev.append(p.emit("vector", lambda E, s8=s8: E.tensor_copy(
                        kn2[:, s8 * 512:(s8 + 1) * 512],
                        psQ2[0:1, s8 % 4 * 512:(s8 % 4 + 1) * 512]),
                        wait=[(ct, tm)], inc=(cv, 1)))
                c1_ = p.emit("vector", lambda E: E.tensor_reduce(
                    tiny[0:2, 56:57], tiny[0:2, 57:57 + DC], AX.X, ALU.max),
                    wait=[(cv, cv.n)], inc=(cv, 1))
                tt = p.emit("tensor", lambda E: E.transpose(
                    psQ2[0:1, 0:2], tiny[0:2, 56:57], id_f[0:2, 0:2]),
                    wait=[(cv, c1_)], inc=(ct, 1))
                c2_ = p.emit("vector", lambda E: E.reduce_max(
                    row11[:, 12:13], psQ2[0:1, 0:2], axis=AX.X),
                    wait=[(ct, tt)], inc=(cv, 1))
                c3_ = p.emit("vector", lambda E: E.reduce_max(
                    row11[:, 13:14], kn2[:, :], axis=AX.X), wait=[(cv, c2_)], inc=(cv, 1))
                c4_ = p.emit("scalar", lambda E: E.activation(
                    row11[:, 14:15], row11[:, 12:13], AF.Sqrt), wait=[(cv, c3_)], inc=(cs, 1))
                c5_ = p.emit("scalar", lambda E: E.activation(
                    row11[:, 15:16], row11[:, 13:14], AF.Sqrt), wait=[(cs, c4_)], inc=(cs, 1))
                c6_ = p.emit("vector", lambda E: E.tensor_tensor(
                    row11[:, 11:12], row11[:, 14:15], row11[:, 15:16], ALU.mult),
                    wait=[(cs, c5_)], inc=(cv, 1))
                c7_ = p.emit("vector", lambda E: E.tensor_scalar(
                    row11[:, 10:11], row11[:, 11:12], -1.0, None, ALU.mult),
                    wait=[(cv, c6_)], inc=(cv, 1))
            with contextlib.ExitStack() as esB0b:
                psB0 = esB0b.enter_context(nc.psum_tensor("psB0", [128, 1], F32))
                tmB2 = p.emit("tensor", lambda E: E.matmul(
                    psB0[:, 0:1], ones_row[:1, :], row11[:, 10:11],
                    start=True, stop=True), wait=[(cv, c7_)], inc=(ct, 1))
                b0done = p.emit("vector", lambda E: E.tensor_copy(
                    scal[:, 8:9], psB0[:, 0:1]), wait=[(ct, tmB2)], inc=(cv, 1))
            p.flush()
            prev_ev = {}
            for hh in range(NHEAD):
                hp, hr = hh // 2, (hh % 2) * 64
                with contextlib.ExitStack() as esH1:
                    esS1 = esH1.enter_context(contextlib.ExitStack())
                    psS1 = esS1.enter_context(nc.psum_tensor("psS1_%d" % hh, [128, TO], F32))
                    prev_exp1 = prev_ev.get("exp")
                    prev_avl = None
                    first_touch = True
                    for sc in range(32):
                        j0 = sc // 2
                        lo = j0 * 128
                        ebuf = eT1[:, (sc % 2) * TO:(sc % 2) * TO + TO]
                        segs = []
                        s0 = lo
                        while s0 < TO:
                            s1 = min((s0 // 512 + 1) * 512, TO)
                            segs.append((s0, s1))
                            s0 = s1
                        tsc = None
                        for gi, (s0, s1) in enumerate(segs):
                            w = []
                            if gi == 0:
                                w = [(cv, cv.n), (ld, tdm)]
                                if prev_exp1 is not None:
                                    w.append((cs, prev_exp1))
                            masks = []
                            for m in range(2):
                                j = j0 + m
                                if j <= 15 and s0 <= j * 128 < s1:
                                    masks.append((m, j * 128))
                            tsc = p.emit("tensor", lambda E, sc=sc, s0=s0, s1=s1, hr=hr, hp=hp:
                                E.matmul(
                                    psS1[:, s0:s1],
                                    k_fm1[hr:hr + 64, agrow(sc):agrow(sc) + 128],
                                    q_fm1[hr:hr + 64, hp * TO + s0:hp * TO + s1],
                                    start=True, stop=(not masks)), wait=w, inc=(ct, 1))
                            for mi, (m, mlo) in enumerate(masks):
                                tsc = p.emit("tensor", lambda E, sc=sc, m=m, mlo=mlo,
                                             last=(mi == len(masks) - 1):
                                    E.matmul(
                                        psS1[:, mlo:mlo + 128], id_sb[:],
                                        dm_sb[:, (sc * 2 + m) * 128:(sc * 2 + m + 1) * 128],
                                        start=False, stop=last), inc=(ct, 1))
                            if masks:
                                pass
                        texp1 = p.emit("scalar", lambda E, lo=lo, ebuf=ebuf: E.activation(
                            ebuf[:, lo:TO], psS1[:, lo:TO], AF.Exp, bias=scal[:, 8:9]),
                            wait=[(ct, tsc), (cv, b0done)] + (
                                [prev_avl] if prev_avl else []), inc=(cs, 1))
                        prev_exp1 = texp1
                        tav = None
                        for gi, (s0, s1) in enumerate(segs):
                            bnk = s0 // 512
                            stp = (sc == min(31, 8 * bnk + 7))
                            avw = [(cs, texp1)] if gi == 0 else []
                            if gi == 0 and first_touch and "y" in prev_ev:
                                avw.append(prev_ev["y"])
                            tav = p.emit("tensor", lambda E, sc=sc, s0=s0, s1=s1, ebuf=ebuf,
                                         stp=stp:
                                E.matmul(
                                    psY[:, s0:s1],
                                    v_aug[:, agrow(sc) // 128 * 65:agrow(sc) // 128 * 65 + 65],
                                    ebuf[:, s0:s1],
                                    start=first_touch, stop=stp),
                                wait=avw, inc=(ct, 1))
                        first_touch = False
                        prev_avl = (ct, tav)
                    prev_ev["exp"] = prev_exp1
                    p.flush()
                    esS1.close()
                    e2 = p.emit("vector", lambda E: E.reciprocal(
                        srowB[:, :], psY[64:65, :]), wait=[(ct, ct.n)], inc=(cv, 1))
                    e3 = p.emit("vector", lambda E: E.tensor_copy(
                        srow2[:, :], srowB[:, :]), wait=[(cv, e2)], inc=(cv, 1))
                    with contextlib.ExitStack() as esB3:
                        psB3 = esB3.enter_context(nc.psum_tensor("psB3_%d" % hh, [64, TO], F32))
                        bm = None
                        for s4 in range(4):
                            bm = p.emit("tensor", lambda E, s4=s4: E.matmul(
                                psB3[:, s4 * 512:(s4 + 1) * 512], onesrb[:1, 0:64],
                                srow2[:, s4 * 512:(s4 + 1) * 512], start=True, stop=True),
                                wait=[(cv, e3)] if s4 == 0 else [], inc=(ct, 1))
                        e4 = p.emit("vector", lambda E: E.tensor_copy(
                            invSb[:, :], psB3[:]), wait=[(ct, bm)], inc=(cv, 1))
                        if hr == 0:
                            e5 = p.emit("vector", lambda E, hp=hp: E.tensor_tensor(
                                y_fm[0:64, hp * TO:(hp + 1) * TO], psY[0:64, :],
                                invSb[:], ALU.mult), wait=[(cv, e4)], inc=(cv, 1))
                            prev_ev["y"] = (cv, e5)
                        else:
                            e5 = p.emit("vector", lambda E: E.tensor_tensor(
                                ystage[:, :], psY[0:64, :],
                                invSb[:], ALU.mult),
                                wait=[(cv, e4)] + ([prev_ev["ys"]] if "ys" in prev_ev else []),
                                inc=(cv, 1))
                            e6 = p.emit("sync", lambda E, hp=hp: E.dma_start(
                                out=y_fm[64:128, hp * TO:(hp + 1) * TO], in_=ystage[:, :]),
                                wait=[(cv, e5)], inc=(ld, 16))
                            prev_ev["ys"] = (ld, e6)
                            prev_ev["y"] = (ld, e6)
            p.flush()
            att2_es.close()
            with contextlib.ExitStack() as esO:
                Wcp_bf = esO.enter_context(nc.sbuf_tensor("Wcp_bf", [128, DC * D], BF16))
                xbO = [esO.enter_context(nc.sbuf_tensor(f"xbO{i}", [128, D], F32)) for i in range(2)]
                sgO = esO.enter_context(nc.sbuf_tensor("sgO", [128, D], F32))
                x2O = esO.enter_context(nc.sbuf_tensor("x2O", [128, D], F32))
                psO = esO.enter_context(nc.psum_tensor("psO", [128, D], F32))
                tcp = p.emit("gpsimd", lambda E: E.dma_start(
                    out=Wcp_bf[:], in_=bass.AP(Wcp_h, 0, [[D, 128], [128 * D, DC], [1, D]])),
                    inc=(gd, 16))
                prevo = None
                for i in range(NTO):
                    buf = xbO[i % 2]
                    tx0 = p.emit("sync", lambda E, i=i, buf=buf: E.dma_start(
                        out=buf[:], in_=x1t[i]), wait=[prevo] if prevo else [], inc=(ld, 16))
                    tm = None
                    for hp2 in range(DC):
                        for nn in range(2):
                            tm = p.emit("tensor", lambda E, i=i, hp2=hp2, nn=nn: E.matmul(
                                psO[:, nn * 512:(nn + 1) * 512],
                                y_fm[:, hp2 * TO + i * 128:hp2 * TO + (i + 1) * 128],
                                Wcp_bf[:, hp2 * D + nn * 512:hp2 * D + (nn + 1) * 512],
                                start=(hp2 == 0), stop=(hp2 == DC - 1)),
                                wait=[(cv, cv.n), (gd, tcp)] if (hp2 == 0 and nn == 0) else [],
                                inc=(ct, 1) if (hp2 == DC - 1 and nn == 1) else None)
                    o1 = p.emit("vector", lambda E, i=i: E.scalar_tensor_tensor(
                        sgO[:], psO[:], 0.0, out_acc[:, i * D:(i + 1) * D],
                        ALU.bypass, ALU.mult), wait=[(ct, tm)], inc=(cv, 1))
                    o2 = p.emit("vector", lambda E, i=i, buf=buf: E.tensor_tensor(
                        x2O[:], sgO[:], buf[:], ALU.add),
                        wait=[(cv, o1), (ld, tx0)], inc=(cv, 1))
                    tso = p.emit("sync", lambda E, i=i: E.dma_start(
                        out=out_h.ap()[i * 128:(i + 1) * 128, :], in_=x2O[:]),
                        wait=[(cv, o2)], inc=(st, 16))
                    prevo = (st, tso)
                p.flush()

        p.emit("sync", lambda E: E.nop(), wait=[(st, st.n)])
        p.barrier()
        p.flush()
        kv_es.close()
        L1_es.close()
        return nc



def kernel(**inputs):
    nc = build_program()
    in_maps = [host_inputs(inputs, c) for c in range(8)]
    res = run_bass_kernel_spmd(nc, in_maps, core_ids=list(range(8)))
    outf = np.zeros((4, TB, D), np.float32)
    for c in range(8):
        b, hc = c // 2, c % 2
        outf[b, own_order(hc)] = res.results[c]["out"]
    return outf


# version 05
# speedup vs baseline: 1.0970x; 1.0970x over previous
import contextlib
import numpy as np
import ml_dtypes
import concourse.bass as bass
import concourse.mybir as mybir
from concourse.bass_utils import run_bass_kernel_spmd

F32 = mybir.dt.float32
BF16 = mybir.dt.bfloat16
AF = mybir.ActivationFunctionType
ALU = mybir.AluOpType
AX = mybir.AxisListType

MAGIC = 12582912.0
EPS = 1e-5
NEGBIG = -30000.0
D, DC = 1024, 8
TB, NTB = 4096, 32
TO, NTO = 2048, 16
I0, NI0 = 2816, 22
LAT, NLAT = 512, 4
NHEAD, HD = 16, 64
FF = 2048

ENGS = ("sync", "gpsimd", "vector", "scalar", "tensor")


class Sem:
    __slots__ = ("h", "n")
    def __init__(self, h):
        self.h, self.n = h, 0
    def bump(self, k):
        self.n += k
        return self.n


class P:
    def __init__(self, nc, es):
        self.nc, self.es = nc, es
        self.ops = {e: [] for e in ENGS}
        self._sems = {}
        self.dbg_en = set()
        self.dbg_shapes = {}

    def sem(self, name):
        s = self._sems.get(name)
        if s is None:
            s = Sem(self.es.enter_context(self.nc.semaphore(name)))
            self._sems[name] = s
        return s

    def emit(self, eng, fn, wait=(), inc=None):
        wait = tuple(wait)
        tgt = inc[0].bump(inc[1]) if inc is not None else None
        def op(E, fn=fn, wait=wait, inc=inc):
            for s, v in wait:
                E.wait_ge(s.h, v)
            r = fn(E)
            if inc is not None:
                r.then_inc(inc[0].h, inc[1])
        self.ops[eng].append(op)
        return tgt

    def wait(self, eng, sem, val):
        self.ops[eng].append(lambda E, s=sem, v=val: E.wait_ge(s.h, v))

    def barrier(self, engs=ENGS):
        pb = self.sem("pb")
        tgt = pb.n + len(engs)
        pb.bump(len(engs))
        for eng in engs:
            def op(E, tgt=tgt, pb=pb):
                E.nop().then_inc(pb.h, 1)
                E.wait_ge(pb.h, tgt)
            self.ops[eng].append(op)

    def flush(self):
        if not any(self.ops.values()):
            return
        with self.nc.Block() as blk:
            for eng in ENGS:
                ops = self.ops[eng]
                if ops:
                    def runner(E, ops=tuple(ops)):
                        for op in ops:
                            op(E)
                    getattr(blk, eng)(runner)
        self.ops = {e: [] for e in ENGS}

    def dbg(self, name, src_ap_fn, wait):
        if name not in self.dbg_en:
            return
        st = self.sem("st")
        def mk(E, f=src_ap_fn, name=name):
            src = f()
            t = self.nc.dram_tensor("dbg_" + name,
                                    [int(x) for x in src.shape], src.dtype,
                                    kind="ExternalOutput").ap()
            return E.dma_start(out=t, in_=src)
        self.emit("sync", mk, wait=wait, inc=(st, 16))


def own_order(hc):
    idx = []
    for j in range(16):
        g = 2 * j + hc
        idx.extend(range(g * 128, (g + 1) * 128))
    return np.array(idx, np.int64)


def make_diagm(hc):
    dm = np.zeros((32, 2, 128, 128), np.float32)
    tril = np.tril(np.full((128, 128), NEGBIG, np.float32), -1)
    full = np.full((128, 128), NEGBIG, np.float32)
    for sc in range(32):
        j0 = sc // 2
        for m in range(2):
            j = j0 + m
            if j > 15:
                continue
            g = 2 * j + hc
            if g == sc:
                dm[sc, m] = tril
            elif g < sc:
                dm[sc, m] = full
    return dm.astype(ml_dtypes.bfloat16)


def host_inputs(inputs, core):
    b, hc = core // 2, core % 2
    f32 = lambda a: np.ascontiguousarray(np.asarray(a), dtype=np.float32)
    x = f32(inputs["L0_attn_input_sequence"][b])
    xro = np.ascontiguousarray(np.concatenate([x[own_order(hc)], x[own_order(1 - hc)]], 0))
    pv = np.stack([
        f32(inputs["L0_attn_bk"]), f32(inputs["L0_attn_gk"]), f32(inputs["L0_attn_betak"]),
        f32(inputs["L0_attn_bq"]), f32(inputs["L0_attn_gq"]), f32(inputs["L0_attn_betaq"]),
        f32(inputs["L0_attn_bv"]), f32(inputs["L0_attn_gv"]), f32(inputs["L0_attn_betav"]),
        f32(inputs["L1_ffn_ln_gamma"]), f32(inputs["L1_ffn_ln_beta"]),
        np.zeros(D, np.float32)]).astype(np.float32)
    return {
        "xro": xro,
        "Wk": f32(inputs["L0_attn_Wk"]), "Wq": f32(inputs["L0_attn_Wq"]),
        "Wv": f32(inputs["L0_attn_Wv"]), "pv": pv,
        "clips": np.array([[float(np.asarray(inputs["L0_ffn_clip_x"]).ravel()[0]),
                            float(np.asarray(inputs["L0_ffn_clip_down"]).ravel()[0])]],
                          np.float32),
        "wg": f32(inputs["L0_ffn_w_gate"]), "wu": f32(inputs["L0_ffn_w_up"]),
        "wd": f32(inputs["L0_ffn_w_down"]),
        "Wcd": f32(inputs["L1_attn_Wc_down"]), "Wqu": f32(inputs["L1_attn_Wq_up"]),
        "Wkv": f32(inputs["L1_attn_Wkv_up"]), "Wcp": f32(inputs["L1_attn_Wc_proj"]),
        "rwT": np.ascontiguousarray(f32(inputs["L1_ffn_router_w"]).T),
        "guT": np.ascontiguousarray(f32(inputs["L1_ffn_gate_up_w"]).transpose(0, 2, 1)),
        "dwT": np.ascontiguousarray(f32(inputs["L1_ffn_down_w"]).transpose(0, 2, 1)),
        "idb": np.eye(128, dtype=ml_dtypes.bfloat16),
        "idf": np.eye(128, dtype=np.float32),
        "ind2": np.concatenate([np.tile([[1, 0]], (64, 1)), np.tile([[0, 1]], (64, 1))],
                               0).astype(ml_dtypes.bfloat16),
        "diagm": make_diagm(hc),
    }



def build_program(dbg=(), upto="DONE"):
    nc = bass.Bass("TRN2", target_bir_lowering=False)
    ein = lambda n, s, d: nc.dram_tensor(n, list(s), d, kind="ExternalInput")
    idr = lambda n, s, d: nc.dram_tensor(n, list(s), d)

    xro_h = ein("xro", [TB, D], F32)
    Wk_h = ein("Wk", [D, D], F32)
    Wq_h = ein("Wq", [D, D], F32)
    Wv_h = ein("Wv", [D, D], F32)
    pv_h = ein("pv", [12, D], F32)
    clips_h = ein("clips", [1, 2], F32)
    wg_h = ein("wg", [I0, D], F32)
    wu_h = ein("wu", [I0, D], F32)
    wd_h = ein("wd", [D, I0], F32)
    Wcd_h = ein("Wcd", [D, LAT], F32)
    Wqu_h = ein("Wqu", [LAT, D], F32)
    Wkv_h = ein("Wkv", [LAT, 2 * HD], F32)
    Wcp_h = ein("Wcp", [D, D], F32)
    rwT_h = ein("rwT", [D, 8], F32)
    guT_h = ein("guT", [8, D, 2 * FF], F32)
    dwT_h = ein("dwT", [8, FF, D], F32)
    idb_h = ein("idb", [128, 128], BF16)
    idf_h = ein("idf", [128, 128], F32)
    ind2_h = ein("ind2", [128, 2], BF16)
    diagm_h = ein("diagm", [32, 2, 128, 128], BF16)
    out_h = nc.dram_tensor("out", [TO, D], F32, kind="ExternalOutput")

    wqT_d = idr("wqT_d", [D, I0], BF16)
    wuT_d = idr("wuT_d", [D, I0], BF16)
    wdT_d = idr("wdT_d", [I0, D], BF16)
    x1_d = idr("x1_d", [TO, D], F32)
    g_d = idr("g_d", [TO, D], BF16)
    xf_d = idr("xf_d", [DC, 128, TO], BF16)
    qf_d = idr("qf_d", [DC, 128, TO], BF16)
    cc1i = idr("cc1i", [1, 1], F32)
    cc1o = idr("cc1o", [1, 1], F32)
    cc2i = idr("cc2i", [1, 1], F32)
    cc2o = idr("cc2o", [1, 1], F32)
    kvi = idr("kvi", [TO, 2 * HD], BF16)
    kvo = idr("kvo", [TB, 2 * HD], BF16)

    xro = xro_h.ap().rearrange("(n p) m -> n p m", p=128)
    x1t = x1_d.ap().rearrange("(n p) m -> n p m", p=128)

    es = contextlib.ExitStack()
    with es:
        p = P(nc, es)
        p.dbg_en = set(dbg)
        SB = lambda n, s, d: es.enter_context(nc.sbuf_tensor(n, list(s), d))
        ld = p.sem("ld");  gd = p.sem("gd");  st = p.sem("st")
        cv = p.sem("cv");  cs = p.sem("cs");  ct = p.sem("ct");  cc = p.sem("cc")

        id_sb = SB("id_sb", [128, 128], BF16)
        id_f = SB("id_f", [128, 128], F32)
        ones_row = SB("ones_row", [1, 128], F32)
        ones_col = SB("ones_col", [128, 1], BF16)
        pv_pp = SB("pv_pp", [128, 12 * DC], F32)
        onesrb = SB("onesrb", [1, 128], BF16)
        stat = SB("stat", [128, 4 * NTB], F32)
        st1 = SB("st1", [128, 4 * NTO], F32)
        st2 = SB("st2", [128, 4 * NTO], F32)
        scal = SB("scal", [128, 16], F32)
        row11 = SB("row11", [1, 16], F32)
        clip_sb = SB("clip_sb", [1, 2], F32)
        tiny = SB("tiny", [128, 80], F32)

        def bcast128(dst_ap_fn, src_row_ap, ncols, ps, wait, tag):
            done = []
            for n0 in range(0, ncols, 512):
                n1 = min(n0 + 512, ncols)
                tm = p.emit("tensor", lambda E, n0=n0, n1=n1: E.matmul(
                    ps[:, n0:n1], ones_row[:1, :], src_row_ap[:, n0:n1],
                    start=True, stop=True), wait=wait if n0 == 0 else [], inc=(ct, 1))
                done.append((n0, n1, tm))
            outs = []
            for n0, n1, tm in done:
                tv = p.emit("vector", lambda E, n0=n0, n1=n1: E.tensor_copy(
                    dst_ap_fn(n0, n1), ps[:, n0:n1]), wait=[(ct, tm)], inc=(cv, 1))
                outs.append(tv)
            return outs[-1]

        l0c_es = contextlib.ExitStack()
        brow = l0c_es.enter_context(nc.sbuf_tensor("brow", [1, 3 * D], BF16))
        gvbc = l0c_es.enter_context(nc.sbuf_tensor("gvbc", [128, D], BF16))
        bvbc = l0c_es.enter_context(nc.sbuf_tensor("bvbc", [128, D], BF16))

        with contextlib.ExitStack() as esA:
            pvrow = esA.enter_context(nc.sbuf_tensor("pvrow", [1, D], F32))
            psA = esA.enter_context(nc.psum_tensor("psA", [128, 1024], F32))
            p.emit("sync", lambda E: E.dma_start(out=id_sb[:], in_=idb_h.ap()), inc=(ld, 16))
            p.emit("sync", lambda E: E.dma_start(out=id_f[:], in_=idf_h.ap()), inc=(ld, 16))
            clip_tgt = p.emit("sync", lambda E: E.dma_start(out=clip_sb[:], in_=clips_h.ap()), inc=(ld, 16))
            p.emit("sync", lambda E: E.dma_start(
                out=pv_pp[:], in_=bass.AP(pv_h, 0, [[1, 128], [1024, 12], [128, 8]]),
                allow_slow_non_contiguous=True), inc=(ld, 16))
            p.emit("vector", lambda E: E.memset(ones_row[:], 1.0), inc=(cv, 1))
            p.emit("vector", lambda E: E.memset(ones_col[:], 1.0), inc=(cv, 1))
            p.emit("vector", lambda E: E.memset(onesrb[:], 1.0), inc=(cv, 1))
            for i, r in enumerate((0, 3, 6)):
                t0 = p.emit("sync", lambda E, r=r: E.dma_start(
                    out=pvrow[:], in_=pv_h.ap()[r:r + 1, :]), wait=[(cv, cv.n)], inc=(ld, 16))
                p.emit("vector", lambda E, i=i: E.tensor_copy(
                    brow[:, i * D:(i + 1) * D], pvrow[:]), wait=[(ld, t0)], inc=(cv, 1))
            for rr, dst in ((7, gvbc), (8, bvbc)):
                t0 = p.emit("sync", lambda E, rr=rr: E.dma_start(
                    out=pvrow[:], in_=pv_h.ap()[rr:rr + 1, :]), wait=[(cv, cv.n)], inc=(ld, 16))
                bcast128(lambda n0, n1, dst=dst: dst[:, n0:n1], pvrow,
                         D, psA, [(ld, t0), (cv, cv.n)], "bc")
            p.flush()

        with contextlib.ExitStack() as esB:
            xbuf = [esB.enter_context(nc.sbuf_tensor(f"xbuf{i}", [128, D], F32)) for i in range(2)]
            sq_scr = esB.enter_context(nc.sbuf_tensor("sq_scr", [128, D], F32))
            xnf = esB.enter_context(nc.sbuf_tensor("xnf", [128, D], F32))
            psB = esB.enter_context(nc.psum_tensor("psB", [1, 128], F32))

            def ln_stats(src_tiles, n_tiles, sbuf_stat):
                tgts = []
                for i in range(n_tiles):
                    buf = xbuf[i % 2]
                    w = tgts[i - 2] if i >= 2 else []
                    t0 = p.emit("sync", lambda E, i=i, buf=buf: E.dma_start(
                        out=buf[:], in_=src_tiles[i]), wait=w, inc=(ld, 16))
                    tv = p.emit("vector", lambda E, i=i, buf=buf: E.reduce_sum(
                        sbuf_stat[:, i:i + 1], buf[:], axis=AX.X),
                        wait=[(ld, t0)], inc=(cv, 1))
                    ts_ = p.emit("scalar", lambda E, i=i, buf=buf: E.activation(
                        sq_scr[:], buf[:], AF.Square,
                        accum_out=sbuf_stat[:, n_tiles + i:n_tiles + i + 1]),
                        wait=[(ld, t0)], inc=(cs, 1))
                    tgts.append([(cv, tv), (cs, ts_)])
                n = n_tiles
                p.emit("vector", lambda E, n=n, sb_=sbuf_stat: E.tensor_scalar(
                    sb_[:, 2 * n:3 * n], sb_[:, 0:n], -1.0 / D, None, ALU.mult),
                    wait=[(cs, cs.n), (cv, cv.n)], inc=(cv, 1))
                tmu2 = p.emit("scalar", lambda E, n=n, sb_=sbuf_stat: E.activation(
                    tiny[:, 0:n], sb_[:, 2 * n:3 * n], AF.Square),
                    wait=[(cv, cv.n)], inc=(cs, 1))
                p.emit("vector", lambda E, n=n, sb_=sbuf_stat: E.tensor_scalar(
                    tiny[:, n:2 * n], sb_[:, n:2 * n], 1.0 / D, EPS, ALU.mult, ALU.add),
                    inc=(cv, 1))
                p.emit("vector", lambda E, n=n, sb_=sbuf_stat: E.tensor_tensor(
                    sb_[:, 3 * n:4 * n], tiny[:, n:2 * n], tiny[:, 0:n], ALU.subtract),
                    wait=[(cs, tmu2)], inc=(cv, 1))
                p.emit("vector", lambda E, n=n, sb_=sbuf_stat: E.reciprocal(
                    tiny[:, 0:n], sb_[:, 3 * n:4 * n]), inc=(cv, 1))
                return p.emit("scalar", lambda E, n=n, sb_=sbuf_stat: E.activation(
                    sb_[:, 3 * n:4 * n], tiny[:, 0:n], AF.Sqrt),
                    wait=[(cv, cv.n)], inc=(cs, 1))

            stats_done = ln_stats(xro, NTB, stat)
            p.dbg("stat", lambda: stat[:, :], [(cs, stats_done)])
            lastv = None
            sweep_cv0, sweep_cs0 = cv.n, cs.n
            for i in range(NTO):
                buf = xbuf[i % 2]
                w = [(cv, lastv)] if lastv else [(cv, sweep_cv0), (cs, sweep_cs0)]
                t0 = p.emit("sync", lambda E, i=i, buf=buf: E.dma_start(
                    out=buf[:], in_=xro[i]), wait=w, inc=(ld, 16))
                p.emit("vector", lambda E, i=i, buf=buf: E.tensor_scalar(
                    xnf[:], buf[:], stat[:, 2 * NTB + i:2 * NTB + i + 1],
                    stat[:, 3 * NTB + i:3 * NTB + i + 1], ALU.add, ALU.mult),
                    wait=[(ld, t0), (cs, stats_done)], inc=(cv, 1))
                lastv = p.emit("vector", lambda E, i=i: E.tensor_reduce(
                    tiny[:, i:i + 1], xnf[:], AX.X, ALU.max, apply_absolute_value=True),
                    inc=(cv, 1))
                if i == 0:
                    p.dbg("xnf0", lambda: xnf[:, :], [(cv, cv.n)])
            p.emit("vector", lambda E: E.tensor_reduce(
                scal[:, 0:1], tiny[:, 0:NTO], AX.X, ALU.max),
                wait=[(cv, lastv)], inc=(cv, 1))
            tt = p.emit("tensor", lambda E: E.transpose(
                psB[:, :], scal[:, 0:1], id_f[:]), wait=[(cv, cv.n)], inc=(ct, 1))
            tv = p.emit("vector", lambda E: E.reduce_max(
                row11[:, 0:1], psB[:, :], axis=AX.X), wait=[(ct, tt)], inc=(cv, 1))
            t0 = p.emit("gpsimd", lambda E: E.dma_start(out=cc1i.ap()[:, :], in_=row11[:, 0:1]),
                        wait=[(cv, tv)], inc=(gd, 16))
            p.wait("gpsimd", gd, t0)
            tc = p.emit("gpsimd", lambda E: E.collective_compute(
                "AllReduce", ALU.max, replica_groups=[list(range(8))],
                ins=[cc1i.ap().opt()], outs=[cc1o.ap().opt()]), inc=(cc, 1))
            p.wait("gpsimd", cc, tc)
            c1_done = p.emit("gpsimd", lambda E: E.dma_start(
                out=row11[:, 1:2], in_=cc1o.ap()[:, :]), inc=(gd, 16))
            p.dbg("gmax1", lambda: row11[:, :], [(gd, c1_done)])
            p.flush()

        with contextlib.ExitStack() as esC:
            wbuf = [esC.enter_context(nc.sbuf_tensor(f"wbuf{i}", [128, I0], F32)) for i in range(2)]
            wq_bf = esC.enter_context(nc.sbuf_tensor("wq_bf", [128, I0], BF16))
            wscr = esC.enter_context(nc.sbuf_tensor("wscr", [128, I0], F32))
            wstage = esC.enter_context(nc.sbuf_tensor("wstage", [128, I0], BF16))
            psT = esC.enter_context(nc.psum_tensor("psT", [128, 2048], BF16))

            def quant_tr(w_h, rows, cols, outT_d):
                nt, ncc = rows // 128, cols // 128
                tq_done, st_done = [], []
                for i in range(nt):
                    buf = wbuf[i % 2]
                    w = [(cv, tq_done[i - 2])] if i >= 2 else []
                    t0 = p.emit("sync", lambda E, i=i, buf=buf, w_h=w_h, cols=cols: E.dma_start(
                        out=buf[:, 0:cols], in_=w_h.ap()[i * 128:(i + 1) * 128, :]),
                        wait=w, inc=(ld, 16))
                    r0 = p.emit("vector", lambda E, buf=buf, cols=cols: E.tensor_reduce(
                        tiny[:, 64:65], buf[:, 0:cols], AX.X, ALU.max,
                        apply_absolute_value=True), wait=[(ld, t0)], inc=(cv, 1))
                    r1 = p.emit("vector", lambda E: E.tensor_scalar(
                        tiny[:, 65:66], tiny[:, 64:65], 1.0 / 127.0, None, ALU.mult),
                        wait=[(cv, r0)], inc=(cv, 1))
                    r2 = p.emit("vector", lambda E: E.reciprocal(
                        tiny[:, 66:67], tiny[:, 64:65]), wait=[(cv, r1)], inc=(cv, 1))
                    r3 = p.emit("vector", lambda E: E.tensor_scalar(
                        tiny[:, 67:68], tiny[:, 66:67], 127.0, None, ALU.mult),
                        wait=[(cv, r2)], inc=(cv, 1))
                    tbufdone = p.emit("vector", lambda E, buf=buf, cols=cols: E.tensor_scalar(
                        wscr[:, 0:cols], buf[:, 0:cols], tiny[:, 67:68], MAGIC,
                        ALU.mult, ALU.add), wait=[(cv, r3)], inc=(cv, 1))
                    tq = p.emit("vector", lambda E, cols=cols: E.tensor_scalar(
                        wq_bf[:, 0:cols], wscr[:, 0:cols], MAGIC, tiny[:, 65:66],
                        ALU.subtract, ALU.mult), wait=[(cv, tbufdone)], inc=(cv, 1))
                    tq_done.append(tbufdone)
                    evs = []
                    for c in range(ncc):
                        off = (c % 2) * 1024
                        wt = [(cv, tq)] if c < 2 else [(cv, evs[c - 2])]
                        if c == 0 and st_done:
                            wt = wt + [(st, st_done[-1])]
                        tt = p.emit("tensor", lambda E, c=c, off=off: E.transpose(
                            psT[:, off:off + 128], wq_bf[:, c * 128:(c + 1) * 128],
                            id_sb[:]), wait=wt, inc=(ct, 1))
                        evs.append(p.emit("vector", lambda E, c=c, off=off: E.tensor_copy(
                            wstage[:, c * 128:(c + 1) * 128], psT[:, off:off + 128]),
                            wait=[(ct, tt)] + ([(st, st_done[-1])] if (c == 0 and st_done) else []),
                            inc=(cv, 1)))
                    tm = p.emit("sync", lambda E, i=i, ncc=ncc, outT_d=outT_d, rows=rows,
                                cols=cols: E.dma_start(
                        out=bass.AP(outT_d, i * 128,
                                    [[rows, 128], [128 * rows, ncc], [1, 128]]),
                        in_=wstage[:, 0:ncc * 128]), wait=[(cv, evs[-1])], inc=(st, 16))
                    st_done.append(tm)
            quant_tr(wg_h, I0, D, wqT_d)
            quant_tr(wu_h, I0, D, wuT_d)
            quant_tr(wd_h, D, I0, wdT_d)
            wq_stores = st.n
            p.flush()

        def scales(col_clip, gmax_col, out_inv, out_s, ps):
            t0 = p.emit("scalar", lambda E: E.activation(
                row11[:, 2:3], clip_sb[:, col_clip:col_clip + 1], AF.Sigmoid),
                wait=[(cv, cv.n), (ld, clip_tgt)], inc=(cs, 1))
            t1 = p.emit("vector", lambda E: E.tensor_tensor(
                row11[:, 3:4], row11[:, 2:3], row11[:, gmax_col:gmax_col + 1], ALU.mult),
                wait=[(cs, t0)], inc=(cv, 1))
            t2 = p.emit("vector", lambda E: E.tensor_scalar(
                row11[:, 9:10], row11[:, 3:4], 1.0 / 127.0, None, ALU.mult),
                wait=[(cv, t1)], inc=(cv, 1))
            p.emit("vector", lambda E: E.reciprocal(row11[:, 10:11], row11[:, 9:10]),
                   wait=[(cv, t2)], inc=(cv, 1))
            tv = cv.n
            tm = p.emit("tensor", lambda E, ps=ps: E.matmul(
                ps[:, 0:2], ones_row[:1, :], row11[:, 9:11], start=True, stop=True),
                wait=[(cv, tv)], inc=(ct, 1))
            p.emit("vector", lambda E, ps=ps: E.tensor_copy(
                scal[:, out_s:out_s + 1], ps[:, 0:1]), wait=[(ct, tm)], inc=(cv, 1))
            return p.emit("vector", lambda E, ps=ps: E.tensor_copy(
                scal[:, out_inv:out_inv + 1], ps[:, 1:2]), inc=(cv, 1))

        xq_es = contextlib.ExitStack()
        xq_fm = xq_es.enter_context(nc.sbuf_tensor("xq_fm", [128, DC * TO], BF16))
        with contextlib.ExitStack() as esE:
            xbuf = [esE.enter_context(nc.sbuf_tensor(f"exbuf{i}", [128, D], F32)) for i in range(2)]
            q_scr = esE.enter_context(nc.sbuf_tensor("q_scr", [128, D], F32))
            q_scr2 = esE.enter_context(nc.sbuf_tensor("q_scr2", [128, D], F32))
            xq_bf = esE.enter_context(nc.sbuf_tensor("xq_bf", [128, D], BF16))
            psS = esE.enter_context(nc.psum_tensor("psS", [128, 16], F32))
            psT = esE.enter_context(nc.psum_tensor("psTE", [128, 2048], BF16))
            p.wait("scalar", gd, c1_done)
            sc1_done = scales(0, 1, 2, 3, psS)
            lastq = None
            for i in range(NTO):
                buf = xbuf[i % 2]
                w = [(cv, lastq)] if lastq else []
                t0 = p.emit("sync", lambda E, i=i, buf=buf: E.dma_start(
                    out=buf[:], in_=xro[i]), wait=w, inc=(ld, 16))
                p.emit("vector", lambda E, i=i, buf=buf: E.tensor_scalar(
                    q_scr[:], buf[:], stat[:, 2 * NTB + i:2 * NTB + i + 1],
                    stat[:, 3 * NTB + i:3 * NTB + i + 1], ALU.add, ALU.mult),
                    wait=[(ld, t0), (cv, max(sc1_done, cv.n))], inc=(cv, 1))
                ta = p.emit("scalar", lambda E: E.activation(
                    q_scr2[:], q_scr[:], AF.Copy, bias=MAGIC, scale=scal[:, 2:3]),
                    wait=[(cv, cv.n)], inc=(cs, 1))
                p.emit("vector", lambda E: E.tensor_scalar(
                    q_scr[:], q_scr2[:], MAGIC, 127.0, ALU.subtract, ALU.min),
                    wait=[(cs, ta)], inc=(cv, 1))
                tq = p.emit("vector", lambda E: E.tensor_scalar(
                    xq_bf[:], q_scr[:], -128.0, scal[:, 3:4], ALU.max, ALU.mult),
                    inc=(cv, 1))
                if i == 0:
                    p.dbg("scal", lambda: scal[:, :], [(cv, tq)])
                    p.dbg("xq_bf0", lambda: xq_bf[:, :], [(cv, tq)])
                evs = []
                for c in range(DC):
                    off = (c % 2) * 1024
                    wt = [(cv, tq)] if c < 2 else [(cv, evs[c - 2])]
                    tt = p.emit("tensor", lambda E, c=c, off=off: E.transpose(
                        psT[:, off:off + 128], xq_bf[:, c * 128:(c + 1) * 128], id_sb[:]),
                        wait=wt, inc=(ct, 1))
                    evs.append(p.emit("vector", lambda E, c=c, i=i, off=off: E.tensor_copy(
                        xq_fm[:, c * TO + i * 128:c * TO + (i + 1) * 128],
                        psT[:, off:off + 128]), wait=[(ct, tt)], inc=(cv, 1)))
                lastq = evs[-1]
            p.dbg("xq_fm", lambda: xq_fm[:, :], [(cv, cv.n)])
            p.flush()

        if upto == "E":
            p.emit("sync", lambda E: E.nop(), wait=[(st, st.n)])
            p.flush()
            xq_es.close()
            return nc

        h_es = contextlib.ExitStack()
        h_sb = h_es.enter_context(nc.sbuf_tensor("h_sb", [128, NI0 * TO], BF16))
        with contextlib.ExitStack() as esF:
            wT_sb = [esF.enter_context(nc.sbuf_tensor(f"wT{i}", [128, D], BF16))
                     for i in range(4)]
            sl_bf = esF.enter_context(nc.sbuf_tensor("sl_bf", [128, TO], BF16))
            psG = esF.enter_context(nc.psum_tensor("psG", [128, TO], F32))
            psU = esF.enter_context(nc.psum_tensor("psU", [128, TO], F32))
            p.wait("sync", st, wq_stores)
            hdone = []
            for i in range(NI0):
                bg, bu = wT_sb[2 * (i % 2)], wT_sb[2 * (i % 2) + 1]
                w = [hdone[i - 2]] if i >= 2 else []
                tg = p.emit("sync", lambda E, i=i, bg=bg: E.dma_start(
                    out=bg[:, 0:D],
                    in_=bass.AP(wqT_d, i * 128, [[I0, 128], [128 * I0, DC], [1, 128]])),
                    wait=w, inc=(ld, 16))
                tu = p.emit("sync", lambda E, i=i, bu=bu: E.dma_start(
                    out=bu[:, 0:D],
                    in_=bass.AP(wuT_d, i * 128, [[I0, 128], [128 * I0, DC], [1, 128]])),
                    inc=(ld, 16))
                sil_prev = cs.n
                hm_prev = cv.n
                tgm = None
                for d in range(DC):
                    for s4 in range(4):
                        wt = [(ld, tg), (cs, sil_prev)] if (d == 0 and s4 == 0) else []
                        tgm = p.emit("tensor", lambda E, d=d, s4=s4, bg=bg: E.matmul(
                            psG[:, s4 * 512:(s4 + 1) * 512],
                            bg[:, d * 128:(d + 1) * 128],
                            xq_fm[:, d * TO + s4 * 512:d * TO + (s4 + 1) * 512],
                            start=(d == 0), stop=(d == DC - 1)), wait=wt,
                            inc=(ct, 1) if (d == DC - 1 and s4 == 3) else None)
                tsl = p.emit("scalar", lambda E: E.activation(
                    sl_bf[:], psG[:], AF.Silu),
                    wait=[(ct, tgm), (cv, hm_prev)], inc=(cs, 1))
                tum = None
                for d in range(DC):
                    for s4 in range(4):
                        wt = [(ld, tu), (cv, hm_prev)] if (d == 0 and s4 == 0) else []
                        tum = p.emit("tensor", lambda E, d=d, s4=s4, bu=bu: E.matmul(
                            psU[:, s4 * 512:(s4 + 1) * 512],
                            bu[:, d * 128:(d + 1) * 128],
                            xq_fm[:, d * TO + s4 * 512:d * TO + (s4 + 1) * 512],
                            start=(d == 0), stop=(d == DC - 1)), wait=wt,
                            inc=(ct, 1) if (d == DC - 1 and s4 == 3) else None)
                th = p.emit("vector", lambda E, i=i: E.scalar_tensor_tensor(
                    h_sb[:, i * TO:(i + 1) * TO], psU[:], 0.0, sl_bf[:],
                    ALU.bypass, ALU.mult), wait=[(ct, tum), (cs, tsl)], inc=(cv, 1))
                th2 = p.emit("vector", lambda E, i=i: E.tensor_reduce(
                    tiny[:, i:i + 1], h_sb[:, i * TO:(i + 1) * TO], AX.X, ALU.max,
                    apply_absolute_value=True), inc=(cv, 1))
                hdone.append((cv, th2))
            p.emit("vector", lambda E: E.tensor_reduce(
                scal[:, 4:5], tiny[:, 0:NI0], AX.X, ALU.max),
                wait=[hdone[-1]], inc=(cv, 1))
            p.flush()
        with contextlib.ExitStack() as esF2:
            psB2 = esF2.enter_context(nc.psum_tensor("psB2", [1, 128], F32))
            tt = p.emit("tensor", lambda E: E.transpose(
                psB2[:, :], scal[:, 4:5], id_f[:]), wait=[(cv, cv.n)], inc=(ct, 1))
            tv = p.emit("vector", lambda E: E.reduce_max(
                row11[:, 5:6], psB2[:, :], axis=AX.X), wait=[(ct, tt)], inc=(cv, 1))
            t0 = p.emit("gpsimd", lambda E: E.dma_start(
                out=cc2i.ap()[:, :], in_=row11[:, 5:6]), wait=[(cv, tv)], inc=(gd, 16))
            p.wait("gpsimd", gd, t0)
            tc = p.emit("gpsimd", lambda E: E.collective_compute(
                "AllReduce", ALU.max, replica_groups=[list(range(8))],
                ins=[cc2i.ap().opt()], outs=[cc2o.ap().opt()]), inc=(cc, 1))
            p.wait("gpsimd", cc, tc)
            c2_done = p.emit("gpsimd", lambda E: E.dma_start(
                out=row11[:, 6:7], in_=cc2o.ap()[:, :]), inc=(gd, 16))
            p.dbg("gmax2", lambda: row11[:, :], [(gd, c2_done)])
            p.dbg("h_sb", lambda: h_sb[:, :], [(cv, cv.n)])
            p.flush()

        with contextlib.ExitStack() as esH0:
            psS2 = esH0.enter_context(nc.psum_tensor("psS2", [128, 16], F32))
            p.wait("scalar", gd, c2_done)
            sc2_done = scales(1, 6, 6, 7, psS2)
            p.flush()
        with contextlib.ExitStack() as esH:
            wdT_sb = [esH.enter_context(nc.sbuf_tensor(f"wdT{i}", [128, D], BF16))
                      for i in range(2)]
            hq_f = [esH.enter_context(nc.sbuf_tensor(f"hq_f{i}", [128, 512], F32)) for i in range(2)]
            hq_g = [esH.enter_context(nc.sbuf_tensor(f"hq_g{i}", [128, 512], F32)) for i in range(2)]
            hq_bf = [esH.enter_context(nc.sbuf_tensor(f"hq_bf{i}", [128, 512], BF16)) for i in range(2)]
            psD = esH.enter_context(nc.psum_tensor("psD", [128, 4 * D], F32))
            gstage = esH.enter_context(nc.sbuf_tensor("gstage", [128, 4 * D], BF16))
            gst = []
            carry_mm = None
            carry_sil = None
            for tg4 in range(4):
                mm_done = []
                for i in range(NI0):
                    bw = wdT_sb[i % 2]
                    hf, hg, hb = hq_f[i % 2], hq_g[i % 2], hq_bf[i % 2]
                    w = [mm_done[i - 2]] if i >= 2 else ([carry_mm] if carry_mm else [])
                    t0 = p.emit("sync", lambda E, i=i, bw=bw: E.dma_start(
                        out=bw[:], in_=wdT_d.ap()[i * 128:(i + 1) * 128, :]),
                        wait=w, inc=(ld, 16))
                    ta = p.emit("scalar", lambda E, i=i, tg4=tg4, hf=hf: E.activation(
                        hf[:], h_sb[:, i * TO + tg4 * 512:i * TO + tg4 * 512 + 512],
                        AF.Copy, bias=MAGIC, scale=scal[:, 6:7]),
                        wait=[(cv, max(sc2_done, cv.n))], inc=(cs, 1))
                    p.emit("vector", lambda E, hf=hf, hg=hg: E.tensor_scalar(
                        hg[:], hf[:], MAGIC, 127.0, ALU.subtract, ALU.min),
                        wait=[(cs, ta)], inc=(cv, 1))
                    tq = p.emit("vector", lambda E, hg=hg, hb=hb: E.tensor_scalar(
                        hb[:], hg[:], -128.0, scal[:, 7:8], ALU.max, ALU.mult),
                        wait=[mm_done[i - 2]] if i >= 2 else ([carry_mm] if carry_mm else []),
                        inc=(cv, 1))
                    tmm = None
                    for ts4 in range(4):
                        for nn2 in range(2):
                            wt = ([(ld, t0), (cv, tq)] +
                                  ([carry_sil] if (carry_sil and i == 0) else [])
                                  ) if (ts4 == 0 and nn2 == 0) else []
                            tmm = p.emit("tensor", lambda E, i=i, ts4=ts4, nn2=nn2, bw=bw,
                                         hb=hb:
                                E.matmul(
                                    psD[:, ts4 * D + nn2 * 512:ts4 * D + (nn2 + 1) * 512],
                                    hb[:, ts4 * 128:(ts4 + 1) * 128],
                                    bw[:, nn2 * 512:(nn2 + 1) * 512],
                                    start=(i == 0), stop=(i == NI0 - 1)), wait=wt,
                                inc=(ct, 1) if (ts4 == 3 and nn2 == 1) else None)
                    mm_done.append((ct, tmm))
                tdn = ct.n
                tsil = None
                for ts4 in range(4):
                    lc = tg4 * 4 + ts4
                    wg_ = [(st, gst[-1])] if gst else []
                    tsil = p.emit("scalar", lambda E, ts4=ts4: E.activation(
                        gstage[:, ts4 * D:(ts4 + 1) * D], psD[:, ts4 * D:(ts4 + 1) * D],
                        AF.Silu), wait=[(ct, tdn)] + (wg_ if ts4 == 0 else []), inc=(cs, 1))
                    gst.append(p.emit("sync", lambda E, ts4=ts4, lc=lc: E.dma_start(
                        out=g_d.ap()[lc * 128:(lc + 1) * 128, :],
                        in_=gstage[:, ts4 * D:(ts4 + 1) * D]),
                        wait=[(cs, tsil)], inc=(st, 16)))
                carry_mm = (ct, tdn)
                carry_sil = (cs, tsil)
            p.flush()
        h_es.close()
        xq_es.close()

        if upto == "H":
            p.emit("sync", lambda E: E.nop(), wait=[(st, st.n)])
            p.barrier()
            p.flush()
            return nc


        katt_es = contextlib.ExitStack()
        k_fm = katt_es.enter_context(nc.sbuf_tensor("k_fm", [128, DC * TO], BF16))
        acc = katt_es.enter_context(nc.sbuf_tensor("acc", [128, NTO * D], BF16))
        fls = katt_es.enter_context(nc.sbuf_tensor("fls", [128, 64], F32))
        mcol, scolf, rcol = fls[:, 0:16], fls[:, 16:32], fls[:, 32:48]

        def ln_mini(ybuf, sqscr, wait_ev):
            t1 = p.emit("vector", lambda E: E.reduce_sum(
                tiny[:, 68:69], ybuf[:], axis=AX.X), wait=wait_ev, inc=(cv, 1))
            t2 = p.emit("scalar", lambda E: E.activation(
                sqscr[:], ybuf[:], AF.Square, accum_out=tiny[:, 69:70]),
                wait=wait_ev, inc=(cs, 1))
            t3 = p.emit("vector", lambda E: E.tensor_scalar(
                tiny[:, 70:71], tiny[:, 68:69], -1.0 / D, None, ALU.mult),
                wait=[(cv, t1)], inc=(cv, 1))
            t4 = p.emit("scalar", lambda E: E.activation(
                tiny[:, 71:72], tiny[:, 70:71], AF.Square), wait=[(cv, t3)], inc=(cs, 1))
            t5 = p.emit("vector", lambda E: E.tensor_scalar(
                tiny[:, 72:73], tiny[:, 69:70], 1.0 / D, EPS, ALU.mult, ALU.add),
                wait=[(cs, t2)], inc=(cv, 1))
            t6 = p.emit("vector", lambda E: E.tensor_tensor(
                tiny[:, 73:74], tiny[:, 72:73], tiny[:, 71:72], ALU.subtract),
                wait=[(cv, t5), (cs, t4)], inc=(cv, 1))
            t7 = p.emit("vector", lambda E: E.reciprocal(
                tiny[:, 74:75], tiny[:, 73:74]), wait=[(cv, t6)], inc=(cv, 1))
            t8 = p.emit("scalar", lambda E: E.activation(
                tiny[:, 75:76], tiny[:, 74:75], AF.Sqrt), wait=[(cv, t7)], inc=(cs, 1))
            return tiny[:, 70:71], tiny[:, 75:76], (cs, t8)

        def tr8(src_bf, dst_fn, psTx, wait_src, fuse=None):
            evs = []
            for c in range(DC):
                off = (c % 2) * 1024
                wt = list(wait_src) if c < 2 else [(cv, evs[c - 2])]
                tt = p.emit("tensor", lambda E, c=c, off=off, s=src_bf: E.transpose(
                    psTx[:, off:off + 128], s[:, c * 128:(c + 1) * 128], id_sb[:]),
                    wait=wt, inc=(ct, 1))
                if fuse is None:
                    ev = p.emit("vector", lambda E, c=c, off=off: E.tensor_copy(
                        dst_fn(c), psTx[:, off:off + 128]), wait=[(ct, tt)], inc=(cv, 1))
                else:
                    gr, br = fuse
                    ev = p.emit("vector", lambda E, c=c, off=off, gr=gr, br=br: E.tensor_scalar(
                        dst_fn(c), psTx[:, off:off + 128],
                        pv_pp[:, gr * DC + c:gr * DC + c + 1],
                        pv_pp[:, br * DC + c:br * DC + c + 1], ALU.mult, ALU.add),
                        wait=[(ct, tt)], inc=(cv, 1))
                evs.append(ev)
            return evs

        def proj_tm(ps, xnT, W_bf, brow_ix, wait_first):
            for d in range(DC):
                for nn in range(2):
                    p.emit("tensor", lambda E, d=d, nn=nn, ps=ps, xnT=xnT, W_bf=W_bf:
                        E.matmul(ps[:, nn * 512:(nn + 1) * 512],
                                 xnT[:, d * 128:(d + 1) * 128],
                                 W_bf[:, d * D + nn * 512:d * D + (nn + 1) * 512],
                                 start=(d == 0), stop=False),
                        wait=wait_first if (d == 0 and nn == 0) else [])
            tb = None
            for nn in range(2):
                tb = p.emit("tensor", lambda E, nn=nn, ps=ps, brow_ix=brow_ix: E.matmul(
                    ps[:, nn * 512:(nn + 1) * 512], onesrb[:1, :],
                    brow[:, brow_ix * D + nn * 512:brow_ix * D + (nn + 1) * 512],
                    start=False, stop=True), inc=(ct, 1) if nn == 1 else None)
            return tb

        with contextlib.ExitStack() as esK:
            xbK = [esK.enter_context(nc.sbuf_tensor(f"xbK{i}", [128, D], F32)) for i in range(2)]
            xnbK = esK.enter_context(nc.sbuf_tensor("xnbK", [128, D], BF16))
            xnTK = esK.enter_context(nc.sbuf_tensor("xnTK", [128, D], BF16))
            ykf = esK.enter_context(nc.sbuf_tensor("ykf", [128, D], F32))
            kb = esK.enter_context(nc.sbuf_tensor("kb", [128, D], BF16))
            sqK = esK.enter_context(nc.sbuf_tensor("sqK", [128, D], F32))
            Wk_bf = esK.enter_context(nc.sbuf_tensor("Wk_bf", [128, DC * D], BF16))
            psTK = esK.enter_context(nc.psum_tensor("psTK", [128, 2048], BF16))
            psK = esK.enter_context(nc.psum_tensor("psK", [128, D], F32))
            wk_t = p.emit("gpsimd", lambda E: E.dma_start(
                out=Wk_bf[:], in_=bass.AP(Wk_h, 0, [[D, 128], [128 * D, DC], [1, D]])),
                inc=(gd, 16))
            prevdone = None
            for i in range(NTO):
                buf = xbK[i % 2]
                w = [prevdone] if prevdone else []
                t0 = p.emit("sync", lambda E, i=i, buf=buf: E.dma_start(
                    out=buf[:], in_=xro[i]), wait=w, inc=(ld, 16))
                txn = p.emit("vector", lambda E, i=i, buf=buf: E.tensor_scalar(
                    xnbK[:], buf[:], stat[:, 2 * NTB + i:2 * NTB + i + 1],
                    stat[:, 3 * NTB + i:3 * NTB + i + 1], ALU.add, ALU.mult),
                    wait=[(ld, t0), (cv, cv.n)], inc=(cv, 1))
                tr8(xnbK, lambda c: xnTK[:, c * 128:(c + 1) * 128], psTK, [(cv, txn)])
                tev = cv.n
                tkm = proj_tm(psK, xnTK, Wk_bf, 0, [(cv, tev), (gd, wk_t), (cv, cv.n)])
                tye = p.emit("vector", lambda E: E.tensor_copy(ykf[:], psK[:]),
                             wait=[(ct, tkm), (cs, cs.n)], inc=(cv, 1))
                negmu, inv, sdone = ln_mini(ykf, sqK, [(cv, tye)])
                tkb = p.emit("vector", lambda E, negmu=negmu, inv=inv: E.tensor_scalar(
                    kb[:], ykf[:], negmu, inv, ALU.add, ALU.mult),
                    wait=[sdone], inc=(cv, 1))
                evs = tr8(kb, (lambda c, i=i: k_fm[:, c * TO + i * 128:c * TO + (i + 1) * 128]
                               ) if False else (lambda c, i=i: k_fm[:, c * TO + i * 128:c * TO + (i + 1) * 128]),
                          psTK, [(cv, tkb)], fuse=(1, 2))
                prevdone = (cv, evs[-1])
            p.barrier()
            p.flush()

        x1_written = []
        for hf in range(2):
            with contextlib.ExitStack() as esQ:
                qh = esQ.enter_context(nc.sbuf_tensor("qh%d" % hf, [128, DC * TO], BF16))
                vh = esQ.enter_context(nc.sbuf_tensor("vh%d" % hf, [128, NTO * D], BF16))
                Wq_bf = esQ.enter_context(nc.sbuf_tensor("Wq_bf%d" % hf, [128, DC * D], BF16))
                Wv_bf = esQ.enter_context(nc.sbuf_tensor("Wv_bf%d" % hf, [128, DC * D], BF16))
                wq_t = p.emit("gpsimd", lambda E: E.dma_start(
                    out=Wq_bf[:], in_=bass.AP(Wq_h, 0, [[D, 128], [128 * D, DC], [1, D]])),
                    inc=(gd, 16))
                wv_t = p.emit("gpsimd", lambda E: E.dma_start(
                    out=Wv_bf[:], in_=bass.AP(Wv_h, 0, [[D, 128], [128 * D, DC], [1, D]])),
                    inc=(gd, 16))
                with contextlib.ExitStack() as esP:
                    xbP = [esP.enter_context(nc.sbuf_tensor(f"xbP{i}_%d" % hf, [128, D], F32)) for i in range(2)]
                    xnbP = esP.enter_context(nc.sbuf_tensor("xnbP%d" % hf, [128, D], BF16))
                    xnTP = esP.enter_context(nc.sbuf_tensor("xnTP%d" % hf, [128, D], BF16))
                    yqf = esP.enter_context(nc.sbuf_tensor("yqf%d" % hf, [128, D], F32))
                    yvf = esP.enter_context(nc.sbuf_tensor("yvf%d" % hf, [128, D], F32))
                    qb = esP.enter_context(nc.sbuf_tensor("qb%d" % hf, [128, D], BF16))
                    sqP = esP.enter_context(nc.sbuf_tensor("sqP%d" % hf, [128, D], F32))
                    psTP = esP.enter_context(nc.psum_tensor("psTP%d" % hf, [128, 2048], BF16))
                    psQ = esP.enter_context(nc.psum_tensor("psQ%d" % hf, [128, D], F32))
                    psV = esP.enter_context(nc.psum_tensor("psV%d" % hf, [128, D], F32))
                    prevq = None
                    for j in range(NTO):
                        ti = hf * NTO + j
                        buf = xbP[j % 2]
                        w = [prevq] if prevq else []
                        t0 = p.emit("sync", lambda E, ti=ti, buf=buf: E.dma_start(
                            out=buf[:], in_=xro[ti]), wait=w, inc=(ld, 16))
                        txn = p.emit("vector", lambda E, ti=ti, buf=buf: E.tensor_scalar(
                            xnbP[:], buf[:], stat[:, 2 * NTB + ti:2 * NTB + ti + 1],
                            stat[:, 3 * NTB + ti:3 * NTB + ti + 1], ALU.add, ALU.mult),
                            wait=[(ld, t0), (cv, cv.n)], inc=(cv, 1))
                        tr8(xnbP, lambda c: xnTP[:, c * 128:(c + 1) * 128], psTP, [(cv, txn)])
                        tev = cv.n
                        tqm = proj_tm(psQ, xnTP, Wq_bf, 1, [(cv, tev), (gd, wq_t), (cv, cv.n)])
                        tvm = proj_tm(psV, xnTP, Wv_bf, 2, [(gd, wv_t), (cv, cv.n)])
                        tqe = p.emit("vector", lambda E: E.tensor_copy(yqf[:], psQ[:]),
                                     wait=[(ct, tqm), (cs, cs.n)], inc=(cv, 1))
                        tve = p.emit("vector", lambda E: E.tensor_copy(yvf[:], psV[:]),
                                     wait=[(ct, tvm), (cs, cs.n)], inc=(cv, 1))
                        negmu, inv, sdone = ln_mini(yqf, sqP, [(cv, tqe)])
                        tqb = p.emit("vector", lambda E, negmu=negmu, inv=inv: E.tensor_scalar(
                            qb[:], yqf[:], negmu, inv, ALU.add, ALU.mult),
                            wait=[sdone], inc=(cv, 1))
                        tr8(qb, (lambda c, j=j: qh[:, c * TO + j * 128:c * TO + (j + 1) * 128]),
                            psTP, [(cv, tqb)], fuse=(4, 5))
                        negmu2, inv2, sdone2 = ln_mini(yvf, sqP, [(cv, tve), (cv, cv.n)])
                        prevq = (cv, p.emit("vector", lambda E, j=j, negmu2=negmu2, inv2=inv2:
                            E.tensor_scalar(vh[:, j * D:(j + 1) * D], yvf[:],
                                            negmu2, inv2, ALU.add, ALU.mult),
                            wait=[sdone2], inc=(cv, 1)))
                    p.barrier()
                    p.flush()
                with contextlib.ExitStack() as esFl:
                    e_bf = esFl.enter_context(nc.sbuf_tensor("e_bf%d" % hf, [128, TO], BF16))
                    eT = esFl.enter_context(nc.sbuf_tensor("eT%d" % hf, [128, TO], BF16))
                    fA = esFl.enter_context(nc.sbuf_tensor("fA%d" % hf, [128, D], F32))
                    fB = esFl.enter_context(nc.sbuf_tensor("fB%d" % hf, [128, D], F32))
                    fC = esFl.enter_context(nc.sbuf_tensor("fC%d" % hf, [128, D], F32))
                    xb3 = esFl.enter_context(nc.sbuf_tensor("xb3%d" % hf, [128, D], F32))
                    gbuf = esFl.enter_context(nc.sbuf_tensor("gbuf%d" % hf, [128, D], BF16))
                    psSC = esFl.enter_context(nc.psum_tensor("psSC%d" % hf, [128, TO], F32))
                    psTF = esFl.enter_context(nc.psum_tensor("psTF%d" % hf, [128, 2048], BF16))
                    psAV = esFl.enter_context(nc.psum_tensor("psAV%d" % hf, [128, D], F32))
                    if hf == 0:
                        p.emit("vector", lambda E: E.memset(mcol, -3.0e38), inc=(cv, 1))
                        p.emit("vector", lambda E: E.memset(scolf, 0.0), inc=(cv, 1))
                        p.emit("vector", lambda E: E.memset(acc[:], 0.0), inc=(cv, 1))
                    prev_exp = None
                    prev_x1 = None
                    prev_tr = None
                    for lc in range(NTO):
                        smw = ([(cs, prev_exp)] if prev_exp else []) + [(cv, cv.n)]
                        tsc = None
                        for c in range(DC):
                            for s4 in range(4):
                                tsc = p.emit("tensor", lambda E, c=c, s4=s4, lc=lc: E.matmul(
                                    psSC[:, s4 * 512:(s4 + 1) * 512],
                                    k_fm[:, c * TO + lc * 128:c * TO + (lc + 1) * 128],
                                    qh[:, c * TO + s4 * 512:c * TO + (s4 + 1) * 512],
                                    start=(c == 0), stop=(c == DC - 1)),
                                    wait=smw if (c == 0 and s4 == 0) else [],
                                    inc=(ct, 1) if (c == DC - 1 and s4 == 3) else None)
                        u1 = p.emit("vector", lambda E: E.tensor_reduce(
                            fls[:, 48:49], psSC[:], AX.X, ALU.max), wait=[(ct, tsc)], inc=(cv, 1))
                        u2 = p.emit("vector", lambda E, lc=lc: E.tensor_tensor(
                            fls[:, 49:50], mcol[:, lc:lc + 1], fls[:, 48:49], ALU.max),
                            wait=[(cv, u1)], inc=(cv, 1))
                        u3 = p.emit("vector", lambda E, lc=lc: E.tensor_tensor(
                            fls[:, 50:51], mcol[:, lc:lc + 1], fls[:, 49:50], ALU.subtract),
                            wait=[(cv, u2)], inc=(cv, 1))
                        u4 = p.emit("scalar", lambda E, lc=lc: E.activation(
                            rcol[:, lc:lc + 1], fls[:, 50:51], AF.Exp),
                            wait=[(cv, u3)], inc=(cs, 1))
                        u5 = p.emit("vector", lambda E: E.tensor_scalar(
                            fls[:, 51:52], fls[:, 49:50], -1.0, None, ALU.mult),
                            wait=[(cv, u3)], inc=(cv, 1))
                        u6 = p.emit("vector", lambda E, lc=lc: E.tensor_copy(
                            mcol[:, lc:lc + 1], fls[:, 49:50]), wait=[(cv, u5)], inc=(cv, 1))
                        texp = p.emit("scalar", lambda E: E.activation(
                            e_bf[:], psSC[:], AF.Exp, bias=fls[:, 51:52],
                            accum_out=fls[:, 52:53]),
                            wait=[(cv, u5)] + ([(ct, prev_tr)] if prev_tr else []),
                            inc=(cs, 1))
                        prev_exp = texp
                        avdone = None
                        evs = []
                        for Lc in range(NTO):
                            off = (Lc % 2) * 1024
                            wt = [(cs, texp)] if Lc < 2 else [(cv, evs[Lc - 2])]
                            tt = p.emit("tensor", lambda E, Lc=Lc, off=off: E.transpose(
                                psTF[:, off:off + 128], e_bf[:, Lc * 128:(Lc + 1) * 128],
                                id_sb[:]), wait=wt, inc=(ct, 1))
                            ev = p.emit("vector", lambda E, Lc=Lc, off=off: E.tensor_copy(
                                eT[:, Lc * 128:(Lc + 1) * 128], psTF[:, off:off + 128]),
                                wait=[(ct, tt)], inc=(cv, 1))
                            evs.append(ev)
                            prev_tr = tt
                            for nn in range(2):
                                avdone = p.emit("tensor", lambda E, Lc=Lc, nn=nn: E.matmul(
                                    psAV[:, nn * 512:(nn + 1) * 512],
                                    eT[:, Lc * 128:(Lc + 1) * 128],
                                    vh[:, Lc * D + nn * 512:Lc * D + (nn + 1) * 512],
                                    start=(Lc == 0), stop=(Lc == NTO - 1)),
                                    wait=[(cv, ev)] if nn == 0 else [],
                                    inc=(ct, 1) if (Lc == NTO - 1 and nn == 1) else None)
                        if hf == 0:
                            ta = p.emit("vector", lambda E, lc=lc: E.tensor_copy(
                                acc[:, lc * D:(lc + 1) * D], psAV[:]),
                                wait=[(ct, avdone)], inc=(cv, 1))
                            p.emit("vector", lambda E, lc=lc: E.tensor_copy(
                                scolf[:, lc:lc + 1], fls[:, 52:53]),
                                wait=[(cs, texp), (cv, ta)], inc=(cv, 1))
                        else:
                            wx = [prev_x1] if prev_x1 else []
                            tg0 = p.emit("sync", lambda E, lc=lc: E.dma_start(
                                out=gbuf[:], in_=g_d.ap()[lc * 128:(lc + 1) * 128, :]),
                                wait=wx, inc=(ld, 16))
                            tx0 = p.emit("sync", lambda E, lc=lc: E.dma_start(
                                out=xb3[:], in_=xro[lc]), inc=(ld, 16))
                            v1_ = p.emit("vector", lambda E, lc=lc: E.scalar_tensor_tensor(
                                fA[:], acc[:, lc * D:(lc + 1) * D], rcol[:, lc:lc + 1],
                                psAV[:], ALU.mult, ALU.add),
                                wait=[(ct, avdone)] + ([prev_x1] if prev_x1 else []),
                                inc=(cv, 1))
                            v2_ = p.emit("vector", lambda E, lc=lc: E.scalar_tensor_tensor(
                                fls[:, 53:54], scolf[:, lc:lc + 1], rcol[:, lc:lc + 1],
                                fls[:, 52:53], ALU.mult, ALU.add),
                                wait=[(cs, texp), (cv, v1_)], inc=(cv, 1))
                            v3_ = p.emit("vector", lambda E: E.reciprocal(
                                fls[:, 54:55], fls[:, 53:54]), wait=[(cv, v2_)], inc=(cv, 1))
                            v4_ = p.emit("vector", lambda E: E.scalar_tensor_tensor(
                                fB[:], fA[:], fls[:, 54:55], gvbc[:], ALU.mult, ALU.mult),
                                wait=[(cv, v3_)], inc=(cv, 1))
                            v5_ = p.emit("vector", lambda E, lc=lc: E.tensor_scalar(
                                fC[:], xb3[:], stat[:, 2 * NTB + lc:2 * NTB + lc + 1],
                                stat[:, 3 * NTB + lc:3 * NTB + lc + 1], ALU.add, ALU.mult),
                                wait=[(ld, tx0), (cv, v4_)], inc=(cv, 1))
                            v6_ = p.emit("vector", lambda E: E.tensor_tensor(
                                fA[:], fC[:], bvbc[:], ALU.add), wait=[(cv, v5_)], inc=(cv, 1))
                            v7_ = p.emit("vector", lambda E: E.tensor_tensor(
                                fC[:], fB[:], fA[:], ALU.add), wait=[(cv, v6_)], inc=(cv, 1))
                            v8_ = p.emit("vector", lambda E: E.tensor_tensor(
                                fB[:], fC[:], gbuf[:], ALU.mult),
                                wait=[(ld, tg0), (cv, v7_)], inc=(cv, 1))
                            v9_ = p.emit("vector", lambda E: E.tensor_tensor(
                                fC[:], fB[:], xb3[:], ALU.add), wait=[(cv, v8_)], inc=(cv, 1))
                            tx1 = p.emit("sync", lambda E, lc=lc: E.dma_start(
                                out=x1t[lc], in_=fC[:]), wait=[(cv, v9_)], inc=(st, 16))
                            if upto == "L0":
                                p.emit("sync", lambda E, lc=lc: E.dma_start(
                                    out=out_h.ap()[lc * 128:(lc + 1) * 128, :], in_=fC[:]),
                                    inc=(st, 16))
                            prev_x1 = (st, st.n)
                            x1_written.append(tx1)
                    p.barrier()
                    p.flush()
        katt_es.close()
        l0c_es.close()

        if upto == "L0":
            p.emit("sync", lambda E: E.nop(), wait=[(st, st.n)])
            p.barrier()
            p.flush()
            return nc

        def agrow(sc):
            return (sc // 2) * 128 + (0 if sc % 2 == 0 else TO)

        L1_es = contextlib.ExitStack()
        out_acc = L1_es.enter_context(nc.sbuf_tensor("out_acc", [128, NTO * D], F32))
        gmoe_es = contextlib.ExitStack()
        gate_fm = gmoe_es.enter_context(nc.sbuf_tensor("gate_fm", [8, TO], BF16))

        with contextlib.ExitStack() as es9:
            xbuf9 = [es9.enter_context(nc.sbuf_tensor(f"xb9{i}", [128, D], F32)) for i in range(2)]
            sq9 = es9.enter_context(nc.sbuf_tensor("sq9", [128, D], F32))
            def ln_stats2(src_tiles, n_tiles, sbuf_stat, xwait):
                tgts = []
                for i in range(n_tiles):
                    buf = xbuf9[i % 2]
                    w = list(xwait) + (tgts[i - 2] if i >= 2 else [])
                    t0 = p.emit("sync", lambda E, i=i, buf=buf: E.dma_start(
                        out=buf[:], in_=src_tiles[i]), wait=w, inc=(ld, 16))
                    tv = p.emit("vector", lambda E, i=i, buf=buf: E.reduce_sum(
                        sbuf_stat[:, i:i + 1], buf[:], axis=AX.X),
                        wait=[(ld, t0)], inc=(cv, 1))
                    ts_ = p.emit("scalar", lambda E, i=i, buf=buf: E.activation(
                        sq9[:], buf[:], AF.Square,
                        accum_out=sbuf_stat[:, n_tiles + i:n_tiles + i + 1]),
                        wait=[(ld, t0)], inc=(cs, 1))
                    tgts.append([(cv, tv), (cs, ts_)])
                n = n_tiles
                p.emit("vector", lambda E, n=n, sb_=sbuf_stat: E.tensor_scalar(
                    sb_[:, 2 * n:3 * n], sb_[:, 0:n], -1.0 / D, None, ALU.mult),
                    wait=[(cs, cs.n), (cv, cv.n)], inc=(cv, 1))
                tmu2 = p.emit("scalar", lambda E, n=n, sb_=sbuf_stat: E.activation(
                    tiny[:, 0:n], sb_[:, 2 * n:3 * n], AF.Square),
                    wait=[(cv, cv.n)], inc=(cs, 1))
                p.emit("vector", lambda E, n=n, sb_=sbuf_stat: E.tensor_scalar(
                    tiny[:, n:2 * n], sb_[:, n:2 * n], 1.0 / D, EPS, ALU.mult, ALU.add),
                    inc=(cv, 1))
                p.emit("vector", lambda E, n=n, sb_=sbuf_stat: E.tensor_tensor(
                    sb_[:, 3 * n:4 * n], tiny[:, n:2 * n], tiny[:, 0:n], ALU.subtract),
                    wait=[(cs, tmu2), (cv, cv.n)], inc=(cv, 1))
                p.emit("vector", lambda E, n=n, sb_=sbuf_stat: E.reciprocal(
                    tiny[:, 0:n], sb_[:, 3 * n:4 * n]), wait=[(cv, cv.n)], inc=(cv, 1))
                return p.emit("scalar", lambda E, n=n, sb_=sbuf_stat: E.activation(
                    sb_[:, 3 * n:4 * n], tiny[:, 0:n], AF.Sqrt),
                    wait=[(cv, cv.n)], inc=(cs, 1))
            st1_done = ln_stats2(x1t, NTO, st1, [(st, st.n)])
            p.barrier()
            p.flush()

        with contextlib.ExitStack() as es10:
            xn1_fm = es10.enter_context(nc.sbuf_tensor("xn1_fm", [128, DC * TO], BF16))
            xf_fm = es10.enter_context(nc.sbuf_tensor("xf_fm", [128, DC * TO], BF16))
            with contextlib.ExitStack() as es10i:
                xb10 = [es10i.enter_context(nc.sbuf_tensor(f"xb10{i}", [128, D], F32)) for i in range(2)]
                xn1f = es10i.enter_context(nc.sbuf_tensor("xn1f", [128, D], F32))
                xn1b = es10i.enter_context(nc.sbuf_tensor("xn1b", [128, D], BF16))
                xfb = es10i.enter_context(nc.sbuf_tensor("xfb", [128, D], BF16))
                sq10 = es10i.enter_context(nc.sbuf_tensor("sq10", [128, D], F32))
                psT10 = es10i.enter_context(nc.psum_tensor("psT10", [128, 2048], BF16))
                prevt = None
                for i in range(NTO):
                    buf = xb10[i % 2]
                    w = [prevt] if prevt else []
                    t0 = p.emit("sync", lambda E, i=i, buf=buf: E.dma_start(
                        out=buf[:], in_=x1t[i]), wait=w, inc=(ld, 16))
                    tx = p.emit("vector", lambda E, i=i, buf=buf: E.tensor_scalar(
                        xn1f[:], buf[:], st1[:, 2 * NTO + i:2 * NTO + i + 1],
                        st1[:, 3 * NTO + i:3 * NTO + i + 1], ALU.add, ALU.mult),
                        wait=[(ld, t0), (cs, max(st1_done, cs.n)), (cv, cv.n)], inc=(cv, 1))
                    txb = p.emit("vector", lambda E: E.tensor_copy(xn1b[:], xn1f[:]),
                                 wait=[(cv, tx)], inc=(cv, 1))
                    tr8(xn1b, (lambda c, i=i: xn1_fm[:, c * TO + i * 128:c * TO + (i + 1) * 128]),
                        psT10, [(cv, txb)])
                    negmu2, inv2, sdone2 = ln_mini(xn1f, sq10, [(cv, tx), (cv, cv.n)])
                    txf = p.emit("vector", lambda E, negmu2=negmu2, inv2=inv2: E.tensor_scalar(
                        xfb[:], xn1f[:], negmu2, inv2, ALU.add, ALU.mult),
                        wait=[sdone2], inc=(cv, 1))
                    evs = tr8(xfb, (lambda c, i=i: xf_fm[:, c * TO + i * 128:c * TO + (i + 1) * 128]),
                              psT10, [(cv, txf)], fuse=(9, 10))
                    prevt = (cv, evs[-1])
                p.barrier()
                p.flush()
            for c_ in range(DC):
                p.emit("sync", lambda E, c_=c_: E.dma_start(
                    out=xf_d.ap()[c_, :, :], in_=xf_fm[:, c_ * TO:(c_ + 1) * TO]),
                    wait=[(cv, cv.n)] if c_ == 0 else [], inc=(st, 16))
            with contextlib.ExitStack() as esR:
                rw_bf = esR.enter_context(nc.sbuf_tensor("rw_bf", [128, DC * 8], BF16))
                lg = esR.enter_context(nc.sbuf_tensor("lg", [128, 8], F32))
                lg2 = esR.enter_context(nc.sbuf_tensor("lg2", [128, 8], F32))
                prb = esR.enter_context(nc.sbuf_tensor("prb", [128, 8], F32))
                gt_tm = esR.enter_context(nc.sbuf_tensor("gt_tm", [128, NTO * 8], BF16))
                psR = esR.enter_context(nc.psum_tensor("psR", [128, 8], F32))
                psRT = esR.enter_context(nc.psum_tensor("psRT", [8, 2048], BF16))
                trw = p.emit("gpsimd", lambda E: E.dma_start(
                    out=rw_bf[:], in_=bass.AP(rwT_h, 0, [[8, 128], [128 * 8, DC], [1, 8]])),
                    inc=(gd, 16))
                for i in range(NTO):
                    tm = None
                    for d in range(DC):
                        tm = p.emit("tensor", lambda E, d=d, i=i: E.matmul(
                            psR[:], xf_fm[:, d * TO + i * 128:d * TO + (i + 1) * 128],
                            rw_bf[:, d * 8:(d + 1) * 8],
                            start=(d == 0), stop=(d == DC - 1)),
                            wait=[(gd, trw), (cv, cv.n)] if d == 0 else [],
                            inc=(ct, 1) if d == DC - 1 else None)
                    a1 = p.emit("vector", lambda E: E.tensor_scalar(
                        lg[:], psR[:], -10000.0, 10000.0, ALU.max, ALU.min),
                        wait=[(ct, tm)], inc=(cv, 1))
                    a2 = p.emit("vector", lambda E: E.tensor_reduce(
                        tiny[:, 48:49], lg[:], AX.X, ALU.max), wait=[(cv, a1)], inc=(cv, 1))
                    a3 = p.emit("vector", lambda E: E.tensor_scalar(
                        tiny[:, 49:50], tiny[:, 48:49], -1.0, None, ALU.mult),
                        wait=[(cv, a2)], inc=(cv, 1))
                    a4 = p.emit("scalar", lambda E: E.activation(
                        lg2[:], lg[:], AF.Exp, bias=tiny[:, 49:50],
                        accum_out=tiny[:, 50:51]), wait=[(cv, a3)], inc=(cs, 1))
                    a5 = p.emit("vector", lambda E: E.tensor_scalar(
                        tiny[:, 51:52], tiny[:, 50:51], 1.0e-8, None, ALU.add),
                        wait=[(cs, a4)], inc=(cv, 1))
                    a6 = p.emit("vector", lambda E: E.reciprocal(
                        tiny[:, 52:53], tiny[:, 51:52]), wait=[(cv, a5)], inc=(cv, 1))
                    a7 = p.emit("vector", lambda E: E.tensor_scalar(
                        prb[:], lg2[:], tiny[:, 52:53], None, ALU.mult),
                        wait=[(cv, a6)], inc=(cv, 1))
                    a8 = p.emit("vector", lambda E: E.tensor_scalar(
                        lg[:], prb[:], 1.0e-8, 1.0, ALU.max, ALU.min),
                        wait=[(cv, a7)], inc=(cv, 1))
                    b1 = p.emit("vector", lambda E: E.tensor_reduce(
                        tiny[:, 53:54], lg[:], AX.X, ALU.max), wait=[(cv, a8)], inc=(cv, 1))
                    b2 = p.emit("vector", lambda E: E.tensor_single_scalar(
                        lg2[:], lg[:], tiny[:, 53:54], ALU.is_ge), wait=[(cv, b1)], inc=(cv, 1))
                    b3 = p.emit("vector", lambda E: E.scalar_tensor_tensor(
                        prb[:], lg2[:], -1.0e30, lg[:], ALU.mult, ALU.add),
                        wait=[(cv, b2)], inc=(cv, 1))
                    b4 = p.emit("vector", lambda E: E.tensor_reduce(
                        tiny[:, 54:55], prb[:], AX.X, ALU.max), wait=[(cv, b3)], inc=(cv, 1))
                    b5 = p.emit("vector", lambda E: E.tensor_single_scalar(
                        lg2[:], lg[:], tiny[:, 54:55], ALU.is_ge), wait=[(cv, b4)], inc=(cv, 1))
                    b6 = p.emit("vector", lambda E, i=i: E.tensor_tensor(
                        gt_tm[:, i * 8:(i + 1) * 8], lg[:], lg2[:], ALU.mult),
                        wait=[(cv, b5)], inc=(cv, 1))
                prevev = None
                for i in range(NTO):
                    wt = [(cv, cv.n)] if prevev is None else [(cv, prevev)]
                    tt = p.emit("tensor", lambda E, i=i: E.transpose(
                        psRT[:, 0:128], gt_tm[:, i * 8:(i + 1) * 8], id_sb[:]),
                        wait=wt, inc=(ct, 1))
                    prevev = p.emit("vector", lambda E, i=i: E.tensor_copy(
                        gate_fm[:, i * 128:(i + 1) * 128], psRT[:, 0:128]),
                        wait=[(ct, tt)], inc=(cv, 1))
                p.barrier()
                p.flush()
            with contextlib.ExitStack() as esL:
                lat_fm = esL.enter_context(nc.sbuf_tensor("lat_fm", [128, NLAT * TO], BF16))
                Wcd_bf = esL.enter_context(nc.sbuf_tensor("Wcd_bf", [128, DC * LAT], BF16))
                Wqu_bf = esL.enter_context(nc.sbuf_tensor("Wqu_bf", [128, NLAT * D], BF16))
                Wkv_bf = esL.enter_context(nc.sbuf_tensor("Wkv_bf", [128, NLAT * 128], BF16))
                qtmp = esL.enter_context(nc.sbuf_tensor("qtmp", [128, TO], BF16))
                kvb = esL.enter_context(nc.sbuf_tensor("kvb", [128, 128], BF16))
                psL = esL.enter_context(nc.psum_tensor("psL", [128, TO], F32))
                psKV = esL.enter_context(nc.psum_tensor("psKV", [128, 128], F32))
                tcd = p.emit("gpsimd", lambda E: E.dma_start(
                    out=Wcd_bf[:], in_=bass.AP(Wcd_h, 0, [[LAT, 128], [128 * LAT, DC], [1, LAT]])),
                    inc=(gd, 16))
                tqu = p.emit("gpsimd", lambda E: E.dma_start(
                    out=Wqu_bf[:], in_=bass.AP(Wqu_h, 0, [[D, 128], [128 * D, NLAT], [1, D]])),
                    inc=(gd, 16))
                tkv = p.emit("gpsimd", lambda E: E.dma_start(
                    out=Wkv_bf[:], in_=bass.AP(Wkv_h, 0, [[128, 128], [128 * 128, NLAT], [1, 128]])),
                    inc=(gd, 16))
                prevev = None
                for L in range(NLAT):
                    tm = None
                    for d in range(DC):
                        for s4 in range(4):
                            tm = p.emit("tensor", lambda E, L=L, d=d, s4=s4: E.matmul(
                                psL[:, s4 * 512:(s4 + 1) * 512],
                                Wcd_bf[:, d * LAT + L * 128:d * LAT + (L + 1) * 128],
                                xn1_fm[:, d * TO + s4 * 512:d * TO + (s4 + 1) * 512],
                                start=(d == 0), stop=(d == DC - 1)),
                                wait=([(gd, tcd)] + ([(cv, prevev)] if prevev else [(cv, cv.n)])
                                      ) if (d == 0 and s4 == 0) else [],
                                inc=(ct, 1) if (d == DC - 1 and s4 == 3) else None)
                    prevev = p.emit("vector", lambda E, L=L: E.tensor_copy(
                        lat_fm[:, L * TO:(L + 1) * TO], psL[:]), wait=[(ct, tm)], inc=(cv, 1))
                qsp = None
                for qc in range(DC):
                    tm = None
                    for L in range(NLAT):
                        for s4 in range(4):
                            tm = p.emit("tensor", lambda E, qc=qc, L=L, s4=s4: E.matmul(
                                psL[:, s4 * 512:(s4 + 1) * 512],
                                Wqu_bf[:, L * D + qc * 128:L * D + (qc + 1) * 128],
                                lat_fm[:, L * TO + s4 * 512:L * TO + (s4 + 1) * 512],
                                start=(L == 0), stop=(L == NLAT - 1)),
                                wait=[(gd, tqu), (cv, prevev)] if (L == 0 and s4 == 0) else [],
                                inc=(ct, 1) if (L == NLAT - 1 and s4 == 3) else None)
                    prevev = p.emit("vector", lambda E: E.tensor_copy(
                        qtmp[:], psL[:]),
                        wait=[(ct, tm)] + ([(st, qsp)] if qsp else []), inc=(cv, 1))
                    qsp = p.emit("sync", lambda E, qc=qc: E.dma_start(
                        out=qf_d.ap()[qc, :, :], in_=qtmp[:]),
                        wait=[(cv, prevev)], inc=(st, 16))
                pkv_cv, pkv_st = None, None
                for i in range(NTO):
                    tm = None
                    for L in range(NLAT):
                        tm = p.emit("tensor", lambda E, i=i, L=L: E.matmul(
                            psKV[:], lat_fm[:, L * TO + i * 128:L * TO + (i + 1) * 128],
                            Wkv_bf[:, L * 128:(L + 1) * 128],
                            start=(L == 0), stop=(L == NLAT - 1)),
                            wait=([(gd, tkv)] + ([(cv, pkv_cv)] if pkv_cv else [(cv, cv.n)])
                                  ) if L == 0 else [],
                            inc=(ct, 1) if L == NLAT - 1 else None)
                    pkv_cv = p.emit("vector", lambda E: E.tensor_copy(
                        kvb[:], psKV[:]),
                        wait=[(ct, tm)] + ([(st, pkv_st)] if pkv_st else []), inc=(cv, 1))
                    pkv_st = p.emit("sync", lambda E, i=i: E.dma_start(
                        out=kvi.ap()[i * 128:(i + 1) * 128, :], in_=kvb[:]),
                        wait=[(cv, pkv_cv)], inc=(st, 16))
                kv_stores = st.n
                p.barrier()
                p.flush()
            p.barrier()
            p.flush()
        p.wait("gpsimd", st, kv_stores)
        tag = p.emit("gpsimd", lambda E: E.collective_compute(
            "AllGather", ALU.bypass, replica_groups=[[0, 1], [2, 3], [4, 5], [6, 7]],
            ins=[kvi.ap().opt()], outs=[kvo.ap().opt()]), inc=(cc, 1))


        moe_es = contextlib.ExitStack()
        with moe_es:
            h1 = moe_es.enter_context(nc.sbuf_tensor("h1", [128, 16 * TO], BF16))
            xe_fm = moe_es.enter_context(nc.sbuf_tensor("xe_fm", [128, DC * TO], BF16))
            lam_b = moe_es.enter_context(nc.sbuf_tensor("lam_b", [128, TO], BF16))
            xfb2 = [moe_es.enter_context(nc.sbuf_tensor(f"xfb2{i}", [128, TO], BF16))
                    for i in range(2)]
            wgu = [moe_es.enter_context(nc.sbuf_tensor(f"wgu{i}", [128, D], BF16))
                   for i in range(4)]
            wdn = [moe_es.enter_context(nc.sbuf_tensor(f"wdn{i}", [128, D], BF16))
                   for i in range(2)]
            sl1 = moe_es.enter_context(nc.sbuf_tensor("sl1", [128, TO], BF16))
            grow = moe_es.enter_context(nc.sbuf_tensor("grow", [1, TO], BF16))
            tz = p.emit("vector", lambda E: E.memset(out_acc[:], 0.0), inc=(cv, 1))
            guT_t = guT_h.ap()
            dwT_t = dwT_h.ap()
            prev_gdn = None
            pxe = None
            carry_gu = None
            for e in range(8):
                esE2 = contextlib.ExitStack()
                psGm = esE2.enter_context(nc.psum_tensor("psGm_%d" % e, [128, TO], F32))
                psUm = esE2.enter_context(nc.psum_tensor("psUm_%d" % e, [128, TO], F32))
                tgr = p.emit("sync", lambda E, e=e: E.dma_start(
                    out=grow[:1, :], in_=gate_fm[e:e + 1, :]),
                    wait=[(ct, ct.n), (cv, cv.n)], inc=(ld, 16))
                tlb = None
                for s4 in range(4):
                    tlb = p.emit("tensor", lambda E, e=e, s4=s4: E.matmul(
                        psGm[:, s4 * 512:(s4 + 1) * 512], onesrb[:1, :],
                        grow[:1, s4 * 512:(s4 + 1) * 512], start=True, stop=True),
                        wait=[(cv, cv.n), (ld, tgr)] if s4 == 0 else [], inc=(ct, 1))
                tlv = p.emit("vector", lambda E: E.tensor_copy(lam_b[:], psGm[:]),
                             wait=[(ct, tlb)], inc=(cv, 1))
                for d in range(DC):
                    bx = xfb2[d % 2]
                    w = ([(st, st.n)] if (e == 0 and d == 0) else []) + ([pxe] if pxe else [])
                    tx0 = p.emit("sync", lambda E, d=d, bx=bx: E.dma_start(
                        out=bx[:], in_=xf_d.ap()[d, :, :]), wait=w, inc=(ld, 16))
                    tx1 = p.emit("vector", lambda E, d=d, bx=bx: E.tensor_tensor(
                        xe_fm[:, d * TO:(d + 1) * TO], bx[:], lam_b[:], ALU.mult),
                        wait=[(ld, tx0), (cv, tlv)], inc=(cv, 1))
                    pxe = (cv, tx1)
                hprev_sl = None
                mml = []
                for j in range(16):
                    bg, bu = wgu[2 * (j % 2)], wgu[2 * (j % 2) + 1]
                    if j >= 2:
                        wld = [mml[j - 2]]
                    elif carry_gu is not None:
                        wld = [carry_gu]
                    else:
                        wld = []
                    tg0 = p.emit("gpsimd", lambda E, e=e, j=j, bg=bg: E.dma_start(
                        out=bg[:], in_=bass.AP(guT_h, e * D * 2 * FF + j * 128,
                                               [[2 * FF, 128], [128 * 2 * FF, DC], [1, 128]])),
                        wait=wld, inc=(gd, 16))
                    tu0 = p.emit("gpsimd", lambda E, e=e, j=j, bu=bu: E.dma_start(
                        out=bu[:], in_=bass.AP(guT_h, e * D * 2 * FF + FF + j * 128,
                                               [[2 * FF, 128], [128 * 2 * FF, DC], [1, 128]])),
                        inc=(gd, 16))
                    tgm = None
                    silw = [(cs, hprev_sl)] if hprev_sl else []
                    for d in range(DC):
                        for s4 in range(4):
                            tgm = p.emit("tensor", lambda E, d=d, s4=s4, bg=bg: E.matmul(
                                psGm[:, s4 * 512:(s4 + 1) * 512],
                                bg[:, d * 128:(d + 1) * 128],
                                xe_fm[:, d * TO + s4 * 512:d * TO + (s4 + 1) * 512],
                                start=(d == 0), stop=(d == DC - 1)),
                                wait=[(gd, tg0), pxe] + silw if (d == 0 and s4 == 0) else [],
                                inc=(ct, 1) if (d == DC - 1 and s4 == 3) else None)
                    tsl = p.emit("scalar", lambda E: E.activation(
                        sl1[:], psGm[:], AF.Silu), wait=[(ct, tgm), (cv, cv.n)], inc=(cs, 1))
                    hprev_sl = tsl
                    tum = None
                    for d in range(DC):
                        for s4 in range(4):
                            tum = p.emit("tensor", lambda E, d=d, s4=s4, bu=bu: E.matmul(
                                psUm[:, s4 * 512:(s4 + 1) * 512],
                                bu[:, d * 128:(d + 1) * 128],
                                xe_fm[:, d * TO + s4 * 512:d * TO + (s4 + 1) * 512],
                                start=(d == 0), stop=(d == DC - 1)),
                                wait=[(gd, tu0), (cv, cv.n)] if (d == 0 and s4 == 0) else [],
                                inc=(ct, 1) if (d == DC - 1 and s4 == 3) else None)
                    p.emit("vector", lambda E, j=j: E.scalar_tensor_tensor(
                        h1[:, j * TO:(j + 1) * TO], psUm[:], 0.0, sl1[:],
                        ALU.bypass, ALU.mult), wait=[(ct, tum), (cs, tsl)], inc=(cv, 1))
                    mml.append((ct, tum))
                carry_gu = (ct, tum)
                p.flush()
                esE2.close()
                with contextlib.ExitStack() as esD:
                    psD3 = esD.enter_context(nc.psum_tensor("psD3_%d" % e, [128, 4 * D], F32))
                    for tg4 in range(4):
                        mmd = []
                        for jI in range(16):
                            bw = wdn[jI % 2]
                            w = [mmd[jI - 2]] if jI >= 2 else ([prev_gdn] if prev_gdn else [])
                            td0 = p.emit("gpsimd", lambda E, e=e, jI=jI, bw=bw: E.dma_start(
                                out=bw[:], in_=dwT_t[e, jI * 128:(jI + 1) * 128, :]),
                                wait=w, inc=(gd, 16))
                            tmm = None
                            for ts4 in range(4):
                                for nn in range(2):
                                    tmm = p.emit("tensor",
                                        lambda E, jI=jI, tg4=tg4, ts4=ts4, nn=nn, bw=bw:
                                        E.matmul(
                                            psD3[:, ts4 * D + nn * 512:ts4 * D + (nn + 1) * 512],
                                            h1[:, jI * TO + tg4 * 512 + ts4 * 128:
                                               jI * TO + tg4 * 512 + (ts4 + 1) * 128],
                                            bw[:, nn * 512:(nn + 1) * 512],
                                            start=(jI == 0), stop=(jI == 15)),
                                        wait=[(gd, td0), (cv, cv.n)] if (ts4 == 0 and nn == 0) else [],
                                        inc=(ct, 1) if (ts4 == 3 and nn == 1) else None)
                            mmd.append((ct, tmm))
                        accw = [(ct, ct.n)]
                        for ts4 in range(4):
                            lc = tg4 * 4 + ts4
                            if e == 0:
                                ta_ = p.emit("vector", lambda E, ts4=ts4, lc=lc: E.tensor_copy(
                                    out_acc[:, lc * D:(lc + 1) * D],
                                    psD3[:, ts4 * D:(ts4 + 1) * D]),
                                    wait=accw if ts4 == 0 else [], inc=(cv, 1))
                            else:
                                ta_ = p.emit("vector", lambda E, ts4=ts4, lc=lc: E.tensor_tensor(
                                    out_acc[:, lc * D:(lc + 1) * D],
                                    out_acc[:, lc * D:(lc + 1) * D],
                                    psD3[:, ts4 * D:(ts4 + 1) * D], ALU.add),
                                    wait=accw if ts4 == 0 else [], inc=(cv, 1))
                        prev_gdn = (cv, ta_)
            p.barrier()
            p.flush()

        gmoe_es.close()

        with contextlib.ExitStack() as esG1:
            xbG = [esG1.enter_context(nc.sbuf_tensor(f"xbG{i}", [128, D], F32)) for i in range(2)]
            tmpG = esG1.enter_context(nc.sbuf_tensor("tmpG", [128, D], F32))
            prevg = None
            for i in range(NTO):
                buf = xbG[i % 2]
                w = [prevg] if prevg else []
                t0 = p.emit("sync", lambda E, i=i, buf=buf: E.dma_start(
                    out=buf[:], in_=x1t[i]), wait=w, inc=(ld, 16))
                t1 = p.emit("vector", lambda E, i=i, buf=buf: E.tensor_scalar(
                    tmpG[:], buf[:], st1[:, 2 * NTO + i:2 * NTO + i + 1],
                    st1[:, 3 * NTO + i:3 * NTO + i + 1], ALU.add, ALU.mult),
                    wait=[(ld, t0), (cv, cv.n)], inc=(cv, 1))
                t2 = p.emit("vector", lambda E, i=i: E.tensor_tensor(
                    tmpG[:], tmpG[:], out_acc[:, i * D:(i + 1) * D], ALU.add),
                    wait=[(cv, t1)], inc=(cv, 1))
                t3 = p.emit("scalar", lambda E, i=i: E.activation(
                    out_acc[:, i * D:(i + 1) * D], tmpG[:], AF.Silu),
                    wait=[(cv, t2)], inc=(cs, 1))
                prevg = (cs, t3)
            p.barrier()
            p.flush()

        kv_es = contextlib.ExitStack()
        k_fm1 = kv_es.enter_context(nc.sbuf_tensor("k_fm1", [128, TB], BF16))
        v_aug = kv_es.enter_context(nc.sbuf_tensor("v_aug", [128, 32 * 65], BF16))
        with contextlib.ExitStack() as esKV:
            kvt = esKV.enter_context(nc.sbuf_tensor("kvt", [128, 64], BF16))
            k2b = esKV.enter_context(nc.sbuf_tensor("k2b", [64, TB], BF16))
            kn2 = esKV.enter_context(nc.sbuf_tensor("kn2", [1, TB], F32))
            psKT = esKV.enter_context(nc.psum_tensor("psKT", [64, 2048], BF16))
            psN = esKV.enter_context(nc.psum_tensor("psN", [1, 512], F32))
            p.wait("sync", cc, tag)
            p.emit("vector", lambda E: E.memset(v_aug[:], 1.0), inc=(cv, 1))
            vload = p.emit("sync", lambda E: E.dma_start(
                out=bass.AP(v_aug.tensor if hasattr(v_aug, "tensor") else v_aug, 0,
                            [[32 * 65, 128], [65, 32], [1, 64]]),
                in_=bass.AP(kvo, 64, [[128, 128], [128 * 128, 32], [1, 64]]),
                allow_slow_non_contiguous=True), wait=[(cv, cv.n)], inc=(ld, 16))
            prevk = None
            for sc in range(32):
                t0 = p.emit("sync", lambda E, sc=sc: E.dma_start(
                    out=kvt[:], in_=kvo.ap()[sc * 128:(sc + 1) * 128, 0:64]),
                    wait=[prevk] if prevk else [], inc=(ld, 16))
                off = (sc % 2) * 1024
                tt = p.emit("tensor", lambda E, off=off: E.transpose(
                    psKT[:, off:off + 128], kvt[:, 0:64], id_sb[:]),
                    wait=[(ld, t0)], inc=(ct, 1))
                prevk = (cv, p.emit("vector", lambda E, sc=sc, off=off: E.tensor_scalar(
                    k_fm1[0:64, sc * 128:(sc + 1) * 128], psKT[:, off:off + 128],
                    0.125, None, ALU.mult), wait=[(ct, tt)], inc=(cv, 1)))
            kdup = p.emit("sync", lambda E: E.dma_start(
                out=k_fm1[64:128, :], in_=k_fm1[0:64, :]), wait=[prevk], inc=(ld, 16))
            p.flush()

        att_es = contextlib.ExitStack()
        with att_es:
            y_fm = att_es.enter_context(nc.sbuf_tensor("y_fm", [128, DC * TO], BF16))
            att2_es = att_es.enter_context(contextlib.ExitStack())
            q_fm1 = att2_es.enter_context(nc.sbuf_tensor("q_fm1", [128, DC * TO], BF16))
            eT1 = att2_es.enter_context(nc.sbuf_tensor("eTL1", [128, 2 * TO], BF16))
            dm_sb = att2_es.enter_context(nc.sbuf_tensor("dm_sb", [128, 64 * 128], BF16))
            srowB = att2_es.enter_context(nc.sbuf_tensor("srowB", [1, TO], F32))
            srow2 = att2_es.enter_context(nc.sbuf_tensor("srow2", [1, TO], BF16))
            invSb = att2_es.enter_context(nc.sbuf_tensor("invSb", [64, TO], BF16))
            ystage = att2_es.enter_context(nc.sbuf_tensor("ystage", [64, TO], BF16))
            q2b = att2_es.enter_context(nc.sbuf_tensor("q2b", [128, TO], BF16))
            ind2_sb = att2_es.enter_context(nc.sbuf_tensor("ind2_sb", [128, 2], BF16))
            qld = []
            for qc in range(DC):
                qld.append(p.emit("sync", lambda E, qc=qc: E.dma_start(
                    out=q_fm1[:, qc * TO:(qc + 1) * TO], in_=qf_d.ap()[qc, :, :]),
                    inc=(ld, 16)))
            tdm = p.emit("sync", lambda E: E.dma_start(
                out=dm_sb[:], in_=bass.AP(diagm_h, 0, [[128, 128], [128 * 128, 64], [1, 128]])),
                inc=(ld, 16))
            ti2 = p.emit("sync", lambda E: E.dma_start(out=ind2_sb[:], in_=ind2_h.ap()),
                         inc=(ld, 16))
            with contextlib.ExitStack() as esB0:
                psQ2 = esB0.enter_context(nc.psum_tensor("psQ2", [2, TO], F32))
                prevq2 = None
                for qc in range(DC):
                    tsq = p.emit("scalar", lambda E, qc=qc: E.activation(
                        q2b[:], q_fm1[:, qc * TO:(qc + 1) * TO], AF.Square),
                        wait=[(ld, qld[qc])] + ([prevq2] if prevq2 else []), inc=(cs, 1))
                    tm = None
                    for s4 in range(4):
                        tm = p.emit("tensor", lambda E, s4=s4: E.matmul(
                            psQ2[:, s4 * 512:(s4 + 1) * 512], ind2_sb[:],
                            q2b[:, s4 * 512:(s4 + 1) * 512], start=True, stop=True),
                            wait=[(cs, tsq), (ld, ti2)] if s4 == 0 else [], inc=(ct, 1))
                    prevq2 = (cv, p.emit("vector", lambda E, qc=qc: E.tensor_reduce(
                        tiny[0:2, 57 + qc:58 + qc], psQ2[:], AX.X, ALU.max),
                        wait=[(ct, tm)], inc=(cv, 1)))
                tk2 = p.emit("scalar", lambda E: E.activation(
                    k2b[:], k_fm1[0:64, :], AF.Square),
                    wait=[(cv, cv.n), (ld, kdup)], inc=(cs, 1))
                kn_ev = []
                for s8 in range(8):
                    tm = p.emit("tensor", lambda E, s8=s8: E.matmul(
                        psQ2[0:1, s8 % 4 * 512:(s8 % 4 + 1) * 512],
                        ones_col[0:64, 0:1],
                        k2b[:, s8 * 512:(s8 + 1) * 512], start=True, stop=True),
                        wait=[(cs, tk2)] + ([(cv, kn_ev[s8 - 4])] if s8 >= 4 else []),
                        inc=(ct, 1))
                    kn_ev.append(p.emit("vector", lambda E, s8=s8: E.tensor_copy(
                        kn2[:, s8 * 512:(s8 + 1) * 512],
                        psQ2[0:1, s8 % 4 * 512:(s8 % 4 + 1) * 512]),
                        wait=[(ct, tm)], inc=(cv, 1)))
                c1_ = p.emit("vector", lambda E: E.tensor_reduce(
                    tiny[0:2, 56:57], tiny[0:2, 57:57 + DC], AX.X, ALU.max),
                    wait=[(cv, cv.n)], inc=(cv, 1))
                tt = p.emit("tensor", lambda E: E.transpose(
                    psQ2[0:1, 0:2], tiny[0:2, 56:57], id_f[0:2, 0:2]),
                    wait=[(cv, c1_)], inc=(ct, 1))
                c2_ = p.emit("vector", lambda E: E.reduce_max(
                    row11[:, 12:13], psQ2[0:1, 0:2], axis=AX.X),
                    wait=[(ct, tt)], inc=(cv, 1))
                c3_ = p.emit("vector", lambda E: E.reduce_max(
                    row11[:, 13:14], kn2[:, :], axis=AX.X), wait=[(cv, c2_)], inc=(cv, 1))
                c4_ = p.emit("scalar", lambda E: E.activation(
                    row11[:, 14:15], row11[:, 12:13], AF.Sqrt), wait=[(cv, c3_)], inc=(cs, 1))
                c5_ = p.emit("scalar", lambda E: E.activation(
                    row11[:, 15:16], row11[:, 13:14], AF.Sqrt), wait=[(cs, c4_)], inc=(cs, 1))
                c6_ = p.emit("vector", lambda E: E.tensor_tensor(
                    row11[:, 11:12], row11[:, 14:15], row11[:, 15:16], ALU.mult),
                    wait=[(cs, c5_)], inc=(cv, 1))
                c7_ = p.emit("vector", lambda E: E.tensor_scalar(
                    row11[:, 10:11], row11[:, 11:12], -1.0, None, ALU.mult),
                    wait=[(cv, c6_)], inc=(cv, 1))
            with contextlib.ExitStack() as esB0b:
                psB0 = esB0b.enter_context(nc.psum_tensor("psB0", [128, 1], F32))
                tmB2 = p.emit("tensor", lambda E: E.matmul(
                    psB0[:, 0:1], ones_row[:1, :], row11[:, 10:11],
                    start=True, stop=True), wait=[(cv, c7_)], inc=(ct, 1))
                b0done = p.emit("vector", lambda E: E.tensor_copy(
                    scal[:, 8:9], psB0[:, 0:1]), wait=[(ct, tmB2)], inc=(cv, 1))
            p.flush()
            prev_ev = {}
            for hh in range(NHEAD):
                hp, hr = hh // 2, (hh % 2) * 64
                for th in range(2):
                    tb0 = th * 1024
                    with contextlib.ExitStack() as esH1:
                        psYh_pad = esH1.enter_context(nc.psum_tensor(
                            "psYh_%d_%d" % (hh, th), [65, 1536], F32))
                        psYh = psYh_pad[:, 512:1536]
                        psS = [esH1.enter_context(nc.psum_tensor(
                            "psS_%d_%d_%d" % (hh, th, k2), [128, 1024], F32))
                            for k2 in range(2)]

                        sc_list = [sc for sc in range(32) if (sc // 2) * 128 < tb0 + 1024]
                        prev_exp_par = [prev_ev.get(("exp", 0)), prev_ev.get(("exp", 1))]
                        avl = {}
                        avl[-2] = prev_ev.get("avl")
                        avl[-1] = prev_ev.get("avl")
                        exp_t = {}
                        state = {}
                        tav = None

                        def emit_av(idx):
                            nonlocal tav
                            sc, segs, ebuf = state[idx]
                            ft = (idx == 0)
                            for gi, (s0, s1) in enumerate(segs):
                                bnk = (tb0 + s0) // 512
                                stp = (sc == min(31, 8 * bnk + 7))
                                w = []
                                if gi == 0:
                                    w = [(cs, exp_t[idx])]
                                    if ft and "y" in prev_ev:
                                        w.append(prev_ev["y"])
                                tav = p.emit("tensor", lambda E, sc=sc, s0=s0, s1=s1,
                                             ebuf=ebuf, stp=stp, ft=ft, psYh=psYh:
                                    E.matmul(
                                        psYh[:, s0:s1],
                                        v_aug[:, agrow(sc) // 128 * 65:
                                              agrow(sc) // 128 * 65 + 65],
                                        ebuf[:, s0:s1],
                                        start=ft, stop=stp), wait=w, inc=(ct, 1))
                            avl[idx] = (ct, tav)

                        for si, sc in enumerate(sc_list):
                            j0 = sc // 2
                            lo = max(j0 * 128, tb0) - tb0
                            pss = psS[si % 2]
                            ebuf = eT1[:, (si % 2) * TO + tb0:(si % 2) * TO + tb0 + 1024]
                            segs = []
                            s0 = lo
                            while s0 < 1024:
                                s1 = min((s0 // 512 + 1) * 512, 1024)
                                segs.append((s0, s1))
                                s0 = s1
                            masks = []
                            for m in range(2):
                                j = j0 + m
                                if j <= 15 and tb0 <= j * 128 < tb0 + 1024:
                                    masks.append((m, j * 128 - tb0))
                            tsc = None
                            for gi, (s0, s1) in enumerate(segs):
                                w = []
                                if gi == 0:
                                    w = [(ld, tdm)]
                                    if si == 0:
                                        w.append((cv, cv.n))
                                    pe_ = prev_exp_par[si % 2]
                                    if pe_ is not None:
                                        w.append((cs, pe_))
                                segmasks = [mm for mm in masks if s0 <= mm[1] < s1]
                                tsc = p.emit("tensor", lambda E, sc=sc, s0=s0, s1=s1,
                                             hr=hr, hp=hp, tb0=tb0, pss=pss,
                                             stp=(not segmasks):
                                    E.matmul(
                                        pss[:, s0:s1],
                                        k_fm1[hr:hr + 64, agrow(sc):agrow(sc) + 128],
                                        q_fm1[hr:hr + 64,
                                              hp * TO + tb0 + s0:hp * TO + tb0 + s1],
                                        start=True, stop=stp), wait=w, inc=(ct, 1))
                                for mi, (m, mlo) in enumerate(segmasks):
                                    tsc = p.emit("tensor", lambda E, sc=sc, m=m, mlo=mlo,
                                                 pss=pss, last=(mi == len(segmasks) - 1):
                                        E.matmul(
                                            pss[:, mlo:mlo + 128], id_sb[:],
                                            dm_sb[:, (sc * 2 + m) * 128:
                                                  (sc * 2 + m + 1) * 128],
                                            start=False, stop=last), inc=(ct, 1))
                            ew = [(ct, tsc), (cv, b0done)]
                            if avl.get(si - 2) is not None:
                                ew.append(avl[si - 2])
                            texp1 = p.emit("scalar", lambda E, lo=lo, ebuf=ebuf, pss=pss:
                                E.activation(
                                    ebuf[:, lo:1024], pss[:, lo:1024], AF.Exp,
                                    bias=scal[:, 8:9]), wait=ew, inc=(cs, 1))
                            prev_exp_par[si % 2] = texp1
                            exp_t[si] = texp1
                            state[si] = (sc, segs, ebuf)
                            if si >= 1:
                                emit_av(si - 1)
                        emit_av(len(sc_list) - 1)
                        prev_exp_par_final = prev_exp_par
                        prev_ev[("exp", 0)] = prev_exp_par[0]
                        prev_ev[("exp", 1)] = prev_exp_par[1]
                        prev_ev["avl"] = (ct, tav)
                        if hh == 0 and th == 0:
                            eS = p.emit("vector", lambda E: E.tensor_copy(
                                q2b[:1, 0:1024], psYh[64:65, :]),
                                wait=[(ct, tav)], inc=(cv, 1))
                            p.dbg("Sraw_%d" % th, lambda: q2b[:1, 0:1024], [(cv, eS)])
                        e2 = p.emit("vector", lambda E: E.reciprocal(
                            srowB[:, 0:1024], psYh[64:65, :]),
                            wait=[(ct, tav), (cv, cv.n)], inc=(cv, 1))
                        e3 = p.emit("vector", lambda E: E.tensor_copy(
                            srow2[:, 0:1024], srowB[:, 0:1024]), wait=[(cv, e2)], inc=(cv, 1))
                        bm = None
                        for s4 in range(2):
                            bm = p.emit("tensor", lambda E, s4=s4: E.matmul(
                                psS[0][0:64, s4 * 512:(s4 + 1) * 512], onesrb[:1, 0:64],
                                srow2[:, s4 * 512:(s4 + 1) * 512], start=True, stop=True),
                                wait=[(cv, e3)] + ([(cs, prev_exp_par[0])]
                                                   if (s4 == 0 and prev_exp_par[0]) else []),
                                inc=(ct, 1))
                        e4 = p.emit("vector", lambda E: E.tensor_copy(
                            invSb[:, 0:1024], psS[0][0:64, 0:1024]),
                            wait=[(ct, bm)], inc=(cv, 1))
                        if hr == 0:
                            e5 = p.emit("vector", lambda E, hp=hp, tb0=tb0: E.tensor_tensor(
                                y_fm[0:64, hp * TO + tb0:hp * TO + tb0 + 1024],
                                psYh[0:64, :], invSb[:, 0:1024], ALU.mult),
                                wait=[(cv, e4)], inc=(cv, 1))
                            prev_ev["y"] = (cv, e5)
                        else:
                            e5 = p.emit("vector", lambda E: E.tensor_tensor(
                                ystage[:, 0:1024], psYh[0:64, :],
                                invSb[:, 0:1024], ALU.mult),
                                wait=[(cv, e4)] + ([prev_ev["ys"]] if "ys" in prev_ev else []),
                                inc=(cv, 1))
                            e6 = p.emit("sync", lambda E, hp=hp, tb0=tb0: E.dma_start(
                                out=y_fm[64:128, hp * TO + tb0:hp * TO + tb0 + 1024],
                                in_=ystage[:, 0:1024]), wait=[(cv, e5)], inc=(ld, 16))
                            prev_ev["ys"] = (ld, e6)
                            prev_ev["y"] = (ld, e6)
            p.dbg("y_fm", lambda: y_fm[:, :], [(cv, cv.n), (ld, ld.n)])
            p.flush()
            att2_es.close()
            with contextlib.ExitStack() as esO:
                Wcp_bf = esO.enter_context(nc.sbuf_tensor("Wcp_bf", [128, DC * D], BF16))
                xbO = [esO.enter_context(nc.sbuf_tensor(f"xbO{i}", [128, D], F32)) for i in range(2)]
                sgO = esO.enter_context(nc.sbuf_tensor("sgO", [128, D], F32))
                x2O = esO.enter_context(nc.sbuf_tensor("x2O", [128, D], F32))
                psO = esO.enter_context(nc.psum_tensor("psO", [128, D], F32))
                tcp = p.emit("gpsimd", lambda E: E.dma_start(
                    out=Wcp_bf[:], in_=bass.AP(Wcp_h, 0, [[D, 128], [128 * D, DC], [1, D]])),
                    inc=(gd, 16))
                prevo = None
                for i in range(NTO):
                    buf = xbO[i % 2]
                    tx0 = p.emit("sync", lambda E, i=i, buf=buf: E.dma_start(
                        out=buf[:], in_=x1t[i]), wait=[prevo] if prevo else [], inc=(ld, 16))
                    tm = None
                    for hp2 in range(DC):
                        for nn in range(2):
                            tm = p.emit("tensor", lambda E, i=i, hp2=hp2, nn=nn: E.matmul(
                                psO[:, nn * 512:(nn + 1) * 512],
                                y_fm[:, hp2 * TO + i * 128:hp2 * TO + (i + 1) * 128],
                                Wcp_bf[:, hp2 * D + nn * 512:hp2 * D + (nn + 1) * 512],
                                start=(hp2 == 0), stop=(hp2 == DC - 1)),
                                wait=[(cv, cv.n), (gd, tcp)] if (hp2 == 0 and nn == 0) else [],
                                inc=(ct, 1) if (hp2 == DC - 1 and nn == 1) else None)
                    o1 = p.emit("vector", lambda E, i=i: E.scalar_tensor_tensor(
                        sgO[:], psO[:], 0.0, out_acc[:, i * D:(i + 1) * D],
                        ALU.bypass, ALU.mult), wait=[(ct, tm)], inc=(cv, 1))
                    o2 = p.emit("vector", lambda E, i=i, buf=buf: E.tensor_tensor(
                        x2O[:], sgO[:], buf[:], ALU.add),
                        wait=[(cv, o1), (ld, tx0)], inc=(cv, 1))
                    tso = p.emit("sync", lambda E, i=i: E.dma_start(
                        out=out_h.ap()[i * 128:(i + 1) * 128, :], in_=x2O[:]),
                        wait=[(cv, o2)], inc=(st, 16))
                    prevo = (st, tso)
                p.flush()

        p.emit("sync", lambda E: E.nop(), wait=[(st, st.n)])
        p.barrier()
        p.flush()
        kv_es.close()
        L1_es.close()
        return nc



def kernel(**inputs):
    nc = build_program()
    in_maps = [host_inputs(inputs, c) for c in range(8)]
    res = run_bass_kernel_spmd(nc, in_maps, core_ids=list(range(8)))
    outf = np.zeros((4, TB, D), np.float32)
    for c in range(8):
        b, hc = c // 2, c % 2
        outf[b, own_order(hc)] = res.results[c]["out"]
    return outf


# version 06
# speedup vs baseline: 1.1116x; 1.0134x over previous
import contextlib
import numpy as np
import ml_dtypes
import concourse.bass as bass
import concourse.mybir as mybir
from concourse.bass_utils import run_bass_kernel_spmd

F32 = mybir.dt.float32
BF16 = mybir.dt.bfloat16
AF = mybir.ActivationFunctionType
ALU = mybir.AluOpType
AX = mybir.AxisListType

MAGIC = 12582912.0
EPS = 1e-5
NEGBIG = -30000.0
D, DC = 1024, 8
TB, NTB = 4096, 32
TO, NTO = 2048, 16
I0, NI0 = 2816, 22
LAT, NLAT = 512, 4
NHEAD, HD = 16, 64
FF = 2048
MOE_OVL = True

ENGS = ("sync", "gpsimd", "vector", "scalar", "tensor")


class Sem:
    __slots__ = ("h", "n")
    def __init__(self, h):
        self.h, self.n = h, 0
    def bump(self, k):
        self.n += k
        return self.n


class P:
    def __init__(self, nc, es):
        self.nc, self.es = nc, es
        self.ops = {e: [] for e in ENGS}
        self._sems = {}
        self.dbg_en = set()
        self.dbg_shapes = {}

    def sem(self, name):
        s = self._sems.get(name)
        if s is None:
            s = Sem(self.es.enter_context(self.nc.semaphore(name)))
            self._sems[name] = s
        return s

    def emit(self, eng, fn, wait=(), inc=None):
        wait = tuple(wait)
        tgt = inc[0].bump(inc[1]) if inc is not None else None
        def op(E, fn=fn, wait=wait, inc=inc):
            for s, v in wait:
                E.wait_ge(s.h, v)
            r = fn(E)
            if inc is not None:
                r.then_inc(inc[0].h, inc[1])
        self.ops[eng].append(op)
        return tgt

    def wait(self, eng, sem, val):
        self.ops[eng].append(lambda E, s=sem, v=val: E.wait_ge(s.h, v))

    def barrier(self, engs=ENGS):
        pb = self.sem("pb")
        tgt = pb.n + len(engs)
        pb.bump(len(engs))
        for eng in engs:
            def op(E, tgt=tgt, pb=pb):
                E.nop().then_inc(pb.h, 1)
                E.wait_ge(pb.h, tgt)
            self.ops[eng].append(op)

    def flush(self):
        if not any(self.ops.values()):
            return
        with self.nc.Block() as blk:
            for eng in ENGS:
                ops = self.ops[eng]
                if ops:
                    def runner(E, ops=tuple(ops)):
                        for op in ops:
                            op(E)
                    getattr(blk, eng)(runner)
        self.ops = {e: [] for e in ENGS}

    def dbg(self, name, src_ap_fn, wait):
        if name not in self.dbg_en:
            return
        st = self.sem("st")
        def mk(E, f=src_ap_fn, name=name):
            src = f()
            t = self.nc.dram_tensor("dbg_" + name,
                                    [int(x) for x in src.shape], src.dtype,
                                    kind="ExternalOutput").ap()
            return E.dma_start(out=t, in_=src)
        self.emit("sync", mk, wait=wait, inc=(st, 16))


def own_order(hc):
    idx = []
    for j in range(16):
        g = 2 * j + hc
        idx.extend(range(g * 128, (g + 1) * 128))
    return np.array(idx, np.int64)


def make_diagm(hc):
    dm = np.zeros((32, 2, 128, 128), np.float32)
    tril = np.tril(np.full((128, 128), NEGBIG, np.float32), -1)
    full = np.full((128, 128), NEGBIG, np.float32)
    for sc in range(32):
        j0 = sc // 2
        for m in range(2):
            j = j0 + m
            if j > 15:
                continue
            g = 2 * j + hc
            if g == sc:
                dm[sc, m] = tril
            elif g < sc:
                dm[sc, m] = full
    return dm.astype(ml_dtypes.bfloat16)


def host_inputs(inputs, core):
    b, hc = core // 2, core % 2
    f32 = lambda a: np.ascontiguousarray(np.asarray(a), dtype=np.float32)
    x = f32(inputs["L0_attn_input_sequence"][b])
    xro = np.ascontiguousarray(np.concatenate([x[own_order(hc)], x[own_order(1 - hc)]], 0))
    pv = np.stack([
        f32(inputs["L0_attn_bk"]), f32(inputs["L0_attn_gk"]), f32(inputs["L0_attn_betak"]),
        f32(inputs["L0_attn_bq"]), f32(inputs["L0_attn_gq"]), f32(inputs["L0_attn_betaq"]),
        f32(inputs["L0_attn_bv"]), f32(inputs["L0_attn_gv"]), f32(inputs["L0_attn_betav"]),
        f32(inputs["L1_ffn_ln_gamma"]), f32(inputs["L1_ffn_ln_beta"]),
        np.zeros(D, np.float32)]).astype(np.float32)
    return {
        "xro": xro,
        "Wk": f32(inputs["L0_attn_Wk"]), "Wq": f32(inputs["L0_attn_Wq"]),
        "Wv": f32(inputs["L0_attn_Wv"]), "pv": pv,
        "clips": np.array([[float(np.asarray(inputs["L0_ffn_clip_x"]).ravel()[0]),
                            float(np.asarray(inputs["L0_ffn_clip_down"]).ravel()[0])]],
                          np.float32),
        "wg": f32(inputs["L0_ffn_w_gate"]), "wu": f32(inputs["L0_ffn_w_up"]),
        "wd": f32(inputs["L0_ffn_w_down"]),
        "Wcd": f32(inputs["L1_attn_Wc_down"]), "Wqu": f32(inputs["L1_attn_Wq_up"]),
        "Wkv": f32(inputs["L1_attn_Wkv_up"]), "Wcp": f32(inputs["L1_attn_Wc_proj"]),
        "rwT": np.ascontiguousarray(f32(inputs["L1_ffn_router_w"]).T),
        "guT": np.ascontiguousarray(f32(inputs["L1_ffn_gate_up_w"]).transpose(0, 2, 1)),
        "dwT": np.ascontiguousarray(f32(inputs["L1_ffn_down_w"]).transpose(0, 2, 1)),
        "idb": np.eye(128, dtype=ml_dtypes.bfloat16),
        "idf": np.eye(128, dtype=np.float32),
        "ind2": np.concatenate([np.tile([[1, 0]], (64, 1)), np.tile([[0, 1]], (64, 1))],
                               0).astype(ml_dtypes.bfloat16),
        "diagm": make_diagm(hc),
    }



def build_program(dbg=(), upto="DONE"):
    nc = bass.Bass("TRN2", target_bir_lowering=False)
    ein = lambda n, s, d: nc.dram_tensor(n, list(s), d, kind="ExternalInput")
    idr = lambda n, s, d: nc.dram_tensor(n, list(s), d)

    xro_h = ein("xro", [TB, D], F32)
    Wk_h = ein("Wk", [D, D], F32)
    Wq_h = ein("Wq", [D, D], F32)
    Wv_h = ein("Wv", [D, D], F32)
    pv_h = ein("pv", [12, D], F32)
    clips_h = ein("clips", [1, 2], F32)
    wg_h = ein("wg", [I0, D], F32)
    wu_h = ein("wu", [I0, D], F32)
    wd_h = ein("wd", [D, I0], F32)
    Wcd_h = ein("Wcd", [D, LAT], F32)
    Wqu_h = ein("Wqu", [LAT, D], F32)
    Wkv_h = ein("Wkv", [LAT, 2 * HD], F32)
    Wcp_h = ein("Wcp", [D, D], F32)
    rwT_h = ein("rwT", [D, 8], F32)
    guT_h = ein("guT", [8, D, 2 * FF], F32)
    dwT_h = ein("dwT", [8, FF, D], F32)
    idb_h = ein("idb", [128, 128], BF16)
    idf_h = ein("idf", [128, 128], F32)
    ind2_h = ein("ind2", [128, 2], BF16)
    diagm_h = ein("diagm", [32, 2, 128, 128], BF16)
    out_h = nc.dram_tensor("out", [TO, D], F32, kind="ExternalOutput")

    wqT_d = idr("wqT_d", [D, I0], BF16)
    wuT_d = idr("wuT_d", [D, I0], BF16)
    wdT_d = idr("wdT_d", [I0, D], BF16)
    x1_d = idr("x1_d", [TO, D], F32)
    g_d = idr("g_d", [TO, D], BF16)
    xf_d = idr("xf_d", [DC, 128, TO], BF16)
    qf_d = idr("qf_d", [DC, 128, TO], BF16)
    cc1i = idr("cc1i", [1, 1], F32)
    cc1o = idr("cc1o", [1, 1], F32)
    cc2i = idr("cc2i", [1, 1], F32)
    cc2o = idr("cc2o", [1, 1], F32)
    kvi = idr("kvi", [TO, 2 * HD], BF16)
    kvo = idr("kvo", [TB, 2 * HD], BF16)

    xro = xro_h.ap().rearrange("(n p) m -> n p m", p=128)
    x1t = x1_d.ap().rearrange("(n p) m -> n p m", p=128)

    es = contextlib.ExitStack()
    with es:
        p = P(nc, es)
        p.dbg_en = set(dbg)
        SB = lambda n, s, d: es.enter_context(nc.sbuf_tensor(n, list(s), d))
        ld = p.sem("ld");  gd = p.sem("gd");  st = p.sem("st")
        cv = p.sem("cv");  cs = p.sem("cs");  ct = p.sem("ct");  cc = p.sem("cc")

        id_sb = SB("id_sb", [128, 128], BF16)
        id_f = SB("id_f", [128, 128], F32)
        ones_row = SB("ones_row", [1, 128], F32)
        ones_col = SB("ones_col", [128, 1], BF16)
        pv_pp = SB("pv_pp", [128, 12 * DC], F32)
        onesrb = SB("onesrb", [1, 128], BF16)
        stat = SB("stat", [128, 4 * NTB], F32)
        st1 = SB("st1", [128, 4 * NTO], F32)
        st2 = SB("st2", [128, 4 * NTO], F32)
        scal = SB("scal", [128, 16], F32)
        row11 = SB("row11", [1, 16], F32)
        clip_sb = SB("clip_sb", [1, 2], F32)
        tiny = SB("tiny", [128, 80], F32)

        def bcast128(dst_ap_fn, src_row_ap, ncols, ps, wait, tag):
            done = []
            for n0 in range(0, ncols, 512):
                n1 = min(n0 + 512, ncols)
                tm = p.emit("tensor", lambda E, n0=n0, n1=n1: E.matmul(
                    ps[:, n0:n1], ones_row[:1, :], src_row_ap[:, n0:n1],
                    start=True, stop=True), wait=wait if n0 == 0 else [], inc=(ct, 1))
                done.append((n0, n1, tm))
            outs = []
            for n0, n1, tm in done:
                tv = p.emit("vector", lambda E, n0=n0, n1=n1: E.tensor_copy(
                    dst_ap_fn(n0, n1), ps[:, n0:n1]), wait=[(ct, tm)], inc=(cv, 1))
                outs.append(tv)
            return outs[-1]

        l0c_es = contextlib.ExitStack()
        brow = l0c_es.enter_context(nc.sbuf_tensor("brow", [1, 3 * D], BF16))
        gvbc = l0c_es.enter_context(nc.sbuf_tensor("gvbc", [128, D], BF16))
        bvbc = l0c_es.enter_context(nc.sbuf_tensor("bvbc", [128, D], BF16))

        with contextlib.ExitStack() as esA:
            pvrow = esA.enter_context(nc.sbuf_tensor("pvrow", [1, D], F32))
            psA = esA.enter_context(nc.psum_tensor("psA", [128, 1024], F32))
            p.emit("sync", lambda E: E.dma_start(out=id_sb[:], in_=idb_h.ap()), inc=(ld, 16))
            p.emit("sync", lambda E: E.dma_start(out=id_f[:], in_=idf_h.ap()), inc=(ld, 16))
            clip_tgt = p.emit("sync", lambda E: E.dma_start(out=clip_sb[:], in_=clips_h.ap()), inc=(ld, 16))
            p.emit("sync", lambda E: E.dma_start(
                out=pv_pp[:], in_=bass.AP(pv_h, 0, [[1, 128], [1024, 12], [128, 8]]),
                allow_slow_non_contiguous=True), inc=(ld, 16))
            p.emit("vector", lambda E: E.memset(ones_row[:], 1.0), inc=(cv, 1))
            p.emit("vector", lambda E: E.memset(ones_col[:], 1.0), inc=(cv, 1))
            p.emit("vector", lambda E: E.memset(onesrb[:], 1.0), inc=(cv, 1))
            for i, r in enumerate((0, 3, 6)):
                t0 = p.emit("sync", lambda E, r=r: E.dma_start(
                    out=pvrow[:], in_=pv_h.ap()[r:r + 1, :]), wait=[(cv, cv.n)], inc=(ld, 16))
                p.emit("vector", lambda E, i=i: E.tensor_copy(
                    brow[:, i * D:(i + 1) * D], pvrow[:]), wait=[(ld, t0)], inc=(cv, 1))
            for rr, dst in ((7, gvbc), (8, bvbc)):
                t0 = p.emit("sync", lambda E, rr=rr: E.dma_start(
                    out=pvrow[:], in_=pv_h.ap()[rr:rr + 1, :]), wait=[(cv, cv.n)], inc=(ld, 16))
                bcast128(lambda n0, n1, dst=dst: dst[:, n0:n1], pvrow,
                         D, psA, [(ld, t0), (cv, cv.n)], "bc")
            p.flush()

        with contextlib.ExitStack() as esB:
            xbuf = [esB.enter_context(nc.sbuf_tensor(f"xbuf{i}", [128, D], F32)) for i in range(2)]
            sq_scr = esB.enter_context(nc.sbuf_tensor("sq_scr", [128, D], F32))
            xnf = esB.enter_context(nc.sbuf_tensor("xnf", [128, D], F32))
            psB = esB.enter_context(nc.psum_tensor("psB", [1, 128], F32))

            def ln_stats(src_tiles, n_tiles, sbuf_stat):
                tgts = []
                for i in range(n_tiles):
                    buf = xbuf[i % 2]
                    w = tgts[i - 2] if i >= 2 else []
                    t0 = p.emit("sync", lambda E, i=i, buf=buf: E.dma_start(
                        out=buf[:], in_=src_tiles[i]), wait=w, inc=(ld, 16))
                    tv = p.emit("vector", lambda E, i=i, buf=buf: E.reduce_sum(
                        sbuf_stat[:, i:i + 1], buf[:], axis=AX.X),
                        wait=[(ld, t0)], inc=(cv, 1))
                    ts_ = p.emit("scalar", lambda E, i=i, buf=buf: E.activation(
                        sq_scr[:], buf[:], AF.Square,
                        accum_out=sbuf_stat[:, n_tiles + i:n_tiles + i + 1]),
                        wait=[(ld, t0)], inc=(cs, 1))
                    tgts.append([(cv, tv), (cs, ts_)])
                n = n_tiles
                p.emit("vector", lambda E, n=n, sb_=sbuf_stat: E.tensor_scalar(
                    sb_[:, 2 * n:3 * n], sb_[:, 0:n], -1.0 / D, None, ALU.mult),
                    wait=[(cs, cs.n), (cv, cv.n)], inc=(cv, 1))
                tmu2 = p.emit("scalar", lambda E, n=n, sb_=sbuf_stat: E.activation(
                    tiny[:, 0:n], sb_[:, 2 * n:3 * n], AF.Square),
                    wait=[(cv, cv.n)], inc=(cs, 1))
                p.emit("vector", lambda E, n=n, sb_=sbuf_stat: E.tensor_scalar(
                    tiny[:, n:2 * n], sb_[:, n:2 * n], 1.0 / D, EPS, ALU.mult, ALU.add),
                    inc=(cv, 1))
                p.emit("vector", lambda E, n=n, sb_=sbuf_stat: E.tensor_tensor(
                    sb_[:, 3 * n:4 * n], tiny[:, n:2 * n], tiny[:, 0:n], ALU.subtract),
                    wait=[(cs, tmu2)], inc=(cv, 1))
                p.emit("vector", lambda E, n=n, sb_=sbuf_stat: E.reciprocal(
                    tiny[:, 0:n], sb_[:, 3 * n:4 * n]), inc=(cv, 1))
                return p.emit("scalar", lambda E, n=n, sb_=sbuf_stat: E.activation(
                    sb_[:, 3 * n:4 * n], tiny[:, 0:n], AF.Sqrt),
                    wait=[(cv, cv.n)], inc=(cs, 1))

            stats_done = ln_stats(xro, NTB, stat)
            p.dbg("stat", lambda: stat[:, :], [(cs, stats_done)])
            lastv = None
            sweep_cv0, sweep_cs0 = cv.n, cs.n
            for i in range(NTO):
                buf = xbuf[i % 2]
                w = [(cv, lastv)] if lastv else [(cv, sweep_cv0), (cs, sweep_cs0)]
                t0 = p.emit("sync", lambda E, i=i, buf=buf: E.dma_start(
                    out=buf[:], in_=xro[i]), wait=w, inc=(ld, 16))
                p.emit("vector", lambda E, i=i, buf=buf: E.tensor_scalar(
                    xnf[:], buf[:], stat[:, 2 * NTB + i:2 * NTB + i + 1],
                    stat[:, 3 * NTB + i:3 * NTB + i + 1], ALU.add, ALU.mult),
                    wait=[(ld, t0), (cs, stats_done)], inc=(cv, 1))
                lastv = p.emit("vector", lambda E, i=i: E.tensor_reduce(
                    tiny[:, i:i + 1], xnf[:], AX.X, ALU.max, apply_absolute_value=True),
                    inc=(cv, 1))
                if i == 0:
                    p.dbg("xnf0", lambda: xnf[:, :], [(cv, cv.n)])
            p.emit("vector", lambda E: E.tensor_reduce(
                scal[:, 0:1], tiny[:, 0:NTO], AX.X, ALU.max),
                wait=[(cv, lastv)], inc=(cv, 1))
            tt = p.emit("tensor", lambda E: E.transpose(
                psB[:, :], scal[:, 0:1], id_f[:]), wait=[(cv, cv.n)], inc=(ct, 1))
            tv = p.emit("vector", lambda E: E.reduce_max(
                row11[:, 0:1], psB[:, :], axis=AX.X), wait=[(ct, tt)], inc=(cv, 1))
            t0 = p.emit("gpsimd", lambda E: E.dma_start(out=cc1i.ap()[:, :], in_=row11[:, 0:1]),
                        wait=[(cv, tv)], inc=(gd, 16))
            p.wait("gpsimd", gd, t0)
            tc = p.emit("gpsimd", lambda E: E.collective_compute(
                "AllReduce", ALU.max, replica_groups=[list(range(8))],
                ins=[cc1i.ap().opt()], outs=[cc1o.ap().opt()]), inc=(cc, 1))
            p.wait("gpsimd", cc, tc)
            c1_done = p.emit("gpsimd", lambda E: E.dma_start(
                out=row11[:, 1:2], in_=cc1o.ap()[:, :]), inc=(gd, 16))
            p.dbg("gmax1", lambda: row11[:, :], [(gd, c1_done)])
            p.flush()

        with contextlib.ExitStack() as esC:
            wbuf = [esC.enter_context(nc.sbuf_tensor(f"wbuf{i}", [128, I0], F32)) for i in range(2)]
            wq_bf = esC.enter_context(nc.sbuf_tensor("wq_bf", [128, I0], BF16))
            wscr = esC.enter_context(nc.sbuf_tensor("wscr", [128, I0], F32))
            wstage = esC.enter_context(nc.sbuf_tensor("wstage", [128, I0], BF16))
            psT = esC.enter_context(nc.psum_tensor("psT", [128, 2048], BF16))

            def quant_tr(w_h, rows, cols, outT_d):
                nt, ncc = rows // 128, cols // 128
                tq_done, st_done = [], []
                for i in range(nt):
                    buf = wbuf[i % 2]
                    w = [(cv, tq_done[i - 2])] if i >= 2 else []
                    t0 = p.emit("sync", lambda E, i=i, buf=buf, w_h=w_h, cols=cols: E.dma_start(
                        out=buf[:, 0:cols], in_=w_h.ap()[i * 128:(i + 1) * 128, :]),
                        wait=w, inc=(ld, 16))
                    r0 = p.emit("vector", lambda E, buf=buf, cols=cols: E.tensor_reduce(
                        tiny[:, 64:65], buf[:, 0:cols], AX.X, ALU.max,
                        apply_absolute_value=True), wait=[(ld, t0)], inc=(cv, 1))
                    r1 = p.emit("vector", lambda E: E.tensor_scalar(
                        tiny[:, 65:66], tiny[:, 64:65], 1.0 / 127.0, None, ALU.mult),
                        wait=[(cv, r0)], inc=(cv, 1))
                    r2 = p.emit("vector", lambda E: E.reciprocal(
                        tiny[:, 66:67], tiny[:, 64:65]), wait=[(cv, r1)], inc=(cv, 1))
                    r3 = p.emit("vector", lambda E: E.tensor_scalar(
                        tiny[:, 67:68], tiny[:, 66:67], 127.0, None, ALU.mult),
                        wait=[(cv, r2)], inc=(cv, 1))
                    tbufdone = p.emit("vector", lambda E, buf=buf, cols=cols: E.tensor_scalar(
                        wscr[:, 0:cols], buf[:, 0:cols], tiny[:, 67:68], MAGIC,
                        ALU.mult, ALU.add), wait=[(cv, r3)], inc=(cv, 1))
                    tq = p.emit("vector", lambda E, cols=cols: E.tensor_scalar(
                        wq_bf[:, 0:cols], wscr[:, 0:cols], MAGIC, tiny[:, 65:66],
                        ALU.subtract, ALU.mult), wait=[(cv, tbufdone)], inc=(cv, 1))
                    tq_done.append(tbufdone)
                    evs = []
                    for c in range(ncc):
                        off = (c % 2) * 1024
                        wt = [(cv, tq)] if c < 2 else [(cv, evs[c - 2])]
                        if c == 0 and st_done:
                            wt = wt + [(st, st_done[-1])]
                        tt = p.emit("tensor", lambda E, c=c, off=off: E.transpose(
                            psT[:, off:off + 128], wq_bf[:, c * 128:(c + 1) * 128],
                            id_sb[:]), wait=wt, inc=(ct, 1))
                        evs.append(p.emit("vector", lambda E, c=c, off=off: E.tensor_copy(
                            wstage[:, c * 128:(c + 1) * 128], psT[:, off:off + 128]),
                            wait=[(ct, tt)] + ([(st, st_done[-1])] if (c == 0 and st_done) else []),
                            inc=(cv, 1)))
                    tm = p.emit("sync", lambda E, i=i, ncc=ncc, outT_d=outT_d, rows=rows,
                                cols=cols: E.dma_start(
                        out=bass.AP(outT_d, i * 128,
                                    [[rows, 128], [128 * rows, ncc], [1, 128]]),
                        in_=wstage[:, 0:ncc * 128]), wait=[(cv, evs[-1])], inc=(st, 16))
                    st_done.append(tm)
            quant_tr(wg_h, I0, D, wqT_d)
            quant_tr(wu_h, I0, D, wuT_d)
            quant_tr(wd_h, D, I0, wdT_d)
            wq_stores = st.n
            p.flush()

        def scales(col_clip, gmax_col, out_inv, out_s, ps):
            t0 = p.emit("scalar", lambda E: E.activation(
                row11[:, 2:3], clip_sb[:, col_clip:col_clip + 1], AF.Sigmoid),
                wait=[(cv, cv.n), (ld, clip_tgt)], inc=(cs, 1))
            t1 = p.emit("vector", lambda E: E.tensor_tensor(
                row11[:, 3:4], row11[:, 2:3], row11[:, gmax_col:gmax_col + 1], ALU.mult),
                wait=[(cs, t0)], inc=(cv, 1))
            t2 = p.emit("vector", lambda E: E.tensor_scalar(
                row11[:, 9:10], row11[:, 3:4], 1.0 / 127.0, None, ALU.mult),
                wait=[(cv, t1)], inc=(cv, 1))
            p.emit("vector", lambda E: E.reciprocal(row11[:, 10:11], row11[:, 9:10]),
                   wait=[(cv, t2)], inc=(cv, 1))
            tv = cv.n
            tm = p.emit("tensor", lambda E, ps=ps: E.matmul(
                ps[:, 0:2], ones_row[:1, :], row11[:, 9:11], start=True, stop=True),
                wait=[(cv, tv)], inc=(ct, 1))
            p.emit("vector", lambda E, ps=ps: E.tensor_copy(
                scal[:, out_s:out_s + 1], ps[:, 0:1]), wait=[(ct, tm)], inc=(cv, 1))
            return p.emit("vector", lambda E, ps=ps: E.tensor_copy(
                scal[:, out_inv:out_inv + 1], ps[:, 1:2]), inc=(cv, 1))

        xq_es = contextlib.ExitStack()
        xq_fm = xq_es.enter_context(nc.sbuf_tensor("xq_fm", [128, DC * TO], BF16))
        with contextlib.ExitStack() as esE:
            xbuf = [esE.enter_context(nc.sbuf_tensor(f"exbuf{i}", [128, D], F32)) for i in range(2)]
            q_scr = esE.enter_context(nc.sbuf_tensor("q_scr", [128, D], F32))
            q_scr2 = esE.enter_context(nc.sbuf_tensor("q_scr2", [128, D], F32))
            xq_bf = esE.enter_context(nc.sbuf_tensor("xq_bf", [128, D], BF16))
            psS = esE.enter_context(nc.psum_tensor("psS", [128, 16], F32))
            psT = esE.enter_context(nc.psum_tensor("psTE", [128, 2048], BF16))
            p.wait("scalar", gd, c1_done)
            sc1_done = scales(0, 1, 2, 3, psS)
            lastq = None
            for i in range(NTO):
                buf = xbuf[i % 2]
                w = [(cv, lastq)] if lastq else []
                t0 = p.emit("sync", lambda E, i=i, buf=buf: E.dma_start(
                    out=buf[:], in_=xro[i]), wait=w, inc=(ld, 16))
                p.emit("vector", lambda E, i=i, buf=buf: E.tensor_scalar(
                    q_scr[:], buf[:], stat[:, 2 * NTB + i:2 * NTB + i + 1],
                    stat[:, 3 * NTB + i:3 * NTB + i + 1], ALU.add, ALU.mult),
                    wait=[(ld, t0), (cv, max(sc1_done, cv.n))], inc=(cv, 1))
                ta = p.emit("scalar", lambda E: E.activation(
                    q_scr2[:], q_scr[:], AF.Copy, bias=MAGIC, scale=scal[:, 2:3]),
                    wait=[(cv, cv.n)], inc=(cs, 1))
                p.emit("vector", lambda E: E.tensor_scalar(
                    q_scr[:], q_scr2[:], MAGIC, 127.0, ALU.subtract, ALU.min),
                    wait=[(cs, ta)], inc=(cv, 1))
                tq = p.emit("vector", lambda E: E.tensor_scalar(
                    xq_bf[:], q_scr[:], -128.0, scal[:, 3:4], ALU.max, ALU.mult),
                    inc=(cv, 1))
                if i == 0:
                    p.dbg("scal", lambda: scal[:, :], [(cv, tq)])
                    p.dbg("xq_bf0", lambda: xq_bf[:, :], [(cv, tq)])
                evs = []
                for c in range(DC):
                    off = (c % 2) * 1024
                    wt = [(cv, tq)] if c < 2 else [(cv, evs[c - 2])]
                    tt = p.emit("tensor", lambda E, c=c, off=off: E.transpose(
                        psT[:, off:off + 128], xq_bf[:, c * 128:(c + 1) * 128], id_sb[:]),
                        wait=wt, inc=(ct, 1))
                    evs.append(p.emit("vector", lambda E, c=c, i=i, off=off: E.tensor_copy(
                        xq_fm[:, c * TO + i * 128:c * TO + (i + 1) * 128],
                        psT[:, off:off + 128]), wait=[(ct, tt)], inc=(cv, 1)))
                lastq = evs[-1]
            p.dbg("xq_fm", lambda: xq_fm[:, :], [(cv, cv.n)])
            p.flush()

        if upto == "E":
            p.emit("sync", lambda E: E.nop(), wait=[(st, st.n)])
            p.flush()
            xq_es.close()
            return nc

        h_es = contextlib.ExitStack()
        h_sb = h_es.enter_context(nc.sbuf_tensor("h_sb", [128, NI0 * TO], BF16))
        with contextlib.ExitStack() as esF:
            wT_sb = [esF.enter_context(nc.sbuf_tensor(f"wT{i}", [128, D], BF16))
                     for i in range(4)]
            sl_bf = esF.enter_context(nc.sbuf_tensor("sl_bf", [128, TO], BF16))
            psG = esF.enter_context(nc.psum_tensor("psG", [128, TO], F32))
            psU = esF.enter_context(nc.psum_tensor("psU", [128, TO], F32))
            p.wait("sync", st, wq_stores)
            hdone = []
            for i in range(NI0):
                bg, bu = wT_sb[2 * (i % 2)], wT_sb[2 * (i % 2) + 1]
                w = [hdone[i - 2]] if i >= 2 else []
                tg = p.emit("sync", lambda E, i=i, bg=bg: E.dma_start(
                    out=bg[:, 0:D],
                    in_=bass.AP(wqT_d, i * 128, [[I0, 128], [128 * I0, DC], [1, 128]])),
                    wait=w, inc=(ld, 16))
                tu = p.emit("sync", lambda E, i=i, bu=bu: E.dma_start(
                    out=bu[:, 0:D],
                    in_=bass.AP(wuT_d, i * 128, [[I0, 128], [128 * I0, DC], [1, 128]])),
                    inc=(ld, 16))
                sil_prev = cs.n
                hm_prev = cv.n
                tgm = None
                for d in range(DC):
                    for s4 in range(4):
                        wt = [(ld, tg), (cs, sil_prev)] if (d == 0 and s4 == 0) else []
                        tgm = p.emit("tensor", lambda E, d=d, s4=s4, bg=bg: E.matmul(
                            psG[:, s4 * 512:(s4 + 1) * 512],
                            bg[:, d * 128:(d + 1) * 128],
                            xq_fm[:, d * TO + s4 * 512:d * TO + (s4 + 1) * 512],
                            start=(d == 0), stop=(d == DC - 1)), wait=wt,
                            inc=(ct, 1) if (d == DC - 1 and s4 == 3) else None)
                tsl = p.emit("scalar", lambda E: E.activation(
                    sl_bf[:], psG[:], AF.Silu),
                    wait=[(ct, tgm), (cv, hm_prev)], inc=(cs, 1))
                tum = None
                for d in range(DC):
                    for s4 in range(4):
                        wt = [(ld, tu), (cv, hm_prev)] if (d == 0 and s4 == 0) else []
                        tum = p.emit("tensor", lambda E, d=d, s4=s4, bu=bu: E.matmul(
                            psU[:, s4 * 512:(s4 + 1) * 512],
                            bu[:, d * 128:(d + 1) * 128],
                            xq_fm[:, d * TO + s4 * 512:d * TO + (s4 + 1) * 512],
                            start=(d == 0), stop=(d == DC - 1)), wait=wt,
                            inc=(ct, 1) if (d == DC - 1 and s4 == 3) else None)
                th = p.emit("vector", lambda E, i=i: E.scalar_tensor_tensor(
                    h_sb[:, i * TO:(i + 1) * TO], psU[:], 0.0, sl_bf[:],
                    ALU.bypass, ALU.mult), wait=[(ct, tum), (cs, tsl)], inc=(cv, 1))
                th2 = p.emit("vector", lambda E, i=i: E.tensor_reduce(
                    tiny[:, i:i + 1], h_sb[:, i * TO:(i + 1) * TO], AX.X, ALU.max,
                    apply_absolute_value=True), inc=(cv, 1))
                hdone.append((cv, th2))
            p.emit("vector", lambda E: E.tensor_reduce(
                scal[:, 4:5], tiny[:, 0:NI0], AX.X, ALU.max),
                wait=[hdone[-1]], inc=(cv, 1))
            p.flush()
        with contextlib.ExitStack() as esF2:
            psB2 = esF2.enter_context(nc.psum_tensor("psB2", [1, 128], F32))
            tt = p.emit("tensor", lambda E: E.transpose(
                psB2[:, :], scal[:, 4:5], id_f[:]), wait=[(cv, cv.n)], inc=(ct, 1))
            tv = p.emit("vector", lambda E: E.reduce_max(
                row11[:, 5:6], psB2[:, :], axis=AX.X), wait=[(ct, tt)], inc=(cv, 1))
            t0 = p.emit("gpsimd", lambda E: E.dma_start(
                out=cc2i.ap()[:, :], in_=row11[:, 5:6]), wait=[(cv, tv)], inc=(gd, 16))
            p.wait("gpsimd", gd, t0)
            tc = p.emit("gpsimd", lambda E: E.collective_compute(
                "AllReduce", ALU.max, replica_groups=[list(range(8))],
                ins=[cc2i.ap().opt()], outs=[cc2o.ap().opt()]), inc=(cc, 1))
            p.wait("gpsimd", cc, tc)
            c2_done = p.emit("gpsimd", lambda E: E.dma_start(
                out=row11[:, 6:7], in_=cc2o.ap()[:, :]), inc=(gd, 16))
            p.dbg("gmax2", lambda: row11[:, :], [(gd, c2_done)])
            p.dbg("h_sb", lambda: h_sb[:, :], [(cv, cv.n)])
            p.flush()

        with contextlib.ExitStack() as esH0:
            psS2 = esH0.enter_context(nc.psum_tensor("psS2", [128, 16], F32))
            p.wait("scalar", gd, c2_done)
            sc2_done = scales(1, 6, 6, 7, psS2)
            p.flush()
        with contextlib.ExitStack() as esH:
            wdT_sb = [esH.enter_context(nc.sbuf_tensor(f"wdT{i}", [128, D], BF16))
                      for i in range(2)]
            hq_f = [esH.enter_context(nc.sbuf_tensor(f"hq_f{i}", [128, 512], F32)) for i in range(2)]
            hq_g = [esH.enter_context(nc.sbuf_tensor(f"hq_g{i}", [128, 512], F32)) for i in range(2)]
            hq_bf = [esH.enter_context(nc.sbuf_tensor(f"hq_bf{i}", [128, 512], BF16)) for i in range(2)]
            psD = esH.enter_context(nc.psum_tensor("psD", [128, 4 * D], F32))
            gstage = esH.enter_context(nc.sbuf_tensor("gstage", [128, 4 * D], BF16))
            gst = []
            carry_mm = None
            carry_sil = None
            hf_guard = {0: None, 1: None}
            for tg4 in range(4):
                mm_done = []
                for i in range(NI0):
                    bw = wdT_sb[i % 2]
                    hf, hg, hb = hq_f[i % 2], hq_g[i % 2], hq_bf[i % 2]
                    w = [mm_done[i - 2]] if i >= 2 else ([carry_mm] if carry_mm else [])
                    t0 = p.emit("sync", lambda E, i=i, bw=bw: E.dma_start(
                        out=bw[:], in_=wdT_d.ap()[i * 128:(i + 1) * 128, :]),
                        wait=w, inc=(ld, 16))
                    hfw = [hf_guard[i % 2]] if hf_guard[i % 2] else [(cv, sc2_done)]
                    ta = p.emit("scalar", lambda E, i=i, tg4=tg4, hf=hf: E.activation(
                        hf[:], h_sb[:, i * TO + tg4 * 512:i * TO + tg4 * 512 + 512],
                        AF.Copy, bias=MAGIC, scale=scal[:, 6:7]),
                        wait=hfw, inc=(cs, 1))
                    hf_guard[i % 2] = (cv, p.emit("vector",
                        lambda E, hf=hf, hg=hg: E.tensor_scalar(
                            hg[:], hf[:], MAGIC, 127.0, ALU.subtract, ALU.min),
                        wait=[(cs, ta)], inc=(cv, 1)))
                    tq = p.emit("vector", lambda E, hg=hg, hb=hb: E.tensor_scalar(
                        hb[:], hg[:], -128.0, scal[:, 7:8], ALU.max, ALU.mult),
                        wait=[mm_done[i - 2]] if i >= 2 else ([carry_mm] if carry_mm else []),
                        inc=(cv, 1))
                    tmm = None
                    for ts4 in range(4):
                        for nn2 in range(2):
                            wt = ([(ld, t0), (cv, tq)] +
                                  ([carry_sil] if (carry_sil and i == 0) else [])
                                  ) if (ts4 == 0 and nn2 == 0) else []
                            tmm = p.emit("tensor", lambda E, i=i, ts4=ts4, nn2=nn2, bw=bw,
                                         hb=hb:
                                E.matmul(
                                    psD[:, ts4 * D + nn2 * 512:ts4 * D + (nn2 + 1) * 512],
                                    hb[:, ts4 * 128:(ts4 + 1) * 128],
                                    bw[:, nn2 * 512:(nn2 + 1) * 512],
                                    start=(i == 0), stop=(i == NI0 - 1)), wait=wt,
                                inc=(ct, 1) if (ts4 == 3 and nn2 == 1) else None)
                    mm_done.append((ct, tmm))
                tdn = ct.n
                tsil = None
                for ts4 in range(4):
                    lc = tg4 * 4 + ts4
                    wg_ = [(st, gst[-1])] if gst else []
                    tsil = p.emit("scalar", lambda E, ts4=ts4: E.activation(
                        gstage[:, ts4 * D:(ts4 + 1) * D], psD[:, ts4 * D:(ts4 + 1) * D],
                        AF.Silu), wait=[(ct, tdn)] + (wg_ if ts4 == 0 else []), inc=(cs, 1))
                    gst.append(p.emit("sync", lambda E, ts4=ts4, lc=lc: E.dma_start(
                        out=g_d.ap()[lc * 128:(lc + 1) * 128, :],
                        in_=gstage[:, ts4 * D:(ts4 + 1) * D]),
                        wait=[(cs, tsil)], inc=(st, 16)))
                carry_mm = (ct, tdn)
                carry_sil = (cs, tsil)
            p.flush()
        h_es.close()
        xq_es.close()

        if upto == "H":
            p.emit("sync", lambda E: E.nop(), wait=[(st, st.n)])
            p.barrier()
            p.flush()
            return nc


        katt_es = contextlib.ExitStack()
        k_fm = katt_es.enter_context(nc.sbuf_tensor("k_fm", [128, DC * TO], BF16))
        acc = katt_es.enter_context(nc.sbuf_tensor("acc", [128, NTO * D], BF16))
        fls = katt_es.enter_context(nc.sbuf_tensor("fls", [128, 64], F32))
        mcol, scolf, rcol = fls[:, 0:16], fls[:, 16:32], fls[:, 32:48]

        def ln_mini(ybuf, sqscr, wait_ev):
            t1 = p.emit("vector", lambda E: E.reduce_sum(
                tiny[:, 68:69], ybuf[:], axis=AX.X), wait=wait_ev, inc=(cv, 1))
            t2 = p.emit("scalar", lambda E: E.activation(
                sqscr[:], ybuf[:], AF.Square, accum_out=tiny[:, 69:70]),
                wait=wait_ev, inc=(cs, 1))
            t3 = p.emit("vector", lambda E: E.tensor_scalar(
                tiny[:, 70:71], tiny[:, 68:69], -1.0 / D, None, ALU.mult),
                wait=[(cv, t1)], inc=(cv, 1))
            t4 = p.emit("scalar", lambda E: E.activation(
                tiny[:, 71:72], tiny[:, 70:71], AF.Square), wait=[(cv, t3)], inc=(cs, 1))
            t5 = p.emit("vector", lambda E: E.tensor_scalar(
                tiny[:, 72:73], tiny[:, 69:70], 1.0 / D, EPS, ALU.mult, ALU.add),
                wait=[(cs, t2)], inc=(cv, 1))
            t6 = p.emit("vector", lambda E: E.tensor_tensor(
                tiny[:, 73:74], tiny[:, 72:73], tiny[:, 71:72], ALU.subtract),
                wait=[(cv, t5), (cs, t4)], inc=(cv, 1))
            t7 = p.emit("vector", lambda E: E.reciprocal(
                tiny[:, 74:75], tiny[:, 73:74]), wait=[(cv, t6)], inc=(cv, 1))
            t8 = p.emit("scalar", lambda E: E.activation(
                tiny[:, 75:76], tiny[:, 74:75], AF.Sqrt), wait=[(cv, t7)], inc=(cs, 1))
            return tiny[:, 70:71], tiny[:, 75:76], (cs, t8)

        def tr8(src_bf, dst_fn, psTx, wait_src, fuse=None):
            evs = []
            for c in range(DC):
                off = (c % 2) * 1024
                wt = list(wait_src) if c < 2 else [(cv, evs[c - 2])]
                tt = p.emit("tensor", lambda E, c=c, off=off, s=src_bf: E.transpose(
                    psTx[:, off:off + 128], s[:, c * 128:(c + 1) * 128], id_sb[:]),
                    wait=wt, inc=(ct, 1))
                if fuse is None:
                    ev = p.emit("vector", lambda E, c=c, off=off: E.tensor_copy(
                        dst_fn(c), psTx[:, off:off + 128]), wait=[(ct, tt)], inc=(cv, 1))
                else:
                    gr, br = fuse
                    ev = p.emit("vector", lambda E, c=c, off=off, gr=gr, br=br: E.tensor_scalar(
                        dst_fn(c), psTx[:, off:off + 128],
                        pv_pp[:, gr * DC + c:gr * DC + c + 1],
                        pv_pp[:, br * DC + c:br * DC + c + 1], ALU.mult, ALU.add),
                        wait=[(ct, tt)], inc=(cv, 1))
                evs.append(ev)
            return evs

        def proj_tm(ps, xnT, W_bf, brow_ix, wait_first):
            for d in range(DC):
                for nn in range(2):
                    p.emit("tensor", lambda E, d=d, nn=nn, ps=ps, xnT=xnT, W_bf=W_bf:
                        E.matmul(ps[:, nn * 512:(nn + 1) * 512],
                                 xnT[:, d * 128:(d + 1) * 128],
                                 W_bf[:, d * D + nn * 512:d * D + (nn + 1) * 512],
                                 start=(d == 0), stop=False),
                        wait=wait_first if (d == 0 and nn == 0) else [])
            tb = None
            for nn in range(2):
                tb = p.emit("tensor", lambda E, nn=nn, ps=ps, brow_ix=brow_ix: E.matmul(
                    ps[:, nn * 512:(nn + 1) * 512], onesrb[:1, :],
                    brow[:, brow_ix * D + nn * 512:brow_ix * D + (nn + 1) * 512],
                    start=False, stop=True), inc=(ct, 1) if nn == 1 else None)
            return tb

        with contextlib.ExitStack() as esK:
            xbK = [esK.enter_context(nc.sbuf_tensor(f"xbK{i}", [128, D], F32)) for i in range(2)]
            xnbK = esK.enter_context(nc.sbuf_tensor("xnbK", [128, D], BF16))
            xnTK = esK.enter_context(nc.sbuf_tensor("xnTK", [128, D], BF16))
            ykf = esK.enter_context(nc.sbuf_tensor("ykf", [128, D], F32))
            kb = esK.enter_context(nc.sbuf_tensor("kb", [128, D], BF16))
            sqK = esK.enter_context(nc.sbuf_tensor("sqK", [128, D], F32))
            Wk_bf = esK.enter_context(nc.sbuf_tensor("Wk_bf", [128, DC * D], BF16))
            psTK = esK.enter_context(nc.psum_tensor("psTK", [128, 2048], BF16))
            psK = esK.enter_context(nc.psum_tensor("psK", [128, D], F32))
            wk_t = p.emit("gpsimd", lambda E: E.dma_start(
                out=Wk_bf[:], in_=bass.AP(Wk_h, 0, [[D, 128], [128 * D, DC], [1, D]])),
                inc=(gd, 16))
            prevdone = None
            for i in range(NTO):
                buf = xbK[i % 2]
                w = [prevdone] if prevdone else []
                t0 = p.emit("sync", lambda E, i=i, buf=buf: E.dma_start(
                    out=buf[:], in_=xro[i]), wait=w, inc=(ld, 16))
                txn = p.emit("vector", lambda E, i=i, buf=buf: E.tensor_scalar(
                    xnbK[:], buf[:], stat[:, 2 * NTB + i:2 * NTB + i + 1],
                    stat[:, 3 * NTB + i:3 * NTB + i + 1], ALU.add, ALU.mult),
                    wait=[(ld, t0), (cv, cv.n)], inc=(cv, 1))
                tr8(xnbK, lambda c: xnTK[:, c * 128:(c + 1) * 128], psTK, [(cv, txn)])
                tev = cv.n
                tkm = proj_tm(psK, xnTK, Wk_bf, 0, [(cv, tev), (gd, wk_t), (cv, cv.n)])
                tye = p.emit("vector", lambda E: E.tensor_copy(ykf[:], psK[:]),
                             wait=[(ct, tkm), (cs, cs.n)], inc=(cv, 1))
                negmu, inv, sdone = ln_mini(ykf, sqK, [(cv, tye)])
                tkb = p.emit("vector", lambda E, negmu=negmu, inv=inv: E.tensor_scalar(
                    kb[:], ykf[:], negmu, inv, ALU.add, ALU.mult),
                    wait=[sdone], inc=(cv, 1))
                evs = tr8(kb, (lambda c, i=i: k_fm[:, c * TO + i * 128:c * TO + (i + 1) * 128]
                               ) if False else (lambda c, i=i: k_fm[:, c * TO + i * 128:c * TO + (i + 1) * 128]),
                          psTK, [(cv, tkb)], fuse=(1, 2))
                prevdone = (cv, evs[-1])
            p.barrier()
            p.flush()

        x1_written = []
        for hf in range(2):
            with contextlib.ExitStack() as esQ:
                qh = esQ.enter_context(nc.sbuf_tensor("qh%d" % hf, [128, DC * TO], BF16))
                vh = esQ.enter_context(nc.sbuf_tensor("vh%d" % hf, [128, NTO * D], BF16))
                Wq_bf = esQ.enter_context(nc.sbuf_tensor("Wq_bf%d" % hf, [128, DC * D], BF16))
                Wv_bf = esQ.enter_context(nc.sbuf_tensor("Wv_bf%d" % hf, [128, DC * D], BF16))
                wq_t = p.emit("gpsimd", lambda E: E.dma_start(
                    out=Wq_bf[:], in_=bass.AP(Wq_h, 0, [[D, 128], [128 * D, DC], [1, D]])),
                    inc=(gd, 16))
                wv_t = p.emit("gpsimd", lambda E: E.dma_start(
                    out=Wv_bf[:], in_=bass.AP(Wv_h, 0, [[D, 128], [128 * D, DC], [1, D]])),
                    inc=(gd, 16))
                with contextlib.ExitStack() as esP:
                    xbP = [esP.enter_context(nc.sbuf_tensor(f"xbP{i}_%d" % hf, [128, D], F32)) for i in range(2)]
                    xnbP = esP.enter_context(nc.sbuf_tensor("xnbP%d" % hf, [128, D], BF16))
                    xnTP = esP.enter_context(nc.sbuf_tensor("xnTP%d" % hf, [128, D], BF16))
                    yqf = esP.enter_context(nc.sbuf_tensor("yqf%d" % hf, [128, D], F32))
                    yvf = esP.enter_context(nc.sbuf_tensor("yvf%d" % hf, [128, D], F32))
                    qb = esP.enter_context(nc.sbuf_tensor("qb%d" % hf, [128, D], BF16))
                    sqP = esP.enter_context(nc.sbuf_tensor("sqP%d" % hf, [128, D], F32))
                    psTP = esP.enter_context(nc.psum_tensor("psTP%d" % hf, [128, 2048], BF16))
                    psQ = esP.enter_context(nc.psum_tensor("psQ%d" % hf, [128, D], F32))
                    psV = esP.enter_context(nc.psum_tensor("psV%d" % hf, [128, D], F32))
                    prevq = None
                    for j in range(NTO):
                        ti = hf * NTO + j
                        buf = xbP[j % 2]
                        w = [prevq] if prevq else []
                        t0 = p.emit("sync", lambda E, ti=ti, buf=buf: E.dma_start(
                            out=buf[:], in_=xro[ti]), wait=w, inc=(ld, 16))
                        txn = p.emit("vector", lambda E, ti=ti, buf=buf: E.tensor_scalar(
                            xnbP[:], buf[:], stat[:, 2 * NTB + ti:2 * NTB + ti + 1],
                            stat[:, 3 * NTB + ti:3 * NTB + ti + 1], ALU.add, ALU.mult),
                            wait=[(ld, t0), (cv, cv.n)], inc=(cv, 1))
                        tr8(xnbP, lambda c: xnTP[:, c * 128:(c + 1) * 128], psTP, [(cv, txn)])
                        tev = cv.n
                        tqm = proj_tm(psQ, xnTP, Wq_bf, 1, [(cv, tev), (gd, wq_t), (cv, cv.n)])
                        tvm = proj_tm(psV, xnTP, Wv_bf, 2, [(gd, wv_t), (cv, cv.n)])
                        tqe = p.emit("vector", lambda E: E.tensor_copy(yqf[:], psQ[:]),
                                     wait=[(ct, tqm), (cs, cs.n)], inc=(cv, 1))
                        tve = p.emit("vector", lambda E: E.tensor_copy(yvf[:], psV[:]),
                                     wait=[(ct, tvm), (cs, cs.n)], inc=(cv, 1))
                        negmu, inv, sdone = ln_mini(yqf, sqP, [(cv, tqe)])
                        tqb = p.emit("vector", lambda E, negmu=negmu, inv=inv: E.tensor_scalar(
                            qb[:], yqf[:], negmu, inv, ALU.add, ALU.mult),
                            wait=[sdone], inc=(cv, 1))
                        tr8(qb, (lambda c, j=j: qh[:, c * TO + j * 128:c * TO + (j + 1) * 128]),
                            psTP, [(cv, tqb)], fuse=(4, 5))
                        negmu2, inv2, sdone2 = ln_mini(yvf, sqP, [(cv, tve), (cv, cv.n)])
                        prevq = (cv, p.emit("vector", lambda E, j=j, negmu2=negmu2, inv2=inv2:
                            E.tensor_scalar(vh[:, j * D:(j + 1) * D], yvf[:],
                                            negmu2, inv2, ALU.add, ALU.mult),
                            wait=[sdone2], inc=(cv, 1)))
                    p.barrier()
                    p.flush()
                with contextlib.ExitStack() as esFl:
                    e_bf = esFl.enter_context(nc.sbuf_tensor("e_bf%d" % hf, [128, TO], BF16))
                    eT = esFl.enter_context(nc.sbuf_tensor("eT%d" % hf, [128, TO], BF16))
                    fA = esFl.enter_context(nc.sbuf_tensor("fA%d" % hf, [128, D], F32))
                    fB = esFl.enter_context(nc.sbuf_tensor("fB%d" % hf, [128, D], F32))
                    fC = esFl.enter_context(nc.sbuf_tensor("fC%d" % hf, [128, D], F32))
                    xb3 = esFl.enter_context(nc.sbuf_tensor("xb3%d" % hf, [128, D], F32))
                    gbuf = esFl.enter_context(nc.sbuf_tensor("gbuf%d" % hf, [128, D], BF16))
                    psSC = esFl.enter_context(nc.psum_tensor("psSC%d" % hf, [128, TO], F32))
                    psTF = esFl.enter_context(nc.psum_tensor("psTF%d" % hf, [128, 2048], BF16))
                    psAV = esFl.enter_context(nc.psum_tensor("psAV%d" % hf, [128, D], F32))
                    if hf == 0:
                        p.emit("vector", lambda E: E.memset(mcol, -3.0e38), inc=(cv, 1))
                        p.emit("vector", lambda E: E.memset(scolf, 0.0), inc=(cv, 1))
                        p.emit("vector", lambda E: E.memset(acc[:], 0.0), inc=(cv, 1))
                    prev_exp = None
                    prev_x1 = None
                    prev_tr = None
                    for lc in range(NTO):
                        smw = ([(cs, prev_exp)] if prev_exp else []) + [(cv, cv.n)]
                        tsc = None
                        for c in range(DC):
                            for s4 in range(4):
                                tsc = p.emit("tensor", lambda E, c=c, s4=s4, lc=lc: E.matmul(
                                    psSC[:, s4 * 512:(s4 + 1) * 512],
                                    k_fm[:, c * TO + lc * 128:c * TO + (lc + 1) * 128],
                                    qh[:, c * TO + s4 * 512:c * TO + (s4 + 1) * 512],
                                    start=(c == 0), stop=(c == DC - 1)),
                                    wait=smw if (c == 0 and s4 == 0) else [],
                                    inc=(ct, 1) if (c == DC - 1 and s4 == 3) else None)
                        u1 = p.emit("vector", lambda E: E.tensor_reduce(
                            fls[:, 48:49], psSC[:], AX.X, ALU.max), wait=[(ct, tsc)], inc=(cv, 1))
                        u2 = p.emit("vector", lambda E, lc=lc: E.tensor_tensor(
                            fls[:, 49:50], mcol[:, lc:lc + 1], fls[:, 48:49], ALU.max),
                            wait=[(cv, u1)], inc=(cv, 1))
                        u3 = p.emit("vector", lambda E, lc=lc: E.tensor_tensor(
                            fls[:, 50:51], mcol[:, lc:lc + 1], fls[:, 49:50], ALU.subtract),
                            wait=[(cv, u2)], inc=(cv, 1))
                        u4 = p.emit("scalar", lambda E, lc=lc: E.activation(
                            rcol[:, lc:lc + 1], fls[:, 50:51], AF.Exp),
                            wait=[(cv, u3)], inc=(cs, 1))
                        u5 = p.emit("vector", lambda E: E.tensor_scalar(
                            fls[:, 51:52], fls[:, 49:50], -1.0, None, ALU.mult),
                            wait=[(cv, u3)], inc=(cv, 1))
                        u6 = p.emit("vector", lambda E, lc=lc: E.tensor_copy(
                            mcol[:, lc:lc + 1], fls[:, 49:50]), wait=[(cv, u5)], inc=(cv, 1))
                        texp = p.emit("scalar", lambda E: E.activation(
                            e_bf[:], psSC[:], AF.Exp, bias=fls[:, 51:52],
                            accum_out=fls[:, 52:53]),
                            wait=[(cv, u5)] + ([(ct, prev_tr)] if prev_tr else []),
                            inc=(cs, 1))
                        prev_exp = texp
                        avdone = None
                        evs = []
                        for Lc in range(NTO):
                            off = (Lc % 2) * 1024
                            wt = [(cs, texp)] if Lc < 2 else [(cv, evs[Lc - 2])]
                            tt = p.emit("tensor", lambda E, Lc=Lc, off=off: E.transpose(
                                psTF[:, off:off + 128], e_bf[:, Lc * 128:(Lc + 1) * 128],
                                id_sb[:]), wait=wt, inc=(ct, 1))
                            ev = p.emit("vector", lambda E, Lc=Lc, off=off: E.tensor_copy(
                                eT[:, Lc * 128:(Lc + 1) * 128], psTF[:, off:off + 128]),
                                wait=[(ct, tt)], inc=(cv, 1))
                            evs.append(ev)
                            prev_tr = tt
                            for nn in range(2):
                                avdone = p.emit("tensor", lambda E, Lc=Lc, nn=nn: E.matmul(
                                    psAV[:, nn * 512:(nn + 1) * 512],
                                    eT[:, Lc * 128:(Lc + 1) * 128],
                                    vh[:, Lc * D + nn * 512:Lc * D + (nn + 1) * 512],
                                    start=(Lc == 0), stop=(Lc == NTO - 1)),
                                    wait=[(cv, ev)] if nn == 0 else [],
                                    inc=(ct, 1) if (Lc == NTO - 1 and nn == 1) else None)
                        if hf == 0:
                            ta = p.emit("vector", lambda E, lc=lc: E.tensor_copy(
                                acc[:, lc * D:(lc + 1) * D], psAV[:]),
                                wait=[(ct, avdone)], inc=(cv, 1))
                            p.emit("vector", lambda E, lc=lc: E.tensor_copy(
                                scolf[:, lc:lc + 1], fls[:, 52:53]),
                                wait=[(cs, texp), (cv, ta)], inc=(cv, 1))
                        else:
                            wx = [prev_x1] if prev_x1 else []
                            tg0 = p.emit("sync", lambda E, lc=lc: E.dma_start(
                                out=gbuf[:], in_=g_d.ap()[lc * 128:(lc + 1) * 128, :]),
                                wait=wx, inc=(ld, 16))
                            tx0 = p.emit("sync", lambda E, lc=lc: E.dma_start(
                                out=xb3[:], in_=xro[lc]), inc=(ld, 16))
                            v1_ = p.emit("vector", lambda E, lc=lc: E.scalar_tensor_tensor(
                                fA[:], acc[:, lc * D:(lc + 1) * D], rcol[:, lc:lc + 1],
                                psAV[:], ALU.mult, ALU.add),
                                wait=[(ct, avdone)] + ([prev_x1] if prev_x1 else []),
                                inc=(cv, 1))
                            v2_ = p.emit("vector", lambda E, lc=lc: E.scalar_tensor_tensor(
                                fls[:, 53:54], scolf[:, lc:lc + 1], rcol[:, lc:lc + 1],
                                fls[:, 52:53], ALU.mult, ALU.add),
                                wait=[(cs, texp), (cv, v1_)], inc=(cv, 1))
                            v3_ = p.emit("vector", lambda E: E.reciprocal(
                                fls[:, 54:55], fls[:, 53:54]), wait=[(cv, v2_)], inc=(cv, 1))
                            v4_ = p.emit("vector", lambda E: E.scalar_tensor_tensor(
                                fB[:], fA[:], fls[:, 54:55], gvbc[:], ALU.mult, ALU.mult),
                                wait=[(cv, v3_)], inc=(cv, 1))
                            v5_ = p.emit("vector", lambda E, lc=lc: E.tensor_scalar(
                                fC[:], xb3[:], stat[:, 2 * NTB + lc:2 * NTB + lc + 1],
                                stat[:, 3 * NTB + lc:3 * NTB + lc + 1], ALU.add, ALU.mult),
                                wait=[(ld, tx0), (cv, v4_)], inc=(cv, 1))
                            v6_ = p.emit("vector", lambda E: E.tensor_tensor(
                                fA[:], fC[:], bvbc[:], ALU.add), wait=[(cv, v5_)], inc=(cv, 1))
                            v7_ = p.emit("vector", lambda E: E.tensor_tensor(
                                fC[:], fB[:], fA[:], ALU.add), wait=[(cv, v6_)], inc=(cv, 1))
                            v8_ = p.emit("vector", lambda E: E.tensor_tensor(
                                fB[:], fC[:], gbuf[:], ALU.mult),
                                wait=[(ld, tg0), (cv, v7_)], inc=(cv, 1))
                            v9_ = p.emit("vector", lambda E: E.tensor_tensor(
                                fC[:], fB[:], xb3[:], ALU.add), wait=[(cv, v8_)], inc=(cv, 1))
                            tx1 = p.emit("sync", lambda E, lc=lc: E.dma_start(
                                out=x1t[lc], in_=fC[:]), wait=[(cv, v9_)], inc=(st, 16))
                            if upto == "L0":
                                p.emit("sync", lambda E, lc=lc: E.dma_start(
                                    out=out_h.ap()[lc * 128:(lc + 1) * 128, :], in_=fC[:]),
                                    inc=(st, 16))
                            prev_x1 = (st, st.n)
                            x1_written.append(tx1)
                    p.barrier()
                    p.flush()
        katt_es.close()
        l0c_es.close()

        if upto == "L0":
            p.emit("sync", lambda E: E.nop(), wait=[(st, st.n)])
            p.barrier()
            p.flush()
            return nc

        def agrow(sc):
            return (sc // 2) * 128 + (0 if sc % 2 == 0 else TO)

        L1_es = contextlib.ExitStack()
        out_acc = L1_es.enter_context(nc.sbuf_tensor("out_acc", [128, NTO * D], F32))
        gmoe_es = contextlib.ExitStack()
        gate_fm = gmoe_es.enter_context(nc.sbuf_tensor("gate_fm", [8, TO], BF16))

        with contextlib.ExitStack() as es9:
            xbuf9 = [es9.enter_context(nc.sbuf_tensor(f"xb9{i}", [128, D], F32)) for i in range(2)]
            sq9 = es9.enter_context(nc.sbuf_tensor("sq9", [128, D], F32))
            def ln_stats2(src_tiles, n_tiles, sbuf_stat, xwait):
                tgts = []
                for i in range(n_tiles):
                    buf = xbuf9[i % 2]
                    w = list(xwait) + (tgts[i - 2] if i >= 2 else [])
                    t0 = p.emit("sync", lambda E, i=i, buf=buf: E.dma_start(
                        out=buf[:], in_=src_tiles[i]), wait=w, inc=(ld, 16))
                    tv = p.emit("vector", lambda E, i=i, buf=buf: E.reduce_sum(
                        sbuf_stat[:, i:i + 1], buf[:], axis=AX.X),
                        wait=[(ld, t0)], inc=(cv, 1))
                    ts_ = p.emit("scalar", lambda E, i=i, buf=buf: E.activation(
                        sq9[:], buf[:], AF.Square,
                        accum_out=sbuf_stat[:, n_tiles + i:n_tiles + i + 1]),
                        wait=[(ld, t0)], inc=(cs, 1))
                    tgts.append([(cv, tv), (cs, ts_)])
                n = n_tiles
                p.emit("vector", lambda E, n=n, sb_=sbuf_stat: E.tensor_scalar(
                    sb_[:, 2 * n:3 * n], sb_[:, 0:n], -1.0 / D, None, ALU.mult),
                    wait=[(cs, cs.n), (cv, cv.n)], inc=(cv, 1))
                tmu2 = p.emit("scalar", lambda E, n=n, sb_=sbuf_stat: E.activation(
                    tiny[:, 0:n], sb_[:, 2 * n:3 * n], AF.Square),
                    wait=[(cv, cv.n)], inc=(cs, 1))
                p.emit("vector", lambda E, n=n, sb_=sbuf_stat: E.tensor_scalar(
                    tiny[:, n:2 * n], sb_[:, n:2 * n], 1.0 / D, EPS, ALU.mult, ALU.add),
                    inc=(cv, 1))
                p.emit("vector", lambda E, n=n, sb_=sbuf_stat: E.tensor_tensor(
                    sb_[:, 3 * n:4 * n], tiny[:, n:2 * n], tiny[:, 0:n], ALU.subtract),
                    wait=[(cs, tmu2), (cv, cv.n)], inc=(cv, 1))
                p.emit("vector", lambda E, n=n, sb_=sbuf_stat: E.reciprocal(
                    tiny[:, 0:n], sb_[:, 3 * n:4 * n]), wait=[(cv, cv.n)], inc=(cv, 1))
                return p.emit("scalar", lambda E, n=n, sb_=sbuf_stat: E.activation(
                    sb_[:, 3 * n:4 * n], tiny[:, 0:n], AF.Sqrt),
                    wait=[(cv, cv.n)], inc=(cs, 1))
            st1_done = ln_stats2(x1t, NTO, st1, [(st, st.n)])
            p.barrier()
            p.flush()

        with contextlib.ExitStack() as es10:
            xn1_fm = es10.enter_context(nc.sbuf_tensor("xn1_fm", [128, DC * TO], BF16))
            xf_fm = es10.enter_context(nc.sbuf_tensor("xf_fm", [128, DC * TO], BF16))
            with contextlib.ExitStack() as es10i:
                xb10 = [es10i.enter_context(nc.sbuf_tensor(f"xb10{i}", [128, D], F32)) for i in range(2)]
                xn1f = es10i.enter_context(nc.sbuf_tensor("xn1f", [128, D], F32))
                xn1b = es10i.enter_context(nc.sbuf_tensor("xn1b", [128, D], BF16))
                xfb = es10i.enter_context(nc.sbuf_tensor("xfb", [128, D], BF16))
                sq10 = es10i.enter_context(nc.sbuf_tensor("sq10", [128, D], F32))
                psT10 = es10i.enter_context(nc.psum_tensor("psT10", [128, 2048], BF16))
                prevt = None
                for i in range(NTO):
                    buf = xb10[i % 2]
                    w = [prevt] if prevt else []
                    t0 = p.emit("sync", lambda E, i=i, buf=buf: E.dma_start(
                        out=buf[:], in_=x1t[i]), wait=w, inc=(ld, 16))
                    tx = p.emit("vector", lambda E, i=i, buf=buf: E.tensor_scalar(
                        xn1f[:], buf[:], st1[:, 2 * NTO + i:2 * NTO + i + 1],
                        st1[:, 3 * NTO + i:3 * NTO + i + 1], ALU.add, ALU.mult),
                        wait=[(ld, t0), (cs, max(st1_done, cs.n)), (cv, cv.n)], inc=(cv, 1))
                    txb = p.emit("vector", lambda E: E.tensor_copy(xn1b[:], xn1f[:]),
                                 wait=[(cv, tx)], inc=(cv, 1))
                    tr8(xn1b, (lambda c, i=i: xn1_fm[:, c * TO + i * 128:c * TO + (i + 1) * 128]),
                        psT10, [(cv, txb)])
                    negmu2, inv2, sdone2 = ln_mini(xn1f, sq10, [(cv, tx), (cv, cv.n)])
                    txf = p.emit("vector", lambda E, negmu2=negmu2, inv2=inv2: E.tensor_scalar(
                        xfb[:], xn1f[:], negmu2, inv2, ALU.add, ALU.mult),
                        wait=[sdone2], inc=(cv, 1))
                    evs = tr8(xfb, (lambda c, i=i: xf_fm[:, c * TO + i * 128:c * TO + (i + 1) * 128]),
                              psT10, [(cv, txf)], fuse=(9, 10))
                    prevt = (cv, evs[-1])
                p.barrier()
                p.flush()
            for c_ in range(DC):
                p.emit("sync", lambda E, c_=c_: E.dma_start(
                    out=xf_d.ap()[c_, :, :], in_=xf_fm[:, c_ * TO:(c_ + 1) * TO]),
                    wait=[(cv, cv.n)] if c_ == 0 else [], inc=(st, 16))
            with contextlib.ExitStack() as esR:
                rw_bf = esR.enter_context(nc.sbuf_tensor("rw_bf", [128, DC * 8], BF16))
                lg = esR.enter_context(nc.sbuf_tensor("lg", [128, 8], F32))
                lg2 = esR.enter_context(nc.sbuf_tensor("lg2", [128, 8], F32))
                prb = esR.enter_context(nc.sbuf_tensor("prb", [128, 8], F32))
                gt_tm = esR.enter_context(nc.sbuf_tensor("gt_tm", [128, NTO * 8], BF16))
                psR = esR.enter_context(nc.psum_tensor("psR", [128, 8], F32))
                psRT = esR.enter_context(nc.psum_tensor("psRT", [8, 2048], BF16))
                trw = p.emit("gpsimd", lambda E: E.dma_start(
                    out=rw_bf[:], in_=bass.AP(rwT_h, 0, [[8, 128], [128 * 8, DC], [1, 8]])),
                    inc=(gd, 16))
                for i in range(NTO):
                    tm = None
                    for d in range(DC):
                        tm = p.emit("tensor", lambda E, d=d, i=i: E.matmul(
                            psR[:], xf_fm[:, d * TO + i * 128:d * TO + (i + 1) * 128],
                            rw_bf[:, d * 8:(d + 1) * 8],
                            start=(d == 0), stop=(d == DC - 1)),
                            wait=[(gd, trw), (cv, cv.n)] if d == 0 else [],
                            inc=(ct, 1) if d == DC - 1 else None)
                    a1 = p.emit("vector", lambda E: E.tensor_scalar(
                        lg[:], psR[:], -10000.0, 10000.0, ALU.max, ALU.min),
                        wait=[(ct, tm)], inc=(cv, 1))
                    a2 = p.emit("vector", lambda E: E.tensor_reduce(
                        tiny[:, 48:49], lg[:], AX.X, ALU.max), wait=[(cv, a1)], inc=(cv, 1))
                    a3 = p.emit("vector", lambda E: E.tensor_scalar(
                        tiny[:, 49:50], tiny[:, 48:49], -1.0, None, ALU.mult),
                        wait=[(cv, a2)], inc=(cv, 1))
                    a4 = p.emit("scalar", lambda E: E.activation(
                        lg2[:], lg[:], AF.Exp, bias=tiny[:, 49:50],
                        accum_out=tiny[:, 50:51]), wait=[(cv, a3)], inc=(cs, 1))
                    a5 = p.emit("vector", lambda E: E.tensor_scalar(
                        tiny[:, 51:52], tiny[:, 50:51], 1.0e-8, None, ALU.add),
                        wait=[(cs, a4)], inc=(cv, 1))
                    a6 = p.emit("vector", lambda E: E.reciprocal(
                        tiny[:, 52:53], tiny[:, 51:52]), wait=[(cv, a5)], inc=(cv, 1))
                    a7 = p.emit("vector", lambda E: E.tensor_scalar(
                        prb[:], lg2[:], tiny[:, 52:53], None, ALU.mult),
                        wait=[(cv, a6)], inc=(cv, 1))
                    a8 = p.emit("vector", lambda E: E.tensor_scalar(
                        lg[:], prb[:], 1.0e-8, 1.0, ALU.max, ALU.min),
                        wait=[(cv, a7)], inc=(cv, 1))
                    b1 = p.emit("vector", lambda E: E.tensor_reduce(
                        tiny[:, 53:54], lg[:], AX.X, ALU.max), wait=[(cv, a8)], inc=(cv, 1))
                    b2 = p.emit("vector", lambda E: E.tensor_single_scalar(
                        lg2[:], lg[:], tiny[:, 53:54], ALU.is_ge), wait=[(cv, b1)], inc=(cv, 1))
                    b3 = p.emit("vector", lambda E: E.scalar_tensor_tensor(
                        prb[:], lg2[:], -1.0e30, lg[:], ALU.mult, ALU.add),
                        wait=[(cv, b2)], inc=(cv, 1))
                    b4 = p.emit("vector", lambda E: E.tensor_reduce(
                        tiny[:, 54:55], prb[:], AX.X, ALU.max), wait=[(cv, b3)], inc=(cv, 1))
                    b5 = p.emit("vector", lambda E: E.tensor_single_scalar(
                        lg2[:], lg[:], tiny[:, 54:55], ALU.is_ge), wait=[(cv, b4)], inc=(cv, 1))
                    b6 = p.emit("vector", lambda E, i=i: E.tensor_tensor(
                        gt_tm[:, i * 8:(i + 1) * 8], lg[:], lg2[:], ALU.mult),
                        wait=[(cv, b5)], inc=(cv, 1))
                prevev = None
                for i in range(NTO):
                    wt = [(cv, cv.n)] if prevev is None else [(cv, prevev)]
                    tt = p.emit("tensor", lambda E, i=i: E.transpose(
                        psRT[:, 0:128], gt_tm[:, i * 8:(i + 1) * 8], id_sb[:]),
                        wait=wt, inc=(ct, 1))
                    prevev = p.emit("vector", lambda E, i=i: E.tensor_copy(
                        gate_fm[:, i * 128:(i + 1) * 128], psRT[:, 0:128]),
                        wait=[(ct, tt)], inc=(cv, 1))
                p.barrier()
                p.flush()
            with contextlib.ExitStack() as esL:
                lat_fm = esL.enter_context(nc.sbuf_tensor("lat_fm", [128, NLAT * TO], BF16))
                Wcd_bf = esL.enter_context(nc.sbuf_tensor("Wcd_bf", [128, DC * LAT], BF16))
                Wqu_bf = esL.enter_context(nc.sbuf_tensor("Wqu_bf", [128, NLAT * D], BF16))
                Wkv_bf = esL.enter_context(nc.sbuf_tensor("Wkv_bf", [128, NLAT * 128], BF16))
                qtmp = esL.enter_context(nc.sbuf_tensor("qtmp", [128, TO], BF16))
                kvb = esL.enter_context(nc.sbuf_tensor("kvb", [128, 128], BF16))
                psL = esL.enter_context(nc.psum_tensor("psL", [128, TO], F32))
                psKV = esL.enter_context(nc.psum_tensor("psKV", [128, 128], F32))
                tcd = p.emit("gpsimd", lambda E: E.dma_start(
                    out=Wcd_bf[:], in_=bass.AP(Wcd_h, 0, [[LAT, 128], [128 * LAT, DC], [1, LAT]])),
                    inc=(gd, 16))
                tqu = p.emit("gpsimd", lambda E: E.dma_start(
                    out=Wqu_bf[:], in_=bass.AP(Wqu_h, 0, [[D, 128], [128 * D, NLAT], [1, D]])),
                    inc=(gd, 16))
                tkv = p.emit("gpsimd", lambda E: E.dma_start(
                    out=Wkv_bf[:], in_=bass.AP(Wkv_h, 0, [[128, 128], [128 * 128, NLAT], [1, 128]])),
                    inc=(gd, 16))
                prevev = None
                for L in range(NLAT):
                    tm = None
                    for d in range(DC):
                        for s4 in range(4):
                            tm = p.emit("tensor", lambda E, L=L, d=d, s4=s4: E.matmul(
                                psL[:, s4 * 512:(s4 + 1) * 512],
                                Wcd_bf[:, d * LAT + L * 128:d * LAT + (L + 1) * 128],
                                xn1_fm[:, d * TO + s4 * 512:d * TO + (s4 + 1) * 512],
                                start=(d == 0), stop=(d == DC - 1)),
                                wait=([(gd, tcd)] + ([(cv, prevev)] if prevev else [(cv, cv.n)])
                                      ) if (d == 0 and s4 == 0) else [],
                                inc=(ct, 1) if (d == DC - 1 and s4 == 3) else None)
                    prevev = p.emit("vector", lambda E, L=L: E.tensor_copy(
                        lat_fm[:, L * TO:(L + 1) * TO], psL[:]), wait=[(ct, tm)], inc=(cv, 1))
                qsp = None
                for qc in range(DC):
                    tm = None
                    for L in range(NLAT):
                        for s4 in range(4):
                            tm = p.emit("tensor", lambda E, qc=qc, L=L, s4=s4: E.matmul(
                                psL[:, s4 * 512:(s4 + 1) * 512],
                                Wqu_bf[:, L * D + qc * 128:L * D + (qc + 1) * 128],
                                lat_fm[:, L * TO + s4 * 512:L * TO + (s4 + 1) * 512],
                                start=(L == 0), stop=(L == NLAT - 1)),
                                wait=[(gd, tqu), (cv, prevev)] if (L == 0 and s4 == 0) else [],
                                inc=(ct, 1) if (L == NLAT - 1 and s4 == 3) else None)
                    prevev = p.emit("vector", lambda E: E.tensor_copy(
                        qtmp[:], psL[:]),
                        wait=[(ct, tm)] + ([(st, qsp)] if qsp else []), inc=(cv, 1))
                    qsp = p.emit("sync", lambda E, qc=qc: E.dma_start(
                        out=qf_d.ap()[qc, :, :], in_=qtmp[:]),
                        wait=[(cv, prevev)], inc=(st, 16))
                pkv_cv, pkv_st = None, None
                for i in range(NTO):
                    tm = None
                    for L in range(NLAT):
                        tm = p.emit("tensor", lambda E, i=i, L=L: E.matmul(
                            psKV[:], lat_fm[:, L * TO + i * 128:L * TO + (i + 1) * 128],
                            Wkv_bf[:, L * 128:(L + 1) * 128],
                            start=(L == 0), stop=(L == NLAT - 1)),
                            wait=([(gd, tkv)] + ([(cv, pkv_cv)] if pkv_cv else [(cv, cv.n)])
                                  ) if L == 0 else [],
                            inc=(ct, 1) if L == NLAT - 1 else None)
                    pkv_cv = p.emit("vector", lambda E: E.tensor_copy(
                        kvb[:], psKV[:]),
                        wait=[(ct, tm)] + ([(st, pkv_st)] if pkv_st else []), inc=(cv, 1))
                    pkv_st = p.emit("sync", lambda E, i=i: E.dma_start(
                        out=kvi.ap()[i * 128:(i + 1) * 128, :], in_=kvb[:]),
                        wait=[(cv, pkv_cv)], inc=(st, 16))
                kv_stores = st.n
                p.barrier()
                p.flush()
            p.barrier()
            p.flush()
        p.wait("gpsimd", st, kv_stores)
        tag = p.emit("gpsimd", lambda E: E.collective_compute(
            "AllGather", ALU.bypass, replica_groups=[[0, 1], [2, 3], [4, 5], [6, 7]],
            ins=[kvi.ap().opt()], outs=[kvo.ap().opt()]), inc=(cc, 1))


        moe_es = contextlib.ExitStack()
        with moe_es:
            h1 = moe_es.enter_context(nc.sbuf_tensor("h1", [128, 16 * TO], BF16))
            xe_fm = moe_es.enter_context(nc.sbuf_tensor("xe_fm", [128, DC * TO], BF16))
            lam_b = moe_es.enter_context(nc.sbuf_tensor("lam_b", [128, TO], BF16))
            xfb2 = [moe_es.enter_context(nc.sbuf_tensor(f"xfb2{i}", [128, TO], BF16))
                    for i in range(2)]
            wgu = [moe_es.enter_context(nc.sbuf_tensor(f"wgu{i}", [128, D], BF16))
                   for i in range(4)]
            wdn = [moe_es.enter_context(nc.sbuf_tensor(f"wdn{i}", [128, D], BF16))
                   for i in range(2)]
            sl1 = moe_es.enter_context(nc.sbuf_tensor("sl1", [128, TO], BF16))
            grow = moe_es.enter_context(nc.sbuf_tensor("grow", [1, TO], BF16))
            tz = p.emit("vector", lambda E: E.memset(out_acc[:], 0.0), inc=(cv, 1))
            guT_t = guT_h.ap()
            dwT_t = dwT_h.ap()
            prev_gdn = None
            carry_gu = None
            prep_state = {}

            def moe_prep(e, psGm_b, wait_bcast, wait_xe):
                tgr = p.emit("sync", lambda E, e=e: E.dma_start(
                    out=grow[:1, :], in_=gate_fm[e:e + 1, :]),
                    wait=[prep_state["growfree"]] if "growfree" in prep_state else
                         [(cv, cv.n)], inc=(ld, 16))
                tlb = None
                for s4 in range(4):
                    tlb = p.emit("tensor", lambda E, s4=s4, psGm_b=psGm_b: E.matmul(
                        psGm_b[:, s4 * 512:(s4 + 1) * 512], onesrb[:1, :],
                        grow[:1, s4 * 512:(s4 + 1) * 512], start=True, stop=True),
                        wait=([(ld, tgr)] + wait_bcast) if s4 == 0 else [], inc=(ct, 1))
                prep_state["growfree"] = (ct, tlb)
                tlv = p.emit("vector", lambda E, psGm_b=psGm_b: E.tensor_copy(
                    lam_b[:], psGm_b[:]), wait=[(ct, tlb)], inc=(cv, 1))
                pxe_list = []
                for d in range(DC):
                    bx = xfb2[d % 2]
                    w = []
                    if d == 0:
                        w += list(wait_xe)
                    if d < 2 and prep_state.get("xeld"):
                        w = w + [prep_state["xeld"][d % 2]]
                    elif d >= 2:
                        w = w + [pxe_list[d - 2]]
                    tx0 = p.emit("sync", lambda E, e=e, d=d, bx=bx: E.dma_start(
                        out=bx[:], in_=xf_d.ap()[d, :, :]), wait=w, inc=(ld, 16))
                    tx1 = p.emit("vector", lambda E, d=d, bx=bx: E.tensor_tensor(
                        xe_fm[:, d * TO:(d + 1) * TO], bx[:], lam_b[:], ALU.mult),
                        wait=[(ld, tx0), (cv, tlv)] + (list(wait_xe) if d == 0 else []),
                        inc=(cv, 1))
                    pxe_list.append((cv, tx1))
                prep_state["xeld"] = pxe_list[-2:]
                return pxe_list[-1]

            for e in range(8):
                esE2 = contextlib.ExitStack()
                psGm = esE2.enter_context(nc.psum_tensor("psGm_%d" % e, [128, TO], F32))
                psUm = esE2.enter_context(nc.psum_tensor("psUm_%d" % e, [128, TO], F32))
                if e == 0:
                    pxe = moe_prep(0, psGm, [(cv, cv.n)], [(st, st.n)])
                elif MOE_OVL:
                    pxe = prep_state["next_pxe"]
                else:
                    pxe = moe_prep(e, psGm, [(cs, cs.n), (cv, cv.n)], [(ct, ct.n)])
                hprev_sl = None
                mml = []
                tum = None
                for j in range(16):
                    bg, bu = wgu[2 * (j % 2)], wgu[2 * (j % 2) + 1]
                    if j >= 2:
                        wld = [mml[j - 2]]
                    elif carry_gu is not None:
                        wld = [carry_gu]
                    else:
                        wld = []
                    tg0 = p.emit("gpsimd", lambda E, e=e, j=j, bg=bg: E.dma_start(
                        out=bg[:], in_=bass.AP(guT_h, e * D * 2 * FF + j * 128,
                                               [[2 * FF, 128], [128 * 2 * FF, DC], [1, 128]])),
                        wait=wld, inc=(gd, 16))
                    tu0 = p.emit("gpsimd", lambda E, e=e, j=j, bu=bu: E.dma_start(
                        out=bu[:], in_=bass.AP(guT_h, e * D * 2 * FF + FF + j * 128,
                                               [[2 * FF, 128], [128 * 2 * FF, DC], [1, 128]])),
                        inc=(gd, 16))
                    tgm = None
                    silw = [(cs, hprev_sl)] if hprev_sl else []
                    if j == 0 and prev_gdn is not None:
                        silw = silw + [prev_gdn]
                    for d in range(DC):
                        for s4 in range(4):
                            tgm = p.emit("tensor", lambda E, d=d, s4=s4, bg=bg, psGm=psGm:
                                E.matmul(
                                    psGm[:, s4 * 512:(s4 + 1) * 512],
                                    bg[:, d * 128:(d + 1) * 128],
                                    xe_fm[:, d * TO + s4 * 512:d * TO + (s4 + 1) * 512],
                                    start=(d == 0), stop=(d == DC - 1)),
                                wait=[(gd, tg0), pxe] + silw if (d == 0 and s4 == 0) else [],
                                inc=(ct, 1) if (d == DC - 1 and s4 == 3) else None)
                    tsl = p.emit("scalar", lambda E, psGm=psGm: E.activation(
                        sl1[:], psGm[:], AF.Silu), wait=[(ct, tgm), (cv, cv.n)], inc=(cs, 1))
                    hprev_sl = tsl
                    uw = [(gd, tu0), (cv, cv.n)]
                    if j == 0 and prev_gdn is not None:
                        uw.append(prev_gdn)
                    for d in range(DC):
                        for s4 in range(4):
                            tum = p.emit("tensor", lambda E, d=d, s4=s4, bu=bu, psUm=psUm:
                                E.matmul(
                                    psUm[:, s4 * 512:(s4 + 1) * 512],
                                    bu[:, d * 128:(d + 1) * 128],
                                    xe_fm[:, d * TO + s4 * 512:d * TO + (s4 + 1) * 512],
                                    start=(d == 0), stop=(d == DC - 1)),
                                wait=uw if (d == 0 and s4 == 0) else [],
                                inc=(ct, 1) if (d == DC - 1 and s4 == 3) else None)
                    p.emit("vector", lambda E, j=j, psUm=psUm: E.scalar_tensor_tensor(
                        h1[:, j * TO:(j + 1) * TO], psUm[:], 0.0, sl1[:],
                        ALU.bypass, ALU.mult), wait=[(ct, tum), (cs, tsl)], inc=(cv, 1))
                    mml.append((ct, tum))
                carry_gu = (ct, tum)
                if MOE_OVL and e < 7:
                    prep_state["next_pxe"] = moe_prep(
                        e + 1, psGm, [(cs, hprev_sl)], [(ct, tum)])
                p.flush()
                esE2.close()
                with contextlib.ExitStack() as esD:
                    psD3 = esD.enter_context(nc.psum_tensor("psD3_%d" % e, [128, 4 * D], F32))
                    for tg4 in range(4):
                        mmd = []
                        for jI in range(16):
                            bw = wdn[jI % 2]
                            w = [mmd[jI - 2]] if jI >= 2 else ([prev_gdn] if prev_gdn else [])
                            td0 = p.emit("gpsimd", lambda E, e=e, jI=jI, bw=bw: E.dma_start(
                                out=bw[:], in_=dwT_t[e, jI * 128:(jI + 1) * 128, :]),
                                wait=w, inc=(gd, 16))
                            tmm = None
                            for ts4 in range(4):
                                for nn in range(2):
                                    tmm = p.emit("tensor",
                                        lambda E, jI=jI, tg4=tg4, ts4=ts4, nn=nn, bw=bw,
                                        psD3=psD3:
                                        E.matmul(
                                            psD3[:, ts4 * D + nn * 512:ts4 * D + (nn + 1) * 512],
                                            h1[:, jI * TO + tg4 * 512 + ts4 * 128:
                                               jI * TO + tg4 * 512 + (ts4 + 1) * 128],
                                            bw[:, nn * 512:(nn + 1) * 512],
                                            start=(jI == 0), stop=(jI == 15)),
                                        wait=[(gd, td0), (cv, cv.n)] if (ts4 == 0 and nn == 0) else [],
                                        inc=(ct, 1) if (ts4 == 3 and nn == 1) else None)
                            mmd.append((ct, tmm))
                        accw = [(ct, ct.n)]
                        for ts4 in range(4):
                            lc = tg4 * 4 + ts4
                            if e == 0:
                                ta_ = p.emit("vector", lambda E, ts4=ts4, lc=lc, psD3=psD3:
                                    E.tensor_copy(
                                        out_acc[:, lc * D:(lc + 1) * D],
                                        psD3[:, ts4 * D:(ts4 + 1) * D]),
                                    wait=accw if ts4 == 0 else [], inc=(cv, 1))
                            else:
                                ta_ = p.emit("vector", lambda E, ts4=ts4, lc=lc, psD3=psD3:
                                    E.tensor_tensor(
                                        out_acc[:, lc * D:(lc + 1) * D],
                                        out_acc[:, lc * D:(lc + 1) * D],
                                        psD3[:, ts4 * D:(ts4 + 1) * D], ALU.add),
                                    wait=accw if ts4 == 0 else [], inc=(cv, 1))
                        prev_gdn = (cv, ta_)
            p.barrier()
            p.flush()

        gmoe_es.close()

        with contextlib.ExitStack() as esG1:
            xbG = [esG1.enter_context(nc.sbuf_tensor(f"xbG{i}", [128, D], F32)) for i in range(2)]
            tmpG = esG1.enter_context(nc.sbuf_tensor("tmpG", [128, D], F32))
            prevg = None
            for i in range(NTO):
                buf = xbG[i % 2]
                w = [prevg] if prevg else []
                t0 = p.emit("sync", lambda E, i=i, buf=buf: E.dma_start(
                    out=buf[:], in_=x1t[i]), wait=w, inc=(ld, 16))
                t1 = p.emit("vector", lambda E, i=i, buf=buf: E.tensor_scalar(
                    tmpG[:], buf[:], st1[:, 2 * NTO + i:2 * NTO + i + 1],
                    st1[:, 3 * NTO + i:3 * NTO + i + 1], ALU.add, ALU.mult),
                    wait=[(ld, t0), (cv, cv.n)], inc=(cv, 1))
                t2 = p.emit("vector", lambda E, i=i: E.tensor_tensor(
                    tmpG[:], tmpG[:], out_acc[:, i * D:(i + 1) * D], ALU.add),
                    wait=[(cv, t1)], inc=(cv, 1))
                t3 = p.emit("scalar", lambda E, i=i: E.activation(
                    out_acc[:, i * D:(i + 1) * D], tmpG[:], AF.Silu),
                    wait=[(cv, t2)], inc=(cs, 1))
                prevg = (cs, t3)
            p.barrier()
            p.flush()

        kv_es = contextlib.ExitStack()
        k_fm1 = kv_es.enter_context(nc.sbuf_tensor("k_fm1", [128, TB], BF16))
        v_aug = kv_es.enter_context(nc.sbuf_tensor("v_aug", [128, 32 * 65], BF16))
        with contextlib.ExitStack() as esKV:
            kvt = esKV.enter_context(nc.sbuf_tensor("kvt", [128, 64], BF16))
            k2b = esKV.enter_context(nc.sbuf_tensor("k2b", [64, TB], BF16))
            kn2 = esKV.enter_context(nc.sbuf_tensor("kn2", [1, TB], F32))
            psKT = esKV.enter_context(nc.psum_tensor("psKT", [64, 2048], BF16))
            psN = esKV.enter_context(nc.psum_tensor("psN", [1, 512], F32))
            p.wait("sync", cc, tag)
            p.emit("vector", lambda E: E.memset(v_aug[:], 1.0), inc=(cv, 1))
            vload = p.emit("sync", lambda E: E.dma_start(
                out=bass.AP(v_aug.tensor if hasattr(v_aug, "tensor") else v_aug, 0,
                            [[32 * 65, 128], [65, 32], [1, 64]]),
                in_=bass.AP(kvo, 64, [[128, 128], [128 * 128, 32], [1, 64]]),
                allow_slow_non_contiguous=True), wait=[(cv, cv.n)], inc=(ld, 16))
            prevk = None
            for sc in range(32):
                t0 = p.emit("sync", lambda E, sc=sc: E.dma_start(
                    out=kvt[:], in_=kvo.ap()[sc * 128:(sc + 1) * 128, 0:64]),
                    wait=[prevk] if prevk else [], inc=(ld, 16))
                off = (sc % 2) * 1024
                tt = p.emit("tensor", lambda E, off=off: E.transpose(
                    psKT[:, off:off + 128], kvt[:, 0:64], id_sb[:]),
                    wait=[(ld, t0)], inc=(ct, 1))
                prevk = (cv, p.emit("vector", lambda E, sc=sc, off=off: E.tensor_scalar(
                    k_fm1[0:64, sc * 128:(sc + 1) * 128], psKT[:, off:off + 128],
                    0.125, None, ALU.mult), wait=[(ct, tt)], inc=(cv, 1)))
            kdup = p.emit("sync", lambda E: E.dma_start(
                out=k_fm1[64:128, :], in_=k_fm1[0:64, :]), wait=[prevk], inc=(ld, 16))
            p.flush()

        att_es = contextlib.ExitStack()
        with att_es:
            y_fm = att_es.enter_context(nc.sbuf_tensor("y_fm", [128, DC * TO], BF16))
            att2_es = att_es.enter_context(contextlib.ExitStack())
            q_fm1 = att2_es.enter_context(nc.sbuf_tensor("q_fm1", [128, DC * TO], BF16))
            eT1 = att2_es.enter_context(nc.sbuf_tensor("eTL1", [128, 2 * TO], BF16))
            dm_sb = att2_es.enter_context(nc.sbuf_tensor("dm_sb", [128, 64 * 128], BF16))
            srowB = att2_es.enter_context(nc.sbuf_tensor("srowB", [1, TO], F32))
            srow2 = att2_es.enter_context(nc.sbuf_tensor("srow2", [1, TO], BF16))
            invSb = att2_es.enter_context(nc.sbuf_tensor("invSb", [64, TO], BF16))
            ystage = att2_es.enter_context(nc.sbuf_tensor("ystage", [64, TO], BF16))
            q2b = att2_es.enter_context(nc.sbuf_tensor("q2b", [128, TO], BF16))
            ind2_sb = att2_es.enter_context(nc.sbuf_tensor("ind2_sb", [128, 2], BF16))
            qld = []
            for qc in range(DC):
                qld.append(p.emit("sync", lambda E, qc=qc: E.dma_start(
                    out=q_fm1[:, qc * TO:(qc + 1) * TO], in_=qf_d.ap()[qc, :, :]),
                    inc=(ld, 16)))
            tdm = p.emit("sync", lambda E: E.dma_start(
                out=dm_sb[:], in_=bass.AP(diagm_h, 0, [[128, 128], [128 * 128, 64], [1, 128]])),
                inc=(ld, 16))
            ti2 = p.emit("sync", lambda E: E.dma_start(out=ind2_sb[:], in_=ind2_h.ap()),
                         inc=(ld, 16))
            with contextlib.ExitStack() as esB0:
                psQ2 = esB0.enter_context(nc.psum_tensor("psQ2", [2, TO], F32))
                prevq2 = None
                for qc in range(DC):
                    tsq = p.emit("scalar", lambda E, qc=qc: E.activation(
                        q2b[:], q_fm1[:, qc * TO:(qc + 1) * TO], AF.Square),
                        wait=[(ld, qld[qc])] + ([prevq2] if prevq2 else []), inc=(cs, 1))
                    tm = None
                    for s4 in range(4):
                        tm = p.emit("tensor", lambda E, s4=s4: E.matmul(
                            psQ2[:, s4 * 512:(s4 + 1) * 512], ind2_sb[:],
                            q2b[:, s4 * 512:(s4 + 1) * 512], start=True, stop=True),
                            wait=[(cs, tsq), (ld, ti2)] if s4 == 0 else [], inc=(ct, 1))
                    prevq2 = (cv, p.emit("vector", lambda E, qc=qc: E.tensor_reduce(
                        tiny[0:2, 57 + qc:58 + qc], psQ2[:], AX.X, ALU.max),
                        wait=[(ct, tm)], inc=(cv, 1)))
                tk2 = p.emit("scalar", lambda E: E.activation(
                    k2b[:], k_fm1[0:64, :], AF.Square),
                    wait=[(cv, cv.n), (ld, kdup)], inc=(cs, 1))
                kn_ev = []
                for s8 in range(8):
                    tm = p.emit("tensor", lambda E, s8=s8: E.matmul(
                        psQ2[0:1, s8 % 4 * 512:(s8 % 4 + 1) * 512],
                        ones_col[0:64, 0:1],
                        k2b[:, s8 * 512:(s8 + 1) * 512], start=True, stop=True),
                        wait=[(cs, tk2)] + ([(cv, kn_ev[s8 - 4])] if s8 >= 4 else []),
                        inc=(ct, 1))
                    kn_ev.append(p.emit("vector", lambda E, s8=s8: E.tensor_copy(
                        kn2[:, s8 * 512:(s8 + 1) * 512],
                        psQ2[0:1, s8 % 4 * 512:(s8 % 4 + 1) * 512]),
                        wait=[(ct, tm)], inc=(cv, 1)))
                c1_ = p.emit("vector", lambda E: E.tensor_reduce(
                    tiny[0:2, 56:57], tiny[0:2, 57:57 + DC], AX.X, ALU.max),
                    wait=[(cv, cv.n)], inc=(cv, 1))
                tt = p.emit("tensor", lambda E: E.transpose(
                    psQ2[0:1, 0:2], tiny[0:2, 56:57], id_f[0:2, 0:2]),
                    wait=[(cv, c1_)], inc=(ct, 1))
                c2_ = p.emit("vector", lambda E: E.reduce_max(
                    row11[:, 12:13], psQ2[0:1, 0:2], axis=AX.X),
                    wait=[(ct, tt)], inc=(cv, 1))
                c3_ = p.emit("vector", lambda E: E.reduce_max(
                    row11[:, 13:14], kn2[:, :], axis=AX.X), wait=[(cv, c2_)], inc=(cv, 1))
                c4_ = p.emit("scalar", lambda E: E.activation(
                    row11[:, 14:15], row11[:, 12:13], AF.Sqrt), wait=[(cv, c3_)], inc=(cs, 1))
                c5_ = p.emit("scalar", lambda E: E.activation(
                    row11[:, 15:16], row11[:, 13:14], AF.Sqrt), wait=[(cs, c4_)], inc=(cs, 1))
                c6_ = p.emit("vector", lambda E: E.tensor_tensor(
                    row11[:, 11:12], row11[:, 14:15], row11[:, 15:16], ALU.mult),
                    wait=[(cs, c5_)], inc=(cv, 1))
                c7_ = p.emit("vector", lambda E: E.tensor_scalar(
                    row11[:, 10:11], row11[:, 11:12], -1.0, None, ALU.mult),
                    wait=[(cv, c6_)], inc=(cv, 1))
            with contextlib.ExitStack() as esB0b:
                psB0 = esB0b.enter_context(nc.psum_tensor("psB0", [128, 1], F32))
                tmB2 = p.emit("tensor", lambda E: E.matmul(
                    psB0[:, 0:1], ones_row[:1, :], row11[:, 10:11],
                    start=True, stop=True), wait=[(cv, c7_)], inc=(ct, 1))
                b0done = p.emit("vector", lambda E: E.tensor_copy(
                    scal[:, 8:9], psB0[:, 0:1]), wait=[(ct, tmB2)], inc=(cv, 1))
            p.flush()
            prev_ev = {}
            for hh in range(NHEAD):
                hp, hr = hh // 2, (hh % 2) * 64
                for th in range(2):
                    tb0 = th * 1024
                    with contextlib.ExitStack() as esH1:
                        psYh_pad = esH1.enter_context(nc.psum_tensor(
                            "psYh_%d_%d" % (hh, th), [65, 1536], F32))
                        psYh = psYh_pad[:, 512:1536]
                        psS = [esH1.enter_context(nc.psum_tensor(
                            "psS_%d_%d_%d" % (hh, th, k2), [128, 1024], F32))
                            for k2 in range(2)]

                        sc_list = [sc for sc in range(32) if (sc // 2) * 128 < tb0 + 1024]
                        prev_exp_par = [prev_ev.get(("exp", 0)), prev_ev.get(("exp", 1))]
                        avl = {}
                        avl[-2] = prev_ev.get("avl")
                        avl[-1] = prev_ev.get("avl")
                        exp_t = {}
                        state = {}
                        tav = None

                        def emit_av(idx):
                            nonlocal tav
                            sc, segs, ebuf = state[idx]
                            ft = (idx == 0)
                            for gi, (s0, s1) in enumerate(segs):
                                bnk = (tb0 + s0) // 512
                                stp = (sc == min(31, 8 * bnk + 7))
                                w = []
                                if gi == 0:
                                    w = [(cs, exp_t[idx])]
                                    if ft and "y" in prev_ev:
                                        w.append(prev_ev["y"])
                                tav = p.emit("tensor", lambda E, sc=sc, s0=s0, s1=s1,
                                             ebuf=ebuf, stp=stp, ft=ft, psYh=psYh:
                                    E.matmul(
                                        psYh[:, s0:s1],
                                        v_aug[:, agrow(sc) // 128 * 65:
                                              agrow(sc) // 128 * 65 + 65],
                                        ebuf[:, s0:s1],
                                        start=ft, stop=stp), wait=w, inc=(ct, 1))
                            avl[idx] = (ct, tav)

                        for si, sc in enumerate(sc_list):
                            j0 = sc // 2
                            lo = max(j0 * 128, tb0) - tb0
                            pss = psS[si % 2]
                            ebuf = eT1[:, (si % 2) * TO + tb0:(si % 2) * TO + tb0 + 1024]
                            segs = []
                            s0 = lo
                            while s0 < 1024:
                                s1 = min((s0 // 512 + 1) * 512, 1024)
                                segs.append((s0, s1))
                                s0 = s1
                            masks = []
                            for m in range(2):
                                j = j0 + m
                                if j <= 15 and tb0 <= j * 128 < tb0 + 1024:
                                    masks.append((m, j * 128 - tb0))
                            tsc = None
                            for gi, (s0, s1) in enumerate(segs):
                                w = []
                                if gi == 0:
                                    w = [(ld, tdm)]
                                    if si == 0:
                                        w.append((cv, cv.n))
                                    pe_ = prev_exp_par[si % 2]
                                    if pe_ is not None:
                                        w.append((cs, pe_))
                                segmasks = [mm for mm in masks if s0 <= mm[1] < s1]
                                tsc = p.emit("tensor", lambda E, sc=sc, s0=s0, s1=s1,
                                             hr=hr, hp=hp, tb0=tb0, pss=pss,
                                             stp=(not segmasks):
                                    E.matmul(
                                        pss[:, s0:s1],
                                        k_fm1[hr:hr + 64, agrow(sc):agrow(sc) + 128],
                                        q_fm1[hr:hr + 64,
                                              hp * TO + tb0 + s0:hp * TO + tb0 + s1],
                                        start=True, stop=stp), wait=w, inc=(ct, 1))
                                for mi, (m, mlo) in enumerate(segmasks):
                                    tsc = p.emit("tensor", lambda E, sc=sc, m=m, mlo=mlo,
                                                 pss=pss, last=(mi == len(segmasks) - 1):
                                        E.matmul(
                                            pss[:, mlo:mlo + 128], id_sb[:],
                                            dm_sb[:, (sc * 2 + m) * 128:
                                                  (sc * 2 + m + 1) * 128],
                                            start=False, stop=last), inc=(ct, 1))
                            ew = [(ct, tsc), (cv, b0done)]
                            if avl.get(si - 2) is not None:
                                ew.append(avl[si - 2])
                            texp1 = p.emit("scalar", lambda E, lo=lo, ebuf=ebuf, pss=pss:
                                E.activation(
                                    ebuf[:, lo:1024], pss[:, lo:1024], AF.Exp,
                                    bias=scal[:, 8:9]), wait=ew, inc=(cs, 1))
                            prev_exp_par[si % 2] = texp1
                            exp_t[si] = texp1
                            state[si] = (sc, segs, ebuf)
                            if si >= 1:
                                emit_av(si - 1)
                        emit_av(len(sc_list) - 1)
                        prev_exp_par_final = prev_exp_par
                        prev_ev[("exp", 0)] = prev_exp_par[0]
                        prev_ev[("exp", 1)] = prev_exp_par[1]
                        prev_ev["avl"] = (ct, tav)
                        if hh == 0 and th == 0:
                            eS = p.emit("vector", lambda E: E.tensor_copy(
                                q2b[:1, 0:1024], psYh[64:65, :]),
                                wait=[(ct, tav)], inc=(cv, 1))
                            p.dbg("Sraw_%d" % th, lambda: q2b[:1, 0:1024], [(cv, eS)])
                        e2 = p.emit("vector", lambda E: E.reciprocal(
                            srowB[:, 0:1024], psYh[64:65, :]),
                            wait=[(ct, tav), (cv, cv.n)], inc=(cv, 1))
                        e3 = p.emit("vector", lambda E: E.tensor_copy(
                            srow2[:, 0:1024], srowB[:, 0:1024]), wait=[(cv, e2)], inc=(cv, 1))
                        bm = None
                        for s4 in range(2):
                            bm = p.emit("tensor", lambda E, s4=s4: E.matmul(
                                psS[0][0:64, s4 * 512:(s4 + 1) * 512], onesrb[:1, 0:64],
                                srow2[:, s4 * 512:(s4 + 1) * 512], start=True, stop=True),
                                wait=[(cv, e3)] + ([(cs, prev_exp_par[0])]
                                                   if (s4 == 0 and prev_exp_par[0]) else []),
                                inc=(ct, 1))
                        e4 = p.emit("vector", lambda E: E.tensor_copy(
                            invSb[:, 0:1024], psS[0][0:64, 0:1024]),
                            wait=[(ct, bm)], inc=(cv, 1))
                        if hr == 0:
                            e5 = p.emit("vector", lambda E, hp=hp, tb0=tb0: E.tensor_tensor(
                                y_fm[0:64, hp * TO + tb0:hp * TO + tb0 + 1024],
                                psYh[0:64, :], invSb[:, 0:1024], ALU.mult),
                                wait=[(cv, e4)], inc=(cv, 1))
                            prev_ev["y"] = (cv, e5)
                        else:
                            e5 = p.emit("vector", lambda E: E.tensor_tensor(
                                ystage[:, 0:1024], psYh[0:64, :],
                                invSb[:, 0:1024], ALU.mult),
                                wait=[(cv, e4)] + ([prev_ev["ys"]] if "ys" in prev_ev else []),
                                inc=(cv, 1))
                            e6 = p.emit("sync", lambda E, hp=hp, tb0=tb0: E.dma_start(
                                out=y_fm[64:128, hp * TO + tb0:hp * TO + tb0 + 1024],
                                in_=ystage[:, 0:1024]), wait=[(cv, e5)], inc=(ld, 16))
                            prev_ev["ys"] = (ld, e6)
                            prev_ev["y"] = (ld, e6)
            p.dbg("y_fm", lambda: y_fm[:, :], [(cv, cv.n), (ld, ld.n)])
            p.flush()
            att2_es.close()
            with contextlib.ExitStack() as esO:
                Wcp_bf = esO.enter_context(nc.sbuf_tensor("Wcp_bf", [128, DC * D], BF16))
                xbO = [esO.enter_context(nc.sbuf_tensor(f"xbO{i}", [128, D], F32)) for i in range(2)]
                sgO = esO.enter_context(nc.sbuf_tensor("sgO", [128, D], F32))
                x2O = esO.enter_context(nc.sbuf_tensor("x2O", [128, D], F32))
                psO = esO.enter_context(nc.psum_tensor("psO", [128, D], F32))
                tcp = p.emit("gpsimd", lambda E: E.dma_start(
                    out=Wcp_bf[:], in_=bass.AP(Wcp_h, 0, [[D, 128], [128 * D, DC], [1, D]])),
                    inc=(gd, 16))
                prevo = None
                for i in range(NTO):
                    buf = xbO[i % 2]
                    tx0 = p.emit("sync", lambda E, i=i, buf=buf: E.dma_start(
                        out=buf[:], in_=x1t[i]), wait=[prevo] if prevo else [], inc=(ld, 16))
                    tm = None
                    for hp2 in range(DC):
                        for nn in range(2):
                            tm = p.emit("tensor", lambda E, i=i, hp2=hp2, nn=nn: E.matmul(
                                psO[:, nn * 512:(nn + 1) * 512],
                                y_fm[:, hp2 * TO + i * 128:hp2 * TO + (i + 1) * 128],
                                Wcp_bf[:, hp2 * D + nn * 512:hp2 * D + (nn + 1) * 512],
                                start=(hp2 == 0), stop=(hp2 == DC - 1)),
                                wait=[(cv, cv.n), (gd, tcp)] if (hp2 == 0 and nn == 0) else [],
                                inc=(ct, 1) if (hp2 == DC - 1 and nn == 1) else None)
                    o1 = p.emit("vector", lambda E, i=i: E.scalar_tensor_tensor(
                        sgO[:], psO[:], 0.0, out_acc[:, i * D:(i + 1) * D],
                        ALU.bypass, ALU.mult), wait=[(ct, tm)], inc=(cv, 1))
                    o2 = p.emit("vector", lambda E, i=i, buf=buf: E.tensor_tensor(
                        x2O[:], sgO[:], buf[:], ALU.add),
                        wait=[(cv, o1), (ld, tx0)], inc=(cv, 1))
                    tso = p.emit("sync", lambda E, i=i: E.dma_start(
                        out=out_h.ap()[i * 128:(i + 1) * 128, :], in_=x2O[:]),
                        wait=[(cv, o2)], inc=(st, 16))
                    prevo = (st, tso)
                p.flush()

        p.emit("sync", lambda E: E.nop(), wait=[(st, st.n)])
        p.barrier()
        p.flush()
        kv_es.close()
        L1_es.close()
        return nc



def kernel(**inputs):
    nc = build_program()
    in_maps = [host_inputs(inputs, c) for c in range(8)]
    res = run_bass_kernel_spmd(nc, in_maps, core_ids=list(range(8)))
    outf = np.zeros((4, TB, D), np.float32)
    for c in range(8):
        b, hc = c // 2, c % 2
        outf[b, own_order(hc)] = res.results[c]["out"]
    return outf
